# version 19
# baseline (speedup 1.0000x reference)
import numpy as np
from contextlib import ExitStack

import concourse.bass as bass
import concourse.mybir as mybir
from concourse.bass_utils import run_bass_kernel_spmd

F32 = mybir.dt.float32
BF16 = mybir.dt.bfloat16
FP8 = mybir.dt.float8e4
I32 = mybir.dt.int32
AF = mybir.ActivationFunctionType
ALU = mybir.AluOpType
AX = mybir.AxisListType

B_, L_, D, S, N = 2, 8192, 2048, 512, 8192
CT, PRE = 2048, 6144
TOPK = 819
LNEPS = 1e-5
NG, G = 4, 512
MT, DT, ST, NT = 4, 16, 4, 64
NPC, NLC, NCH = 12, 4, 16
AIM = float(TOPK) + 3.0
SIG_SPAN = 30.0
BR_LO = 0.8
BR_HI = 2.0
LAD0 = 1.38
LAD1 = 1.15
NPROBE = 7
N_FP = 3


ENGS = ("tensor", "vector", "scalar", "gpsimd", "sync")


class Seq:

    def __init__(self, nc, ctx):
        self.nc = nc
        self.chains = {e: ctx.enter_context(nc.semaphore(f"ch_{e}"))
                       for e in ENGS}
        self.counts = {e: 0 for e in ENGS}
        self.where = {}
        self.n = 0
        self.stages = []

    def stage(self, eng, fn, dep=None, after=None, waits=(), inc=True,
              dma=None):
        if after is None:
            after = [self.n if dep is None else dep]
        elif dep is not None:
            after = list(after) + [dep]
        cw = {}
        for sid in after:
            if not sid:
                continue
            deng, didx = self.where[sid]
            cw[deng] = max(cw.get(deng, 0), didx)
        chw = tuple((self.chains[k], v) for k, v in cw.items())
        if dma is not None:
            dma[0].v += 16 * dma[1]
        self.stages.append((eng, chw, tuple(waits), fn, inc, dma))
        if inc:
            self.counts[eng] += 1
            self.n += 1
            self.where[self.n] = (eng, self.counts[eng])
            return self.n
        return None

    def dma(self, eng, fn, sem, n, dep=None, after=None, waits=()):
        return self.stage(eng, fn, dep=dep, after=after, waits=waits,
                          inc=False, dma=(sem, n))

    def flush(self):
        stages = self.stages
        self.stages = []
        if not stages:
            return
        nc = self.nc
        used = sorted({s[0] for s in stages})
        with nc.Block() as blk:
            for name in used:
                def mk(name):
                    def run(e):
                        for eng, chw, waits, fn, inc, dma in stages:
                            if eng != name:
                                continue
                            for sem, val in chw:
                                e.wait_ge(sem, val)
                            for sem, val in waits:
                                e.wait_ge(sem, val)
                            r = fn(e)
                            if dma is not None:
                                insts = r if isinstance(r, (list, tuple)) else \
                                    ([] if r is None else [r])
                                assert len(insts) == dma[1], \
                                    f"dma stage returned {len(insts)} != {dma[1]}"
                                for ins in insts:
                                    ins.then_inc(dma[0].s, 16)
                            elif inc:
                                assert r is not None and \
                                    not isinstance(r, (list, tuple)), \
                                    "stage must return its last instruction"
                                r.then_inc(self.chains[name], 1)
                    return run
                getattr(blk, name)(mk(name))


class Dsem:
    def __init__(self, ctx, nc, name):
        self.s = ctx.enter_context(nc.semaphore(name))
        self.v = 0


class _TruncDone(Exception):
    pass


def build_nc(debug_taps=(), trunc=None):
    nc = bass.Bass("TRN2", target_bir_lowering=False, debug=False,
                   detect_race_conditions=False)
    taps = set(debug_taps)
    dr = {}

    def din(name, shape, dt=F32):
        dr[name] = nc.dram_tensor(name, shape, dt, kind="ExternalInput").ap()
        return dr[name]

    def dint(name, shape, dt):
        dr[name] = nc.dram_tensor(name, shape, dt, kind="Internal").ap()
        return dr[name]

    def dout(name, shape, dt=F32):
        dr[name] = nc.dram_tensor(name, shape, dt, kind="ExternalOutput").ap()
        return dr[name]

    x_own = din("x_own", [D, CT + 4])
    x_pre = din("x_pre", [D, PRE + 4])
    wxT = din("wxT", [D, 1025])
    wiT = din("wiT", [D, S])
    woT = din("woT", [S, D])
    din("cwv", [128, DT * 4])
    din("cbv", [128, DT])
    din("alog", [128, ST])
    din("dvec", [128, DT])
    wsynT = din("wsynT", [D, N])
    woutT = din("woutT", [N, D])
    din("bsyn_r", [128, N])
    din("bout_r", [128, D])
    din("gam_r", [128, D])
    din("bet_r", [128, D])

    t8syn = dint("t8syn", [D, N], FP8)
    t8out = dint("t8out", [N, D], FP8)
    x1fm_h = dint("x1fm_h", [NG, D, G], BF16)
    x1tk_h = dint("x1tk_h", [CT, D], BF16)

    out = dout("out", [CT, D])
    tp = {}
    for tname, shp, tdt in [("x1", [CT, D], BF16), ("z", [G, N], F32),
                            ("thr", [128, 64], F32), ("quant", [128, 16], F32),
                            ("carry", [128, ST], F32), ("dbg", [128, 64], F32)]:
        if tname in taps:
            tp[tname] = dout("tap_" + tname, shp, tdt)

    class _Suppress:
        def __enter__(self):
            return self

        def __exit__(self, et, ev, tb):
            return et is _TruncDone

    ctx = ExitStack()
    with ctx:
        ctx.enter_context(_Suppress())

        def sb(name, shape, dt=F32):
            return ctx.enter_context(nc.sbuf_tensor(name, shape, dt))

        seq = Seq(nc, ctx)

        def trunc_point(name):
            if trunc != name:
                return
            def tf(e):
                return e.dma_start(out=out[0:128, 0:32], in_=s_scal[:])
            seq.dma("sync", tf, sm_o, 1)
            sems = [sm_w, sm_g, sm_x, sm_q, sm_8, sm_t, sm_1, sm_o]
            def tw(e):
                for sm in sems:
                    e.wait_ge(sm.s, sm.v)
                return None
            seq.stage("sync", tw, inc=False)
            seq.flush()
            raise _TruncDone()

        s_bout = sb("s_bout", [128, D], BF16)
        s_gam = sb("s_gam", [128, D], BF16)
        s_bet = sb("s_bet", [128, D], BF16)
        s_ones = sb("s_ones", [128, 128], BF16)
        s_eye = sb("s_eye", [128, 128], BF16)
        s_io8 = sb("s_io8", [128, 8])
        s_scal = sb("s_scal", [128, 32])
        s_jk = sb("s_jk", [128, 8])

        def vjk(e):
            return e.tensor_scalar(out=s_jk[:, 0:1], in0=s_jk[:, 2:3],
                                   scalar1=1.0, scalar2=None, op0=ALU.mult)

        def sjk(e):
            return e.activation(out=s_jk[:, 1:2], in_=s_jk[:, 1:2],
                                func=AF.Copy)

        sm_w = Dsem(ctx, nc, "sm_w")
        sm_g = Dsem(ctx, nc, "sm_g")
        sm_x = Dsem(ctx, nc, "sm_x")
        sm_q = Dsem(ctx, nc, "sm_q")
        sm_8 = Dsem(ctx, nc, "sm_8")
        sm_t = Dsem(ctx, nc, "sm_t")
        sm_1 = Dsem(ctx, nc, "sm_1")
        sm_o = Dsem(ctx, nc, "sm_o")

        wscope = ExitStack()
        s_wx = wscope.enter_context(nc.sbuf_tensor("s_wx", [128, DT * 1024], BF16))
        s_wi = wscope.enter_context(nc.sbuf_tensor("s_wi", [128, DT * S], BF16))
        s_wo = wscope.enter_context(nc.sbuf_tensor("s_wo", [128, ST * D], BF16))
        s_wdrep = wscope.enter_context(
            nc.sbuf_tensor("s_wdrep", [128, DT * 128], BF16))
        s_cw = wscope.enter_context(nc.sbuf_tensor("s_cw", [128, DT * 4], F32))
        s_cb = wscope.enter_context(nc.sbuf_tensor("s_cb", [128, DT], F32))
        s_A = wscope.enter_context(nc.sbuf_tensor("s_A", [128, ST], F32))
        s_D = wscope.enter_context(nc.sbuf_tensor("s_D", [128, DT], F32))
        s_carry = wscope.enter_context(nc.sbuf_tensor("s_carry", [128, ST], F32))
        s_on32 = wscope.enter_context(nc.sbuf_tensor("s_on32", [128, 128], F32))

        def st0_loads(e):
            r = []
            for t, dst in [("cwv", s_cw), ("cbv", s_cb), ("dvec", s_D)]:
                r.append(e.dma_start(out=dst[:], in_=dr[t][:]))
            r.append(e.dma_start(out=s_scal[:, 8:8 + ST], in_=dr["alog"][:]))
            r.append(e.dma_start(
                out=s_scal[:, 16:16 + DT].rearrange("p (t o) -> p t o", o=1),
                in_=wxT[:, 1024:1025].rearrange("(t p) o -> p t o", p=128),
                allow_slow_non_contiguous=True))
            return r
        seq.dma("sync", st0_loads, sm_w, 5)

        def st0_casts(e):
            return [e.dma_start(out=dst[:], in_=dr[t][:])
                    for t, dst in [("bout_r", s_bout), ("gam_r", s_gam),
                                   ("bet_r", s_bet)]]
        seq.dma("gpsimd", st0_casts, sm_g, 3)
        w_small = sm_w.v

        def st0_iota(e):
            e.iota(s_eye[:], pattern=[[-1, 128]], channel_multiplier=1,
                   allow_small_or_imprecise_dtypes=True)
            return e.iota(s_io8[:], pattern=[[1, 8]], channel_multiplier=0,
                          allow_small_or_imprecise_dtypes=True)
        seq.stage("gpsimd", st0_iota)

        def st0_v(e):
            e.memset(s_ones[:], 1.0)
            e.memset(s_on32[:], 1.0)
            e.memset(s_jk[:], 0.0)
            e.memset(s_carry[:], 0.0)
            e.tensor_scalar(out=s_eye[:], in0=s_eye[:], scalar1=0.0,
                            scalar2=None, op0=ALU.is_equal)
            ins = None
            for dt_ in range(DT):
                ins = e.tensor_scalar(
                    out=s_wdrep[:, dt_ * 128:(dt_ + 1) * 128], in0=s_ones[:],
                    scalar1=s_scal[:, 16 + dt_:17 + dt_], scalar2=None,
                    op0=ALU.mult)
            return ins
        seq.stage("vector", st0_v, waits=((sm_w.s, w_small), (sm_g.s, sm_g.v)))

        def st0_s(e):
            return e.activation(out=s_A[:], in_=s_scal[:, 8:8 + ST],
                                func=AF.Exp)
        seq.stage("scalar", st0_s)

        def st0_v2(e):
            return e.tensor_scalar(out=s_A[:], in0=s_A[:], scalar1=-1.0,
                                   scalar2=None, op0=ALU.mult)
        seq.stage("vector", st0_v2)
        seq.flush()
        trunc_point("t0")

        NSL = DT + 16

        def slab_in(i):
            if i < DT:
                return wsynT[i * 128:(i + 1) * 128, :]
            j = i - DT
            return woutT[j * 512:(j + 1) * 512, :].rearrange(
                "(a p) c -> p a c", p=128)

        def slab_out8(i):
            if i < DT:
                return t8syn[i * 128:(i + 1) * 128, :]
            j = i - DT
            return t8out[j * 512:(j + 1) * 512, :].rearrange(
                "(a p) c -> p a c", p=128)

        with ExitStack() as qctx:
            qsb = lambda nm, shp, dt=F32: qctx.enter_context(
                nc.sbuf_tensor(nm, shp, dt))
            s_qa = qsb("s_qa", [128, N])
            s_qb = qsb("s_qb", [128, N])
            s_qi = qsb("s_qi", [128, N], I32)
            s_qo = s_qi[:].bitcast(F32)
            s_q8a = qsb("s_q8a", [128, N], FP8)
            s_q8b = qsb("s_q8b", [128, N], FP8)
            s_abs = qsb("s_abs", [128, 32])
            p_red = qctx.enter_context(nc.psum_tensor("p_red", [128, 32], F32))

            consumed = {}
            for i in range(NSL):
                buf = s_qa if i % 2 == 0 else s_qb
                def ld(e, i=i, buf=buf):
                    dst = buf[:] if i < DT else buf[:].rearrange(
                        "p (a c) -> p a c", a=4)
                    return e.dma_start(out=dst, in_=slab_in(i))
                seq.dma("sync", ld, sm_q, 1, dep=consumed.get(i - 2, 0))
                qv = sm_q.v
                def ab(e, i=i, buf=buf):
                    return e.activation(out=s_qo, in_=buf[:], func=AF.Abs,
                                        accum_out=s_abs[:, i:i + 1])
                consumed[i] = seq.stage("scalar", ab, waits=((sm_q.s, qv),))

            def red_mm(e):
                return e.matmul(p_red[:], s_on32[:], s_abs[:],
                                start=True, stop=True)
            seq.stage("tensor", red_mm)

            def red_v(e):
                e.tensor_reduce(out=s_scal[:, 0:1], in_=p_red[:, 0:DT],
                                axis=AX.X, op=ALU.add)
                e.tensor_reduce(out=s_scal[:, 1:2], in_=p_red[:, DT:NSL],
                                axis=AX.X, op=ALU.add)
                for c in range(2):
                    e.tensor_scalar(out=s_scal[:, c:c + 1],
                                    in0=s_scal[:, c:c + 1],
                                    scalar1=1.0 / (D * N), scalar2=None,
                                    op0=ALU.mult)
                for c in range(2):
                    e.tensor_scalar(out=s_scal[:, 5 + c:6 + c],
                                    in0=s_scal[:, c:c + 1],
                                    scalar1=1e-8, scalar2=None, op0=ALU.add)
                for c in range(2):
                    e.tensor_scalar(out=s_scal[:, 12 + c:13 + c],
                                    in0=s_scal[:, 5 + c:6 + c],
                                    scalar1=0.5, scalar2=None, op0=ALU.mult)
                for c in range(2):
                    e.tensor_scalar(out=s_scal[:, 14 + c:15 + c],
                                    in0=s_scal[:, 5 + c:6 + c],
                                    scalar1=-0.5, scalar2=None, op0=ALU.mult)
                return e.reciprocal(out=s_scal[:, 2:4], in_=s_scal[:, 5:7])
            seq.stage("vector", red_v)

            stored = {}
            for i in range(NSL):
                buf = s_qa if i % 2 == 0 else s_qb
                b8 = s_q8a if i % 2 == 0 else s_q8b
                def ld(e, i=i, buf=buf):
                    dst = buf[:] if i < DT else buf[:].rearrange(
                        "p (a c) -> p a c", a=4)
                    return e.dma_start(out=dst, in_=slab_in(i))
                seq.dma("sync", ld, sm_q, 1, dep=stored.get(i - 2, 0))
                qv = sm_q.v
                def qz(e, i=i, buf=buf, b8=b8):
                    co = 0 if i < DT else 1
                    e.tensor_scalar(out=s_qo, in0=buf[:],
                                    scalar1=s_scal[:, 14 + co:15 + co],
                                    scalar2=None, op0=ALU.is_le)
                    vjk(e)
                    return e.scalar_tensor_tensor(
                        out=b8[:], in0=buf[:],
                        scalar=s_scal[:, 12 + co:13 + co],
                        in1=s_qo, op0=ALU.is_ge, op1=ALU.subtract)
                seq.stage("vector", qz, waits=((sm_q.s, qv),))
                def stq(e, i=i, b8=b8):
                    srcb = b8[:] if i < DT else b8[:].rearrange(
                        "p (a c) -> p a c", a=4)
                    return e.dma_start(out=slab_out8(i), in_=srcb)
                seq.dma("gpsimd", stq, sm_8, 1)
                stored[i] = seq.n
            t8_done = sm_8.v

            def ldwx(e):
                return [
                    e.dma_start(out=s_qa[:, 0:N].rearrange(
                        "p (t c) -> p t c", t=8),
                        in_=wxT[0:1024, 0:1024].rearrange(
                            "(t p) c -> p t c", p=128)),
                    e.dma_start(out=s_qb[:, 0:N].rearrange(
                        "p (t c) -> p t c", t=8),
                        in_=wxT[1024:2048, 0:1024].rearrange(
                            "(t p) c -> p t c", p=128)),
                ]
            seq.dma("sync", ldwx, sm_q, 2)
            qv = sm_q.v

            def cvwx(e):
                e.tensor_copy(out=s_wx[:, 0:8 * 1024], in_=s_qa[:, 0:N])
                return e.tensor_copy(out=s_wx[:, 8 * 1024:DT * 1024],
                                     in_=s_qb[:, 0:N])
            seq.stage("vector", cvwx, waits=((sm_q.s, qv),))

            def ldwio(e):
                return [
                    e.dma_start(out=s_qa[:, 0:DT * S].rearrange(
                        "p (t c) -> p t c", t=DT),
                        in_=wiT.rearrange("(t p) c -> p t c", p=128)),
                    e.dma_start(out=s_qb[:, 0:ST * D].rearrange(
                        "p (t c) -> p t c", t=ST),
                        in_=woT.rearrange("(t p) c -> p t c", p=128)),
                ]
            seq.dma("sync", ldwio, sm_q, 2)
            qv = sm_q.v

            def cvwio(e):
                e.tensor_copy(out=s_wi[:], in_=s_qa[:, 0:DT * S])
                return e.tensor_copy(out=s_wo[:], in_=s_qb[:, 0:ST * D])
            quant_tail = seq.stage("vector", cvwio, waits=((sm_q.s, qv),))

            if "quant" in tp:
                def tq(e):
                    return e.dma_start(out=tp["quant"][:, 0:4], in_=s_scal[:, 0:4])
                seq.dma("sync", tq, sm_o, 1)
            if trunc == "quant":
                def tdbg(e):
                    return e.dma_start(out=out[128:256, 0:32], in_=s_abs[:])
                seq.dma("sync", tdbg, sm_o, 1)
            seq.flush()
        trunc_point("quant")

        with ExitStack() as mctx:
            msb = lambda nm, shp, dt=F32: mctx.enter_context(
                nc.sbuf_tensor(nm, shp, dt))
            s_xbs = [msb(f"s_xb{i}", [128, DT * 520], BF16) for i in range(2)]
            s_xc = msb("s_xc", [128, DT * 512], BF16)
            s_B = msb("s_B", [128, ST * 512], BF16)
            s_C = msb("s_C", [128, ST * 512], BF16)
            s_dc = msb("s_dc", [128, 512])
            s_dA = msb("s_dA", [128, ST * 512], BF16)
            s_Bu = msb("s_Bu", [128, ST * 512], BF16)
            s_h = msb("s_h", [128, ST * 512])
            s_hm = msb("s_hm", [128, ST * 512], BF16)
            s_mu = msb("s_mu", [128, 512])
            s_va = msb("s_va", [128, 512])
            s_rs = msb("s_rs", [128, 512])
            s_tm = msb("s_tm", [128, 512])
            s_x1f = msb("s_x1f", [128, DT * 512], BF16)
            s_sq = s_x1f
            s_x1t = msb("s_x1t", [128, MT * D], BF16)

            PS = [mctx.enter_context(nc.psum_tensor(f"p_s{i}", [128, 512], F32))
                  for i in range(4)]
            p_mu = mctx.enter_context(nc.psum_tensor("p_mu", [128, 512], F32))
            p_vr = mctx.enter_context(nc.psum_tensor("p_vr", [128, 512], F32))
            p_tb = mctx.enter_context(nc.psum_tensor("p_tb", [128, 1024], BF16))
            p_tb2 = mctx.enter_context(
                nc.psum_tensor("p_tb2", [128, 1024], BF16))

            def chunk_src(i):
                if i < NPC:
                    return x_pre[:, i * 512:i * 512 + 516]
                j = i - NPC
                return x_own[:, j * 512:j * 512 + 516]

            x_used = {}
            pv = {}
            for ci in range(NPC + NLC):
                own = ci >= NPC
                oc = ci - NPC
                cur = {}
                s_xb = s_xbs[ci % 2]

                def ldx(e, ci=ci, s_xb=s_xb):
                    return e.dma_start(
                        out=s_xb[:].rearrange("p (t c) -> p t c",
                                              t=DT)[:, :, 4:520],
                        in_=chunk_src(ci).rearrange("(t p) c -> p t c", p=128))
                seq.dma("gpsimd", ldx, sm_x, 1,
                        dep=x_used.get(ci - 2, quant_tail))
                xv = sm_x.v

                def conv(e, s_xb=s_xb):
                    for da in range(0, DT, 2):
                        pair = (da, da + 1)
                        for dt_ in pair:
                            xo = dt_ * 520 + 5
                            e.tensor_scalar(
                                out=s_xc[:, dt_ * 512:(dt_ + 1) * 512],
                                in0=s_xb[:, xo:xo + 512],
                                scalar1=s_cw[:, dt_ * 4:dt_ * 4 + 1],
                                scalar2=s_cb[:, dt_:dt_ + 1],
                                op0=ALU.mult, op1=ALU.add)
                        for k in range(1, 4):
                            for dt_ in pair:
                                xo = dt_ * 520 + 5
                                acc = s_xc[:, dt_ * 512:(dt_ + 1) * 512]
                                ins = e.scalar_tensor_tensor(
                                    out=acc,
                                    in0=s_xb[:, xo + k:xo + k + 512],
                                    scalar=s_cw[:, dt_ * 4 + k:
                                                 dt_ * 4 + k + 1],
                                    in1=acc, op0=ALU.mult, op1=ALU.add)
                    return ins
                cur["conv"] = seq.stage(
                    "vector", conv, waits=((sm_x.s, xv),),
                    after=[pv.get("projXS", quant_tail), pv.get("evq3", 0),
                           pv.get("ln1c", 0)])
                if ci == 0:
                    trunc_point("mc")

                def projB(e, s_xb=s_xb):
                    ins = None
                    for j in range(ST):
                        for dt_ in range(DT):
                            ins = e.matmul(
                                PS[j][:],
                                s_wx[:, dt_ * 1024 + j * 128:
                                     dt_ * 1024 + (j + 1) * 128],
                                s_xb[:, dt_ * 520 + 8:dt_ * 520 + 520],
                                start=(dt_ == 0), stop=(dt_ == DT - 1))
                    return ins
                cur["projB"] = seq.stage(
                    "tensor", projB, waits=((sm_x.s, xv),),
                    after=[pv.get("evXS", quant_tail), pv.get("evC", 0),
                           pv.get("evq3", 0)])
                if ci == 0:
                    trunc_point("mpb")
                def projD(e, s_xb=s_xb):
                    ins = None
                    for dt_ in range(DT):
                        ins = e.matmul(
                            p_mu[:], s_wdrep[:, dt_ * 128:(dt_ + 1) * 128],
                            s_xb[:, dt_ * 520 + 8:dt_ * 520 + 520],
                            start=(dt_ == 0), stop=(dt_ == DT - 1))
                    return ins
                cur["projD"] = seq.stage(
                    "tensor", projD, waits=((sm_x.s, xv),),
                    after=[pv.get("sp1", quant_tail), pv.get("ln1a", 0)])

                def sp1(e):
                    e.activation(out=s_tm[:], in_=p_mu[:], func=AF.Abs)
                    e.activation(out=s_va[:], in_=p_mu[:], func=AF.Relu)
                    e.activation(out=s_tm[:], in_=s_tm[:], func=AF.Exp,
                                 scale=-1.0)
                    sjk(e)
                    return e.activation(out=s_tm[:], in_=s_tm[:], func=AF.Ln,
                                        bias=1.0)
                cur["sp1"] = seq.stage(
                    "scalar", sp1,
                    after=[cur["projD"], pv.get("ln1a", 0),
                           pv.get("sp2", 0)])

                def sp2(e):
                    return e.tensor_tensor(out=s_dc[:], in0=s_va[:],
                                           in1=s_tm[:], op=ALU.add)
                cur["sp2"] = seq.stage("vector", sp2, after=[cur["sp1"]])

                def dax(e):
                    ins = None
                    for j in range(ST):
                        ins = e.activation(
                            out=s_dA[:, j * 512:(j + 1) * 512], in_=s_dc[:],
                            func=AF.Exp, scale=s_A[:, j:j + 1])
                    return ins
                cur["dax"] = seq.stage("scalar", dax, after=[cur["sp2"]])

                def evB(e):
                    ins = None
                    for j in range(ST):
                        ins = e.tensor_copy(out=s_B[:, j * 512:(j + 1) * 512],
                                            in_=PS[j][:])
                    return ins
                cur["evB"] = seq.stage(
                    "vector", evB,
                    after=[cur["projB"], pv.get("buscan", 0)])
                if ci == 0:
                    trunc_point("mb")

                def projXS(e):
                    ins = None
                    for j in range(ST):
                        for dt_ in range(DT):
                            ins = e.matmul(
                                PS[j][:],
                                s_wi[:, dt_ * 512 + j * 128:
                                     dt_ * 512 + (j + 1) * 128],
                                s_xc[:, dt_ * 512:(dt_ + 1) * 512],
                                start=(dt_ == 0), stop=(dt_ == DT - 1))
                    return ins
                cur["projXS"] = seq.stage(
                    "tensor", projXS, after=[cur["conv"], cur["evB"]])

                def evXS(e):
                    ins = None
                    for j in range(ST):
                        ins = e.tensor_tensor(
                            out=s_Bu[:, j * 512:(j + 1) * 512], in0=PS[j][:],
                            in1=s_B[:, j * 512:(j + 1) * 512], op=ALU.mult)
                    return ins
                cur["evXS"] = seq.stage(
                    "vector", evXS,
                    after=[cur["projXS"], pv.get("buscan", 0)])

                if own:
                    def projC(e, s_xb=s_xb):
                        ins = None
                        for j in range(ST):
                            for dt_ in range(DT):
                                ins = e.matmul(
                                    PS[j][:],
                                    s_wx[:, dt_ * 1024 + 512 + j * 128:
                                         dt_ * 1024 + 512 + (j + 1) * 128],
                                    s_xb[:, dt_ * 520 + 8:dt_ * 520 + 520],
                                    start=(dt_ == 0), stop=(dt_ == DT - 1))
                        return ins
                    cur["projC"] = seq.stage("tensor", projC,
                                             after=[cur["evXS"]])

                    def evC(e):
                        ins = None
                        for j in range(ST):
                            ins = e.tensor_copy(
                                out=s_C[:, j * 512:(j + 1) * 512],
                                in_=PS[j][:])
                        return ins
                    cur["evC"] = seq.stage(
                        "vector", evC,
                        after=[cur["projC"], pv.get("buscan", 0)])


                def buscan(e, own=own):
                    for j in range(ST):
                        o = j * 512
                        e.tensor_tensor(out=s_Bu[:, o:o + 512],
                                        in0=s_Bu[:, o:o + 512], in1=s_dc[:],
                                        op=ALU.mult)
                    for j in range(ST):
                        o = j * 512
                        e.tensor_tensor_scan(
                            out=s_h[:, o:o + 512], data0=s_dA[:, o:o + 512],
                            data1=s_Bu[:, o:o + 512],
                            initial=s_carry[:, j:j + 1],
                            op0=ALU.mult, op1=ALU.add)
                    for j in range(ST):
                        o = j * 512
                        ins = e.tensor_copy(out=s_carry[:, j:j + 1],
                                            in_=s_h[:, o + 511:o + 512])
                    if own:
                        for j in range(ST):
                            o = j * 512
                            ins = e.tensor_tensor(out=s_hm[:, o:o + 512],
                                                  in0=s_h[:, o:o + 512],
                                                  in1=s_C[:, o:o + 512],
                                                  op=ALU.mult)
                    return ins
                bus_after = [cur["dax"], cur["evB"], cur["evXS"]]
                if own:
                    bus_after.append(cur["evC"])
                cur["buscan"] = seq.stage("vector", buscan,
                                          after=bus_after)
                if ci == 0:
                    trunc_point("ms")
                if ci == 2:
                    trunc_point("ms3")

                if not own:
                    x_used[ci] = cur["buscan"]
                    pv = cur
                    continue

                prev_evq = cur["evC"]
                for quad in range(4):
                    def opq(e, quad=quad):
                        ins = None
                        for dt_ in range(quad * 4, quad * 4 + 4):
                            pb = PS[dt_ % 4]
                            for j in range(ST):
                                ins = e.matmul(
                                    pb[:],
                                    s_wo[:, j * D + dt_ * 128:
                                         j * D + (dt_ + 1) * 128],
                                    s_hm[:, j * 512:(j + 1) * 512],
                                    start=(j == 0), stop=(j == ST - 1))
                        return ins
                    cur[f"opq{quad}"] = seq.stage(
                        "tensor", opq,
                        after=[cur["buscan"], prev_evq])

                    def evq(e, quad=quad, s_xb=s_xb):
                        for dt_ in range(quad * 4, quad * 4 + 4):
                            pb = PS[dt_ % 4]
                            xo = dt_ * 520 + 8
                            e.scalar_tensor_tensor(
                                out=s_xb[:, xo:xo + 512], in0=pb[:], scalar=1.0,
                                in1=s_xb[:, xo:xo + 512],
                                op0=ALU.mult, op1=ALU.add)
                        ins = None
                        for dt_ in range(quad * 4, quad * 4 + 4):
                            xo = dt_ * 520 + 8
                            ins = e.scalar_tensor_tensor(
                                out=s_xb[:, xo:xo + 512],
                                in0=s_xc[:, dt_ * 512:(dt_ + 1) * 512],
                                scalar=s_D[:, dt_:dt_ + 1],
                                in1=s_xb[:, xo:xo + 512],
                                op0=ALU.mult, op1=ALU.add)
                        return ins
                    prev_evq = seq.stage("vector", evq,
                                         after=[cur[f"opq{quad}"]])
                    cur[f"evq{quad}"] = prev_evq

                def sqs(e, s_xb=s_xb):
                    ins = None
                    for dt_ in range(DT):
                        xo = dt_ * 520 + 8
                        ins = e.activation(
                            out=s_sq[:, dt_ * 512:(dt_ + 1) * 512],
                            in_=s_xb[:, xo:xo + 512], func=AF.Square)
                    return ins
                cur["sqs"] = seq.stage(
                    "scalar", sqs,
                    after=[cur["evq3"], pv.get("evt7", 0)],
                    waits=((sm_1.s, sm_1.v),))

                def stat_mm(e, s_xb=s_xb):
                    ins = None
                    for dt_ in range(DT):
                        xo = dt_ * 520 + 8
                        e.matmul(p_mu[:], s_ones[:], s_xb[:, xo:xo + 512],
                                 start=(dt_ == 0), stop=(dt_ == DT - 1))
                    for dt_ in range(DT):
                        ins = e.matmul(p_vr[:], s_ones[:],
                                       s_sq[:, dt_ * 512:(dt_ + 1) * 512],
                                       start=(dt_ == 0), stop=(dt_ == DT - 1))
                    return ins
                cur["stat_mm"] = seq.stage("tensor", stat_mm,
                                           after=[cur["sqs"]])

                def ln1a(e):
                    e.tensor_scalar(out=s_mu[:], in0=p_mu[:], scalar1=1.0 / D,
                                    scalar2=None, op0=ALU.mult)
                    e.tensor_scalar(out=s_va[:], in0=p_vr[:], scalar1=1.0 / D,
                                    scalar2=LNEPS, op0=ALU.mult, op1=ALU.add)
                    e.tensor_tensor(out=s_tm[:], in0=s_mu[:], in1=s_mu[:],
                                    op=ALU.mult)
                    vjk(e)
                    return e.tensor_tensor(out=s_va[:], in0=s_va[:],
                                           in1=s_tm[:], op=ALU.subtract)
                cur["ln1a"] = seq.stage("vector", ln1a,
                                        after=[cur["stat_mm"]])

                def ln1b(e):
                    e.activation(out=s_rs[:], in_=s_va[:], func=AF.Ln)
                    sjk(e)
                    return e.activation(out=s_rs[:], in_=s_rs[:], func=AF.Exp,
                                        scale=-0.5)
                cur["ln1b"] = seq.stage("scalar", ln1b, after=[cur["ln1a"]])

                def ln1c(e, s_xb=s_xb):
                    ins = None
                    for da in range(0, DT, 2):
                        for dt_ in (da, da + 1):
                            xo = dt_ * 520 + 8
                            o = dt_ * 512
                            e.tensor_tensor(out=s_x1f[:, o:o + 512],
                                            in0=s_xb[:, xo:xo + 512],
                                            in1=s_mu[:], op=ALU.subtract)
                        for dt_ in (da, da + 1):
                            o = dt_ * 512
                            ins = e.tensor_tensor(out=s_x1f[:, o:o + 512],
                                                  in0=s_x1f[:, o:o + 512],
                                                  in1=s_rs[:], op=ALU.mult)
                    return ins
                cur["ln1c"] = seq.stage("vector", ln1c, after=[cur["ln1b"]])
                x_used[ci] = cur["ln1c"]
                if ci == NPC:
                    trunc_point("ml")

                evt_ids = {-2: pv.get("evt6", 0), -1: pv.get("evt7", 0)}
                for m in range(MT):
                    for half in range(2):
                        k = m * 2 + half
                        ptb = p_tb if k % 2 == 0 else p_tb2

                        def trx(e, m=m, half=half, ptb=ptb):
                            ins = None
                            for bi in range(8):
                                dt_ = half * 8 + bi
                                ins = e.transpose(
                                    ptb[:, bi * 128:(bi + 1) * 128],
                                    s_x1f[:, dt_ * 512 + m * 128:
                                          dt_ * 512 + (m + 1) * 128],
                                    s_eye[:])
                            return ins
                        tid = seq.stage(
                            "tensor", trx,
                            after=[cur["ln1c"], evt_ids.get(k - 2, 0)])

                        def evt(e, m=m, half=half, ptb=ptb):
                            return e.tensor_copy(
                                out=s_x1t[:, m * D + half * 1024:
                                          m * D + (half + 1) * 1024],
                                in_=ptb[:])
                        w_ = ((sm_1.s, sm_1.v),) if k == 0 else ()
                        evt_ids[k] = seq.stage("vector", evt, after=[tid],
                                               waits=w_)
                cur["evt6"] = evt_ids[6]
                cur["evt7"] = evt_ids[7]

                def stx1(e, oc=oc):
                    r = [
                        e.dma_start(
                            out=x1fm_h[oc, :, :].rearrange(
                                "(t p) c -> p t c", p=128),
                            in_=s_x1f[:].rearrange("p (t c) -> p t c", t=DT)),
                        e.dma_start(
                            out=x1tk_h[oc * 512:(oc + 1) * 512, :].rearrange(
                                "(m p) c -> p m c", p=128),
                            in_=s_x1t[:].rearrange("p (m c) -> p m c", m=MT)),
                    ]
                    if "x1" in tp:
                        r.append(e.dma_start(
                            out=tp["x1"][oc * 512:(oc + 1) * 512, :].rearrange(
                                "(m p) c -> p m c", p=128),
                            in_=s_x1t[:].rearrange("p (m c) -> p m c", m=MT)))
                    return r
                seq.dma("sync", stx1, sm_1, 3 if "x1" in tp else 2)
                pv = cur

            if "carry" in tp:
                def tcar(e):
                    return e.dma_start(out=tp["carry"][:], in_=s_carry[:])
                seq.dma("sync", tcar, sm_o, 1)
            x1_all = sm_1.v
            seq.flush()
        wscope.close()
        trunc_point("mamba")

        with ExitStack() as fctx:
            fsb = lambda nm, shp, dt=F32: fctx.enter_context(
                nc.sbuf_tensor(nm, shp, dt))
            s_za = [fsb(f"s_z{m}", [128, N]) for m in range(MT)]
            s_zoneA = fsb("s_zoneA", [128, N], BF16)
            s_zoneB = fsb("s_zoneB", [128, 2 * N], BF16)
            s_x1k = fsb("s_x1k", [128, MT * D], BF16)
            s_bs = fsb("s_bs", [128, 2 * 512], BF16)
            s_zp = fsb("s_zp", [128, MT * 16])
            s_pk = fsb("s_pk", [128, MT * 16])
            s_w8 = fsb("s_w8", [128, 8])
            s_sg = fsb("s_sg", [128, 8])

            def r2m(m):
                return s_za[m][:].bitcast(BF16)[:, N:N + D]

            def ffn_init(e):
                e.memset(s_pk[:], 0.0)
                return e.memset(s_zp[:], 0.0)
            g_prev_psum = seq.stage("vector", ffn_init)
            g_prev_za = 0
            g_prev_mm2 = 0
            g_prev_lnap = 0
            g_prev_sov = {}
            s_x1g = s_zoneA
            s_hidt = s_zoneA
            s_to1 = s_zoneA
            s_ts = (s_zoneB[:, 0:N], s_zoneB[:, N:2 * N])
            s_wv = s_zoneB[:].bitcast(F32)
            s_to0 = s_zoneB[:, 0:N]
            s_r3 = s_zoneB[:].bitcast(F32)[:, 0:D]
            s_o2 = s_zoneB[:].bitcast(F32)[:, D:2 * D]


            for g in range(NG):
                with ExitStack() as gctx:
                    zscope = ExitStack()
                    p_z = [zscope.enter_context(nc.psum_tensor(
                        f"p_z_{g}_{i}", [128, 512], F32)) for i in range(8)]

                    def ldx1g(e, g=g):
                        return e.dma_start(
                            out=s_x1g[:].rearrange("p (t c) -> p t c", t=DT),
                            in_=x1fm_h[g, :, :].rearrange(
                                "(t p) c -> p t c", p=128))
                    seq.dma("sync", ldx1g, sm_1, 1,
                            dep=g_prev_mm2, waits=((sm_1.s, x1_all),))
                    x1v = sm_1.v

                    def ldx1k(e, g=g):
                        return e.dma_start(
                            out=s_x1k[:].rearrange("p (m c) -> p m c", m=MT),
                            in_=x1tk_h[g * 512:(g + 1) * 512, :].rearrange(
                                "(m p) c -> p m c", p=128))
                    seq.dma("sync", ldx1k, sm_1, 1,
                            dep=g_prev_lnap, waits=((sm_1.s, x1_all),))
                    x1kv = sm_1.v

                    ts_used = {}
                    ev1_ids = {}
                    for nch in range(NCH):
                        buf = s_ts[nch % 2]
                        def ldt(e, nch=nch, buf=buf):
                            return [e.dma_start(
                                out=buf.rearrange("p (t c) -> p t c", t=DT),
                                in_=t8syn[:, nch * 512:(nch + 1) * 512]
                                .rearrange("(t p) c -> p t c", p=128)),
                                e.dma_start(
                                    out=s_bs[:, (nch % 2) * 512:
                                             (nch % 2) * 512 + 512],
                                    in_=dr["bsyn_r"][0:128,
                                                     nch * 512:(nch + 1) * 512])]
                        ldt_sov = g_prev_sov.get(2 + nch % 2, sm_o.v) \
                            if nch <= 1 else sm_o.v
                        seq.dma("gpsimd", ldt, sm_t, 2,
                                dep=ts_used.get(nch - 2, None),
                                waits=((sm_8.s, t8_done),
                                       (sm_o.s, ldt_sov),
                                       (sm_1.s, x1_all)))
                        tv = sm_t.v

                        par = 4 * (nch % 2)

                        def mm1(e, nch=nch, buf=buf, par=par):
                            ins = None
                            for m in range(MT):
                                for dt_ in range(DT):
                                    ins = e.matmul(
                                        p_z[par + m][:],
                                        s_x1g[:, dt_ * 512 + m * 128:
                                              dt_ * 512 + (m + 1) * 128],
                                        buf[:, dt_ * 512:(dt_ + 1) * 512],
                                        start=(dt_ == 0),
                                        stop=(dt_ == DT - 1))
                            return ins
                        w = [(sm_t.s, tv)]
                        if nch == 0:
                            w.append((sm_1.s, x1v))
                        mm1_after = [ev1_ids.get(nch - 2, 0)]
                        if nch <= 1:
                            mm1_after.append(g_prev_psum)
                        ts_used[nch] = seq.stage("tensor", mm1, waits=w,
                                                 after=mm1_after)

                        def ev1(e, nch=nch, par=par):
                            ins = None
                            for m in range(MT):
                                ins = e.scalar_tensor_tensor(
                                    out=s_za[m][:, nch * 512:(nch + 1) * 512],
                                    in0=p_z[par + m][:], scalar=s_scal[:, 0:1],
                                    in1=s_bs[:, (nch % 2) * 512:
                                             (nch % 2) * 512 + 512],
                                    op0=ALU.mult, op1=ALU.add,
                                    accum_out=s_zp[:, m * 16 + nch:
                                                   m * 16 + nch + 1])
                            return ins
                        ev1_after = [ts_used[nch]]
                        if nch == 0:
                            ev1_after.append(g_prev_za)
                        ev1_ids[nch] = seq.stage("vector", ev1,
                                                 after=ev1_after)

                    if "z" in tp and g == 2:
                        def tz(e):
                            return [e.dma_start(
                                out=tp["z"][m * 128:(m + 1) * 128, :],
                                in_=s_za[m][:]) for m in range(MT)]
                        seq.dma("sync", tz, sm_o, MT)

                    zscope.close()
                    htscope = ExitStack()
                    p_ht = htscope.enter_context(
                        nc.psum_tensor(f"p_ht_{g}", [128, 1024], BF16))
                    p_ht2 = htscope.enter_context(
                        nc.psum_tensor(f"p_ht2_{g}", [128, 1024], BF16))

                    def col(m, j):
                        return s_pk[:, m * 16 + j:m * 16 + j + 1]

                    def tk0a(e):
                        ins = None
                        for m in range(MT):
                            ins = e.activation(out=s_zoneA[:], in_=s_za[m][:],
                                               func=AF.Square,
                                               accum_out=col(m, 13))
                        return ins
                    id_tk0a = seq.stage("scalar", tk0a,
                                        after=[ev1_ids[NCH - 1]])

                    def tk0b(e):
                        for m in range(MT):
                            e.tensor_reduce(out=col(m, 0),
                                            in_=s_zp[:, m * 16:m * 16 + 16],
                                            axis=AX.X, op=ALU.add)
                        for m in range(MT):
                            e.tensor_scalar(out=col(m, 0), in0=col(m, 0),
                                            scalar1=1.0 / N, scalar2=None,
                                            op0=ALU.mult)
                        for m in range(MT):
                            e.tensor_tensor(out=col(m, 11), in0=col(m, 0),
                                            in1=col(m, 0), op=ALU.mult)
                        ins = None
                        for m in range(MT):
                            ins = e.scalar_tensor_tensor(
                                out=col(m, 12), in0=col(m, 13),
                                scalar=1.0 / N, in1=col(m, 11),
                                op0=ALU.mult, op1=ALU.subtract)
                        return ins
                    id_tk0b = seq.stage("vector", tk0b, after=[id_tk0a])

                    def tk0c(e):
                        for m in range(MT):
                            e.activation(out=col(m, 12), in_=col(m, 12),
                                         func=AF.Ln)
                        ins = None
                        for m in range(MT):
                            ins = e.activation(out=col(m, 12), in_=col(m, 12),
                                               func=AF.Exp, scale=0.5)
                        return ins
                    id_tk0c = seq.stage("scalar", tk0c, after=[id_tk0b])

                    def tk0d(e):
                        ins = None
                        for m in range(MT):
                            for c_, coef in ((8, SIG_SPAN), (1, BR_LO),
                                             (2, BR_HI), (5, LAD0)):
                                e.scalar_tensor_tensor(
                                    out=col(m, c_), in0=col(m, 12),
                                    scalar=coef, in1=col(m, 0),
                                    op0=ALU.mult, op1=ALU.add)
                            e.memset(col(m, 3), 8192.0)
                            e.memset(col(m, 4), 0.0)
                            e.memset(col(m, 9), 0.0)
                            ins = e.memset(col(m, 10), -1.0)
                        return ins
                    id_upd = seq.stage("vector", tk0d, after=[id_tk0c])

                    for pi in range(NPROBE):
                        def cnt_v(e):
                            ins = None
                            for m in range(2):
                                ins = e.tensor_scalar(
                                    out=s_zoneA[:], in0=s_za[m][:],
                                    scalar1=col(m, 5),
                                    scalar2=None, op0=ALU.is_ge, op1=ALU.add,
                                    accum_out=col(m, 6))
                            return ins
                        id_cv = seq.stage("vector", cnt_v, after=[id_upd])

                        def cnt_s(e):
                            ins = None
                            for m in range(2, MT):
                                ins = e.activation(
                                    out=s_zoneA[:], in_=s_za[m][:],
                                    func=AF.Sign, scale=-1.0,
                                    bias=col(m, 5),
                                    accum_out=col(m, 6))
                            return ins
                        id_cs = seq.stage("scalar", cnt_s, after=[id_upd])

                        def upd(e, pi=pi, last=(pi == NPROBE - 1)):
                            for m in range(2, MT):
                                e.tensor_scalar(out=col(m, 6), in0=col(m, 6),
                                                scalar1=-0.5,
                                                scalar2=float(N // 2),
                                                op0=ALU.mult, op1=ALU.add)
                            for m in range(MT):
                                e.tensor_scalar(out=col(m, 7), in0=col(m, 6),
                                                scalar1=float(TOPK) - 0.5,
                                                scalar2=None, op0=ALU.is_ge)
                            for dst, src in ((1, 5), (3, 6)):
                                for m in range(MT):
                                    e.tensor_tensor(out=col(m, 11),
                                                    in0=col(m, src),
                                                    in1=col(m, dst),
                                                    op=ALU.subtract)
                                for m in range(MT):
                                    e.tensor_tensor(out=col(m, 11),
                                                    in0=col(m, 11),
                                                    in1=col(m, 7), op=ALU.mult)
                                for m in range(MT):
                                    e.tensor_tensor(out=col(m, dst),
                                                    in0=col(m, dst),
                                                    in1=col(m, 11), op=ALU.add)
                            for m in range(MT):
                                e.tensor_scalar(out=col(m, 7), in0=col(m, 7),
                                                scalar1=-1.0, scalar2=1.0,
                                                op0=ALU.mult, op1=ALU.add)
                            for dst, src in ((2, 5), (4, 6)):
                                for m in range(MT):
                                    e.tensor_tensor(out=col(m, 11),
                                                    in0=col(m, src),
                                                    in1=col(m, dst),
                                                    op=ALU.subtract)
                                for m in range(MT):
                                    e.tensor_tensor(out=col(m, 11),
                                                    in0=col(m, 11),
                                                    in1=col(m, 7), op=ALU.mult)
                                for m in range(MT):
                                    e.tensor_tensor(out=col(m, dst),
                                                    in0=col(m, dst),
                                                    in1=col(m, 11), op=ALU.add)
                            for m in range(MT):
                                e.tensor_scalar(out=col(m, 7), in0=col(m, 6),
                                                scalar1=float(TOPK) - 0.5,
                                                scalar2=None, op0=ALU.is_ge)
                            for m in range(MT):
                                e.tensor_scalar(out=col(m, 11), in0=col(m, 6),
                                                scalar1=float(TOPK + 7),
                                                scalar2=None, op0=ALU.is_le)
                            for m in range(MT):
                                e.tensor_tensor(out=col(m, 7), in0=col(m, 7),
                                                in1=col(m, 11), op=ALU.mult)
                            for dst, src in ((9, 5), (10, 6)):
                                for m in range(MT):
                                    e.tensor_tensor(out=col(m, 11),
                                                    in0=col(m, src),
                                                    in1=col(m, dst),
                                                    op=ALU.subtract)
                                for m in range(MT):
                                    e.tensor_tensor(out=col(m, 11),
                                                    in0=col(m, 11),
                                                    in1=col(m, 7), op=ALU.mult)
                                for m in range(MT):
                                    e.tensor_tensor(out=col(m, dst),
                                                    in0=col(m, dst),
                                                    in1=col(m, 11), op=ALU.add)
                            if pi == 0:
                                for m in range(MT):
                                    e.scalar_tensor_tensor(
                                        out=col(m, 5), in0=col(m, 12),
                                        scalar=LAD1, in1=col(m, 0),
                                        op0=ALU.mult, op1=ALU.add)
                            elif pi < NPROBE - 1 - N_FP:
                                for m in range(MT):
                                    e.tensor_tensor(out=col(m, 5),
                                                    in0=col(m, 1),
                                                    in1=col(m, 2), op=ALU.add)
                                for m in range(MT):
                                    e.tensor_scalar(out=col(m, 5),
                                                    in0=col(m, 5), scalar1=0.5,
                                                    scalar2=None, op0=ALU.mult)
                            elif not last:
                                for m in range(MT):
                                    e.tensor_tensor(out=col(m, 7),
                                                    in0=col(m, 3),
                                                    in1=col(m, 4),
                                                    op=ALU.subtract)
                                for m in range(MT):
                                    e.tensor_scalar(out=col(m, 7),
                                                    in0=col(m, 7),
                                                    scalar1=1e-6, scalar2=None,
                                                    op0=ALU.add)
                                for m in range(MT):
                                    e.reciprocal(out=col(m, 7), in_=col(m, 7))
                                for m in range(MT):
                                    e.tensor_scalar(out=col(m, 11),
                                                    in0=col(m, 3),
                                                    scalar1=-AIM, scalar2=None,
                                                    op0=ALU.add)
                                for m in range(MT):
                                    e.tensor_tensor(out=col(m, 11),
                                                    in0=col(m, 11),
                                                    in1=col(m, 7), op=ALU.mult)
                                for m in range(MT):
                                    e.tensor_tensor(out=col(m, 7),
                                                    in0=col(m, 2),
                                                    in1=col(m, 1),
                                                    op=ALU.subtract)
                                for m in range(MT):
                                    e.tensor_tensor(out=col(m, 11),
                                                    in0=col(m, 11),
                                                    in1=col(m, 7), op=ALU.mult)
                                for m in range(MT):
                                    e.tensor_tensor(out=col(m, 5),
                                                    in0=col(m, 1),
                                                    in1=col(m, 11), op=ALU.add)
                            return vjk(e)
                        id_upd = seq.stage("vector", upd,
                                           after=[id_cs, id_cv])

                    def fin(e):
                        for m in range(MT):
                            e.tensor_scalar(out=col(m, 7), in0=col(m, 10),
                                            scalar1=0.0, scalar2=None,
                                            op0=ALU.is_ge)
                        for dst, src in ((1, 9), (3, 10)):
                            for m in range(MT):
                                e.tensor_tensor(out=col(m, 11),
                                                in0=col(m, src),
                                                in1=col(m, dst),
                                                op=ALU.subtract)
                            for m in range(MT):
                                e.tensor_tensor(out=col(m, 11), in0=col(m, 11),
                                                in1=col(m, 7), op=ALU.mult)
                            for m in range(MT):
                                e.tensor_tensor(out=col(m, dst),
                                                in0=col(m, dst),
                                                in1=col(m, 11), op=ALU.add)
                        for m in range(MT):
                            e.tensor_scalar(out=col(m, 6), in0=col(m, 3),
                                            scalar1=float(-TOPK), scalar2=0.0,
                                            op0=ALU.add, op1=ALU.max)
                        for m in range(MT):
                            e.tensor_scalar(out=col(m, 6), in0=col(m, 6),
                                            scalar1=7.0, scalar2=None,
                                            op0=ALU.min)
                        for m in range(MT):
                            e.tensor_scalar(out=col(m, 14), in0=col(m, 6),
                                            scalar1=-0.25, scalar2=None,
                                            op0=ALU.add)
                        ins = None
                        for m in range(MT):
                            ins = e.tensor_scalar(out=col(m, 15),
                                                  in0=col(m, 6),
                                                  scalar1=0.6, scalar2=None,
                                                  op0=ALU.add)
                        return ins
                    seq.stage("vector", fin)

                    def tcm(e):
                        ins = None
                        for m in range(MT):
                            ins = e.tensor_scalar(
                                out=col(m, 7), in0=col(m, 1), scalar1=-1e-5,
                                scalar2=0.0, op0=ALU.add, op1=ALU.max)
                        return ins
                    seq.stage("vector", tcm)

                    evh_ids = {}
                    for m in range(MT):
                        zs = s_za[m][:]
                        hTm = s_za[m][:].bitcast(BF16)[:, 0:N]
                        pkb = m * 16

                        def hid(e, pkb=pkb, zs=zs):
                            return e.scalar_tensor_tensor(
                                out=s_hidt[:], in0=zs,
                                scalar=s_pk[:, pkb + 7:pkb + 8], in1=zs,
                                op0=ALU.is_ge, op1=ALU.mult)
                        id_hid = seq.stage("vector", hid)

                        for half in range(8):
                            kg = m * 8 + half
                            pht = p_ht if kg % 2 == 0 else p_ht2

                            def trh(e, half=half, pht=pht):
                                ins = None
                                for bi in range(8):
                                    nt = half * 8 + bi
                                    ins = e.transpose(
                                        pht[:, bi * 128:(bi + 1) * 128],
                                        s_hidt[:, nt * 128:(nt + 1) * 128],
                                        s_eye[:])
                                return ins
                            id_trh = seq.stage(
                                "tensor", trh,
                                after=[id_hid, evh_ids.get(kg - 2, 0)])

                            def evh(e, m=m, half=half, hTm=hTm, pht=pht):
                                ins = None
                                for bi in range(8):
                                    nt = half * 8 + bi
                                    ins = e.tensor_copy(
                                        out=hTm[:, nt * 128:(nt + 1) * 128],
                                        in_=pht[:, bi * 128:(bi + 1) * 128])
                                return ins
                            evh_ids[kg] = seq.stage("vector", evh,
                                                    after=[id_trh])

                    if "thr" in tp and g == 2:
                        def tthr(e):
                            return e.dma_start(out=tp["thr"][:],
                                               in_=s_pk[:])
                        seq.dma("sync", tthr, sm_o, 1)

                    htscope.close()
                    p_o = [gctx.enter_context(nc.psum_tensor(
                        f"p_o_{g}_{i}", [128, 512], F32)) for i in range(8)]
                    to_used = {}
                    ev2_ids = {}
                    for dh in range(2):
                        for ntq in range(NT // 4):
                            buf = s_to0 if ntq % 2 == 0 else s_to1

                            def ldo(e, ntq=ntq, dh=dh, buf=buf):
                                return e.dma_start(
                                    out=buf[:, 0:4096].rearrange(
                                        "p (a c) -> p a c", a=4),
                                    in_=t8out[ntq * 512:(ntq + 1) * 512,
                                              dh * 1024:(dh + 1) * 1024]
                                    .rearrange("(a p) c -> p a c", p=128))
                            ldep = to_used.get((dh, ntq - 2), None)
                            if dh == 0 and ntq == 0:
                                ldep = ts_used[NCH - 1]
                            seq.dma("gpsimd", ldo, sm_t, 1, dep=ldep)
                            tv = sm_t.v

                            def mm2(e, dh=dh, ntq=ntq, buf=buf):
                                ins = None
                                for m in range(MT):
                                    hTm = s_za[m][:].bitcast(BF16)
                                    for a in range(4):
                                        nt = ntq * 4 + a
                                        first = (ntq == 0 and a == 0)
                                        last = (ntq == NT // 4 - 1 and a == 3)
                                        for dq in range(2):
                                            ins = e.matmul(
                                                p_o[m * 2 + dq][:],
                                                hTm[:, nt * 128:
                                                    (nt + 1) * 128],
                                                buf[:, a * 1024 + dq * 512:
                                                    a * 1024 + (dq + 1) * 512],
                                                start=first, stop=last)
                                return ins
                            to_used[(dh, ntq)] = seq.stage(
                                "tensor", mm2, waits=((sm_t.s, tv),))

                        def ev2(e, dh=dh):
                            ins = None
                            for m in range(MT):
                                r2 = r2m(m)
                                for dq in range(2):
                                    do_ = dh * 1024 + dq * 512
                                    ins = e.scalar_tensor_tensor(
                                        out=r2[:, do_:do_ + 512],
                                        in0=p_o[m * 2 + dq][:],
                                        scalar=s_scal[:, 1:2],
                                        in1=s_x1k[:, m * D + do_:
                                                  m * D + do_ + 512],
                                        op0=ALU.mult, op1=ALU.add)
                            return ins
                        ev2_ids[dh] = seq.stage(
                            "vector", ev2, waits=((sm_1.s, x1kv),))

                    zBf = s_zoneB[:].bitcast(F32)
                    s_lnd = s_zoneA[:].bitcast(F32)[:, 0:D]

                    def r3q(m):
                        return zBf[:, (m % 2) * 2 * D:(m % 2) * 2 * D + D]

                    def o2q(m):
                        return zBf[:, (m % 2) * 2 * D + D:
                                   (m % 2) * 2 * D + 2 * D]

                    def pcol(m, j):
                        return s_pk[:, m * 16 + j:m * 16 + j + 1]

                    ln_sov = sm_o.v
                    sov = {}
                    ln2sq2_ids = {}
                    lni2_ids = {}
                    lnap2_ids = {}
                    for pr in range(2):
                        mm_ = (2 * pr, 2 * pr + 1)

                        def lni2(e, mm_=mm_):
                            for m in mm_:
                                e.tensor_tensor(out=r3q(m), in0=r2m(m),
                                                in1=s_bout[:], op=ALU.add)
                            ins = None
                            for m in mm_:
                                ins = e.tensor_scalar(
                                    out=s_lnd, in0=r3q(m),
                                    scalar1=1.0, scalar2=None, op0=ALU.mult,
                                    op1=ALU.add, accum_out=pcol(m, 0))
                            return ins
                        lni2_ids[pr] = seq.stage(
                            "vector", lni2, waits=((sm_o.s, ln_sov),),
                            after=[ln2sq2_ids.get(pr - 1, 0), ev2_ids[1]])

                        def lnsq2(e, mm_=mm_):
                            ins = None
                            for m in mm_:
                                ins = e.activation(
                                    out=s_lnd, in_=r3q(m), func=AF.Square,
                                    accum_out=pcol(m, 1))
                            return ins
                        id_lnsq = seq.stage("scalar", lnsq2,
                                            after=[lni2_ids[pr]])

                        def lnab2(e, mm_=mm_):
                            for m in mm_:
                                e.tensor_scalar(out=pcol(m, 0),
                                                in0=pcol(m, 0),
                                                scalar1=1.0 / D,
                                                scalar2=None, op0=ALU.mult)
                            for m in mm_:
                                e.tensor_scalar(out=pcol(m, 2),
                                                in0=pcol(m, 1),
                                                scalar1=1.0 / D,
                                                scalar2=LNEPS, op0=ALU.mult,
                                                op1=ALU.add)
                            for m in mm_:
                                e.tensor_tensor(out=pcol(m, 8),
                                                in0=pcol(m, 0),
                                                in1=pcol(m, 0), op=ALU.mult)
                            ins = None
                            for m in mm_:
                                ins = e.tensor_tensor(out=pcol(m, 2),
                                                      in0=pcol(m, 2),
                                                      in1=pcol(m, 8),
                                                      op=ALU.subtract)
                            return ins
                        id_lnab = seq.stage("vector", lnab2, after=[id_lnsq])

                        def lnrs2(e, mm_=mm_):
                            for m in mm_:
                                e.activation(out=pcol(m, 3), in_=pcol(m, 2),
                                             func=AF.Ln)
                            ins = None
                            for m in mm_:
                                ins = e.activation(out=pcol(m, 3),
                                                   in_=pcol(m, 3),
                                                   func=AF.Exp, scale=-0.5)
                            return ins
                        id_lnrs = seq.stage("scalar", lnrs2, after=[id_lnab])

                        def lnap2(e, mm_=mm_):
                            for m in mm_:
                                e.tensor_scalar(out=r3q(m), in0=r3q(m),
                                                scalar1=pcol(m, 0),
                                                scalar2=pcol(m, 3),
                                                op0=ALU.subtract,
                                                op1=ALU.mult)
                            for m in mm_:
                                e.tensor_tensor(out=r3q(m), in0=r3q(m),
                                                in1=s_gam[:], op=ALU.mult)
                            for m in mm_:
                                e.tensor_tensor(out=r3q(m), in0=r3q(m),
                                                in1=s_bet[:], op=ALU.add)
                            for m in mm_:
                                e.tensor_tensor(out=r3q(m), in0=r3q(m),
                                                in1=s_x1k[:,
                                                         m * D:(m + 1) * D],
                                                op=ALU.add)
                            ins = None
                            for m in mm_:
                                ins = e.tensor_scalar(
                                    out=s_lnd, in0=r3q(m), scalar1=1.0,
                                    scalar2=None, op0=ALU.mult, op1=ALU.add,
                                    accum_out=pcol(m, 4))
                            return ins
                        lnap2_ids[pr] = seq.stage("vector", lnap2,
                                                  after=[id_lnrs])

                        def ln2sq2(e, mm_=mm_):
                            ins = None
                            for m in mm_:
                                ins = e.activation(
                                    out=s_lnd, in_=r3q(m), func=AF.Square,
                                    accum_out=pcol(m, 5))
                            return ins
                        ln2sq2_ids[pr] = seq.stage("scalar", ln2sq2,
                                                   after=[lnap2_ids[pr]])

                        def ln2ab2(e, mm_=mm_):
                            for m in mm_:
                                e.tensor_scalar(out=pcol(m, 4),
                                                in0=pcol(m, 4),
                                                scalar1=1.0 / D,
                                                scalar2=None, op0=ALU.mult)
                            for m in mm_:
                                e.tensor_scalar(out=pcol(m, 6),
                                                in0=pcol(m, 5),
                                                scalar1=1.0 / D,
                                                scalar2=LNEPS, op0=ALU.mult,
                                                op1=ALU.add)
                            for m in mm_:
                                e.tensor_tensor(out=pcol(m, 9),
                                                in0=pcol(m, 4),
                                                in1=pcol(m, 4), op=ALU.mult)
                            ins = None
                            for m in mm_:
                                ins = e.tensor_tensor(out=pcol(m, 6),
                                                      in0=pcol(m, 6),
                                                      in1=pcol(m, 9),
                                                      op=ALU.subtract)
                            return ins
                        id_ln2ab = seq.stage("vector", ln2ab2,
                                             after=[ln2sq2_ids[pr]])

                        def ln2rs2(e, mm_=mm_):
                            for m in mm_:
                                e.activation(out=pcol(m, 7), in_=pcol(m, 6),
                                             func=AF.Ln)
                            ins = None
                            for m in mm_:
                                ins = e.activation(out=pcol(m, 7),
                                                   in_=pcol(m, 7),
                                                   func=AF.Exp, scale=-0.5)
                            return ins
                        id_ln2rs = seq.stage("scalar", ln2rs2,
                                             after=[id_ln2ab])

                        def ln2ap2(e, mm_=mm_):
                            ins = None
                            for m in mm_:
                                ins = e.tensor_scalar(out=o2q(m), in0=r3q(m),
                                                      scalar1=pcol(m, 4),
                                                      scalar2=pcol(m, 7),
                                                      op0=ALU.subtract,
                                                      op1=ALU.mult)
                            return ins
                        id_ln2ap = seq.stage(
                            "vector", ln2ap2, after=[id_ln2rs],
                            waits=((sm_o.s, sov.get(pr - 1, ln_sov)),))

                        def sto2(e, g=g, mm_=mm_):
                            r = []
                            for m in mm_:
                                r0 = g * 512 + m * 128
                                r.append(e.dma_start(out=out[r0:r0 + 128, :],
                                                     in_=o2q(m)))
                            return r
                        seq.dma("sync", sto2, sm_o, 2, dep=id_ln2ap)
                        sov[pr] = sm_o.v

                    g_prev_psum = ev2_ids[1]
                    g_prev_za = lni2_ids[1]
                    g_prev_mm2 = to_used[(1, NT // 4 - 1)]
                    g_prev_lnap = lnap2_ids[1]
                    g_prev_sov = {2: sov[1], 3: sov[1]}
                    seq.flush()
                trunc_point(f"g{g}")

            def final(e):
                e.wait_ge(sm_o.s, sm_o.v)
                return None
            seq.stage("sync", final, inc=False)
            seq.flush()

    return nc, dr, tp



def stage_inputs(inputs):
    x = np.ascontiguousarray(np.asarray(inputs["L0_attn_x"], dtype=np.float32))
    Wx = np.asarray(inputs["L0_attn_Wxproj"], dtype=np.float32)
    Wi = np.asarray(inputs["L0_attn_Winproj"], dtype=np.float32)
    Wo = np.asarray(inputs["L0_attn_Woutproj"], dtype=np.float32)
    cwn = np.asarray(inputs["L0_attn_conv_w"], dtype=np.float32)
    cbn = np.asarray(inputs["L0_attn_conv_b"], dtype=np.float32)
    Alog = np.asarray(inputs["L0_attn_A_log"], dtype=np.float32)
    Dv = np.asarray(inputs["L0_attn_D"], dtype=np.float32)
    Wsyn = np.asarray(inputs["L0_ffn_W_syn"], dtype=np.float32)
    bsyn = np.asarray(inputs["L0_ffn_b_syn"], dtype=np.float32)
    Wout = np.asarray(inputs["L0_ffn_W_out"], dtype=np.float32)
    bout = np.asarray(inputs["L0_ffn_b_out"], dtype=np.float32)
    gam = np.asarray(inputs["L0_ffn_gamma"], dtype=np.float32)
    bet = np.asarray(inputs["L0_ffn_beta"], dtype=np.float32)

    shared = {
        "wxT": np.ascontiguousarray(Wx.T),
        "wiT": np.ascontiguousarray(Wi.T),
        "woT": np.ascontiguousarray(Wo.T),
        "cwv": np.ascontiguousarray(
            cwn[:, 0, :].reshape(DT, 128, 4).transpose(1, 0, 2).reshape(128, DT * 4)),
        "cbv": np.ascontiguousarray(cbn.reshape(DT, 128).T),
        "alog": np.ascontiguousarray(Alog.reshape(ST, 128).T),
        "dvec": np.ascontiguousarray(Dv.reshape(DT, 128).T),
        "wsynT": np.ascontiguousarray(Wsyn.T),
        "woutT": np.ascontiguousarray(Wout.T),
        "bsyn_r": np.ascontiguousarray(np.broadcast_to(bsyn[None, :], (128, N))),
        "bout_r": np.ascontiguousarray(np.broadcast_to(bout[None, :], (128, D))),
        "gam_r": np.ascontiguousarray(np.broadcast_to(gam[None, :], (128, D))),
        "bet_r": np.ascontiguousarray(np.broadcast_to(bet[None, :], (128, D))),
    }

    in_maps = []
    for c in range(8):
        seqi, idx = divmod(c, 4)
        start = idx * CT
        xo = np.zeros((CT + 4, D), np.float32)
        lo = start - 4
        src0 = max(lo, 0)
        xo[src0 - lo:, :] = x[seqi, src0:start + CT, :]
        xp = np.zeros((PRE + 4, D), np.float32)
        lo2 = start - PRE - 4
        src0 = max(lo2, 0)
        if start > 0:
            xp[src0 - lo2:, :] = x[seqi, src0:start, :]
        m = dict(shared)
        m["x_own"] = np.ascontiguousarray(xo.T)
        m["x_pre"] = np.ascontiguousarray(xp.T)
        in_maps.append(m)
    return in_maps


_NC_CACHE = {}


def kernel(**inputs):
    key = ()
    if key not in _NC_CACHE:
        _NC_CACHE[key] = build_nc()
    nc, dr, tp = _NC_CACHE[key]
    in_maps = stage_inputs(inputs)
    res = run_bass_kernel_spmd(nc, in_maps, core_ids=list(range(8)))
    outs = [np.asarray(res.results[c]["out"], dtype=np.float32)
            for c in range(8)]
    return np.concatenate(outs, axis=0).reshape(B_, L_, D)


# version 20
# speedup vs baseline: 1.0277x; 1.0277x over previous
import numpy as np
from contextlib import ExitStack

import concourse.bass as bass
import concourse.mybir as mybir
from concourse.bass_utils import run_bass_kernel_spmd

F32 = mybir.dt.float32
BF16 = mybir.dt.bfloat16
FP8 = mybir.dt.float8e4
I32 = mybir.dt.int32
AF = mybir.ActivationFunctionType
ALU = mybir.AluOpType
AX = mybir.AxisListType

B_, L_, D, S, N = 2, 8192, 2048, 512, 8192
CT, PRE = 2048, 6144
TOPK = 819
LNEPS = 1e-5
NG, G = 4, 512
MT, DT, ST, NT = 4, 16, 4, 64
NPC, NLC, NCH = 12, 4, 16
AIM = float(TOPK) + 3.0
SIG_SPAN = 30.0
BR_LO = 0.8
BR_HI = 2.0
LAD0 = 1.38
LAD1 = 1.15
NPROBE = 7
N_FP = 3


ENGS = ("tensor", "vector", "scalar", "gpsimd", "sync")


class Seq:

    def __init__(self, nc, ctx):
        self.nc = nc
        self.chains = {e: ctx.enter_context(nc.semaphore(f"ch_{e}"))
                       for e in ENGS}
        self.counts = {e: 0 for e in ENGS}
        self.where = {}
        self.n = 0
        self.stages = []

    def stage(self, eng, fn, dep=None, after=None, waits=(), inc=True,
              dma=None):
        if after is None:
            after = [self.n if dep is None else dep]
        elif dep is not None:
            after = list(after) + [dep]
        cw = {}
        for sid in after:
            if not sid:
                continue
            deng, didx = self.where[sid]
            cw[deng] = max(cw.get(deng, 0), didx)
        chw = tuple((self.chains[k], v) for k, v in cw.items())
        if dma is not None:
            dma[0].v += 16 * dma[1]
        self.stages.append((eng, chw, tuple(waits), fn, inc, dma))
        if inc:
            self.counts[eng] += 1
            self.n += 1
            self.where[self.n] = (eng, self.counts[eng])
            return self.n
        return None

    def dma(self, eng, fn, sem, n, dep=None, after=None, waits=()):
        return self.stage(eng, fn, dep=dep, after=after, waits=waits,
                          inc=False, dma=(sem, n))

    def flush(self):
        stages = self.stages
        self.stages = []
        if not stages:
            return
        nc = self.nc
        used = sorted({s[0] for s in stages})
        with nc.Block() as blk:
            for name in used:
                def mk(name):
                    def run(e):
                        for eng, chw, waits, fn, inc, dma in stages:
                            if eng != name:
                                continue
                            for sem, val in chw:
                                e.wait_ge(sem, val)
                            for sem, val in waits:
                                e.wait_ge(sem, val)
                            r = fn(e)
                            if dma is not None:
                                insts = r if isinstance(r, (list, tuple)) else \
                                    ([] if r is None else [r])
                                assert len(insts) == dma[1], \
                                    f"dma stage returned {len(insts)} != {dma[1]}"
                                for ins in insts:
                                    ins.then_inc(dma[0].s, 16)
                            elif inc:
                                assert r is not None and \
                                    not isinstance(r, (list, tuple)), \
                                    "stage must return its last instruction"
                                r.then_inc(self.chains[name], 1)
                    return run
                getattr(blk, name)(mk(name))


class Dsem:
    def __init__(self, ctx, nc, name):
        self.s = ctx.enter_context(nc.semaphore(name))
        self.v = 0


class _TruncDone(Exception):
    pass


def build_nc(debug_taps=(), trunc=None):
    nc = bass.Bass("TRN2", target_bir_lowering=False, debug=False,
                   detect_race_conditions=False)
    taps = set(debug_taps)
    dr = {}

    def din(name, shape, dt=F32):
        dr[name] = nc.dram_tensor(name, shape, dt, kind="ExternalInput").ap()
        return dr[name]

    def dint(name, shape, dt):
        dr[name] = nc.dram_tensor(name, shape, dt, kind="Internal").ap()
        return dr[name]

    def dout(name, shape, dt=F32):
        dr[name] = nc.dram_tensor(name, shape, dt, kind="ExternalOutput").ap()
        return dr[name]

    x_own = din("x_own", [D, CT + 4])
    x_pre = din("x_pre", [D, PRE + 4])
    wxT = din("wxT", [D, 1025])
    wiT = din("wiT", [D, S])
    woT = din("woT", [S, D])
    din("cwv", [128, DT * 4])
    din("cbv", [128, DT])
    din("alog", [128, ST])
    din("dvec", [128, DT])
    wsynT = din("wsynT", [D, N])
    woutT = din("woutT", [N, D])
    din("bsyn_r", [128, N])
    din("bout_r", [128, D])
    din("gam_r", [128, D])
    din("bet_r", [128, D])

    t8syn = dint("t8syn", [D, N], FP8)
    t8out = dint("t8out", [N, D], FP8)
    x1fm_h = dint("x1fm_h", [NG, D, G], BF16)
    x1tk_h = dint("x1tk_h", [CT, D], BF16)

    out = dout("out", [CT, D])
    tp = {}
    for tname, shp, tdt in [("x1", [CT, D], BF16), ("z", [G, N], F32),
                            ("thr", [128, 64], F32), ("quant", [128, 16], F32),
                            ("carry", [128, ST], F32), ("dbg", [128, 64], F32)]:
        if tname in taps:
            tp[tname] = dout("tap_" + tname, shp, tdt)

    class _Suppress:
        def __enter__(self):
            return self

        def __exit__(self, et, ev, tb):
            return et is _TruncDone

    ctx = ExitStack()
    with ctx:
        ctx.enter_context(_Suppress())

        def sb(name, shape, dt=F32):
            return ctx.enter_context(nc.sbuf_tensor(name, shape, dt))

        seq = Seq(nc, ctx)

        def trunc_point(name):
            if trunc != name:
                return
            def tf(e):
                return e.dma_start(out=out[0:128, 0:32], in_=s_scal[:])
            seq.dma("sync", tf, sm_o, 1)
            sems = [sm_w, sm_g, sm_x, sm_q, sm_8, sm_t, sm_1, sm_o]
            def tw(e):
                for sm in sems:
                    e.wait_ge(sm.s, sm.v)
                return None
            seq.stage("sync", tw, inc=False)
            seq.flush()
            raise _TruncDone()

        s_bout = sb("s_bout", [128, D], BF16)
        s_gam = sb("s_gam", [128, D], BF16)
        s_bet = sb("s_bet", [128, D], BF16)
        s_ones = sb("s_ones", [128, 128], BF16)
        s_eye = sb("s_eye", [128, 128], BF16)
        s_io8 = sb("s_io8", [128, 8])
        s_scal = sb("s_scal", [128, 32])
        s_jk = sb("s_jk", [128, 8])

        def vjk(e):
            return e.tensor_scalar(out=s_jk[:, 0:1], in0=s_jk[:, 2:3],
                                   scalar1=1.0, scalar2=None, op0=ALU.mult)

        def sjk(e):
            return e.activation(out=s_jk[:, 1:2], in_=s_jk[:, 1:2],
                                func=AF.Copy)

        sm_w = Dsem(ctx, nc, "sm_w")
        sm_g = Dsem(ctx, nc, "sm_g")
        sm_x = Dsem(ctx, nc, "sm_x")
        sm_q = Dsem(ctx, nc, "sm_q")
        sm_8 = Dsem(ctx, nc, "sm_8")
        sm_t = Dsem(ctx, nc, "sm_t")
        sm_1 = Dsem(ctx, nc, "sm_1")
        sm_o = Dsem(ctx, nc, "sm_o")

        wscope = ExitStack()
        s_wx = wscope.enter_context(nc.sbuf_tensor("s_wx", [128, DT * 1024], BF16))
        s_wi = wscope.enter_context(nc.sbuf_tensor("s_wi", [128, DT * S], BF16))
        s_wo = wscope.enter_context(nc.sbuf_tensor("s_wo", [128, ST * D], BF16))
        s_wdrep = wscope.enter_context(
            nc.sbuf_tensor("s_wdrep", [128, DT * 128], BF16))
        s_cw = wscope.enter_context(nc.sbuf_tensor("s_cw", [128, DT * 4], F32))
        s_cb = wscope.enter_context(nc.sbuf_tensor("s_cb", [128, DT], F32))
        s_A = wscope.enter_context(nc.sbuf_tensor("s_A", [128, ST], F32))
        s_D = wscope.enter_context(nc.sbuf_tensor("s_D", [128, DT], F32))
        s_carry = wscope.enter_context(nc.sbuf_tensor("s_carry", [128, ST], F32))
        s_on32 = wscope.enter_context(nc.sbuf_tensor("s_on32", [128, 128], F32))

        def st0_loads(e):
            r = []
            for t, dst in [("cwv", s_cw), ("cbv", s_cb), ("dvec", s_D)]:
                r.append(e.dma_start(out=dst[:], in_=dr[t][:]))
            r.append(e.dma_start(out=s_scal[:, 8:8 + ST], in_=dr["alog"][:]))
            r.append(e.dma_start(
                out=s_scal[:, 16:16 + DT].rearrange("p (t o) -> p t o", o=1),
                in_=wxT[:, 1024:1025].rearrange("(t p) o -> p t o", p=128),
                allow_slow_non_contiguous=True))
            return r
        seq.dma("sync", st0_loads, sm_w, 5)

        def st0_casts(e):
            return [e.dma_start(out=dst[:], in_=dr[t][:])
                    for t, dst in [("bout_r", s_bout), ("gam_r", s_gam),
                                   ("bet_r", s_bet)]]
        seq.dma("gpsimd", st0_casts, sm_g, 3)
        w_small = sm_w.v

        def st0_iota(e):
            e.iota(s_eye[:], pattern=[[-1, 128]], channel_multiplier=1,
                   allow_small_or_imprecise_dtypes=True)
            return e.iota(s_io8[:], pattern=[[1, 8]], channel_multiplier=0,
                          allow_small_or_imprecise_dtypes=True)
        seq.stage("gpsimd", st0_iota)

        def st0_v(e):
            e.memset(s_ones[:], 1.0)
            e.memset(s_on32[:], 1.0)
            e.memset(s_jk[:], 0.0)
            e.memset(s_carry[:], 0.0)
            e.tensor_scalar(out=s_eye[:], in0=s_eye[:], scalar1=0.0,
                            scalar2=None, op0=ALU.is_equal)
            ins = None
            for dt_ in range(DT):
                ins = e.tensor_scalar(
                    out=s_wdrep[:, dt_ * 128:(dt_ + 1) * 128], in0=s_ones[:],
                    scalar1=s_scal[:, 16 + dt_:17 + dt_], scalar2=None,
                    op0=ALU.mult)
            return ins
        seq.stage("vector", st0_v, waits=((sm_w.s, w_small), (sm_g.s, sm_g.v)))

        def st0_s(e):
            return e.activation(out=s_A[:], in_=s_scal[:, 8:8 + ST],
                                func=AF.Exp)
        seq.stage("scalar", st0_s)

        def st0_v2(e):
            return e.tensor_scalar(out=s_A[:], in0=s_A[:], scalar1=-1.0,
                                   scalar2=None, op0=ALU.mult)
        seq.stage("vector", st0_v2)
        seq.flush()
        trunc_point("t0")

        NSL = DT + 16

        def slab_in(i):
            if i < DT:
                return wsynT[i * 128:(i + 1) * 128, :]
            j = i - DT
            return woutT[j * 512:(j + 1) * 512, :].rearrange(
                "(a p) c -> p a c", p=128)

        def slab_out8(i):
            if i < DT:
                return t8syn[i * 128:(i + 1) * 128, :]
            j = i - DT
            return t8out[j * 512:(j + 1) * 512, :].rearrange(
                "(a p) c -> p a c", p=128)

        with ExitStack() as qctx:
            qsb = lambda nm, shp, dt=F32: qctx.enter_context(
                nc.sbuf_tensor(nm, shp, dt))
            s_qa = qsb("s_qa", [128, N])
            s_qb = qsb("s_qb", [128, N])
            s_qi = qsb("s_qi", [128, N], I32)
            s_qo = s_qi[:].bitcast(F32)
            s_q8a = qsb("s_q8a", [128, N], FP8)
            s_q8b = qsb("s_q8b", [128, N], FP8)
            s_abs = qsb("s_abs", [128, 32])
            p_red = qctx.enter_context(nc.psum_tensor("p_red", [128, 32], F32))

            consumed = {}
            for i in range(NSL):
                buf = s_qa if i % 2 == 0 else s_qb
                def ld(e, i=i, buf=buf):
                    dst = buf[:] if i < DT else buf[:].rearrange(
                        "p (a c) -> p a c", a=4)
                    return e.dma_start(out=dst, in_=slab_in(i))
                seq.dma("sync", ld, sm_q, 1, dep=consumed.get(i - 2, 0))
                qv = sm_q.v
                def ab(e, i=i, buf=buf):
                    return e.activation(out=s_qo, in_=buf[:], func=AF.Abs,
                                        accum_out=s_abs[:, i:i + 1])
                consumed[i] = seq.stage("scalar", ab, waits=((sm_q.s, qv),))

            def red_mm(e):
                return e.matmul(p_red[:], s_on32[:], s_abs[:],
                                start=True, stop=True)
            seq.stage("tensor", red_mm)

            def red_v(e):
                e.tensor_reduce(out=s_scal[:, 0:1], in_=p_red[:, 0:DT],
                                axis=AX.X, op=ALU.add)
                e.tensor_reduce(out=s_scal[:, 1:2], in_=p_red[:, DT:NSL],
                                axis=AX.X, op=ALU.add)
                for c in range(2):
                    e.tensor_scalar(out=s_scal[:, c:c + 1],
                                    in0=s_scal[:, c:c + 1],
                                    scalar1=1.0 / (D * N), scalar2=None,
                                    op0=ALU.mult)
                for c in range(2):
                    e.tensor_scalar(out=s_scal[:, 5 + c:6 + c],
                                    in0=s_scal[:, c:c + 1],
                                    scalar1=1e-8, scalar2=None, op0=ALU.add)
                for c in range(2):
                    e.tensor_scalar(out=s_scal[:, 12 + c:13 + c],
                                    in0=s_scal[:, 5 + c:6 + c],
                                    scalar1=0.5, scalar2=None, op0=ALU.mult)
                for c in range(2):
                    e.tensor_scalar(out=s_scal[:, 14 + c:15 + c],
                                    in0=s_scal[:, 5 + c:6 + c],
                                    scalar1=-0.5, scalar2=None, op0=ALU.mult)
                return e.reciprocal(out=s_scal[:, 2:4], in_=s_scal[:, 5:7])
            seq.stage("vector", red_v)

            stored = {}
            for i in range(NSL):
                buf = s_qa if i % 2 == 0 else s_qb
                b8 = s_q8a if i % 2 == 0 else s_q8b
                def ld(e, i=i, buf=buf):
                    dst = buf[:] if i < DT else buf[:].rearrange(
                        "p (a c) -> p a c", a=4)
                    return e.dma_start(out=dst, in_=slab_in(i))
                seq.dma("sync", ld, sm_q, 1, dep=stored.get(i - 2, 0))
                qv = sm_q.v
                def qz(e, i=i, buf=buf, b8=b8):
                    co = 0 if i < DT else 1
                    e.tensor_scalar(out=s_qo, in0=buf[:],
                                    scalar1=s_scal[:, 14 + co:15 + co],
                                    scalar2=None, op0=ALU.is_le)
                    vjk(e)
                    return e.scalar_tensor_tensor(
                        out=b8[:], in0=buf[:],
                        scalar=s_scal[:, 12 + co:13 + co],
                        in1=s_qo, op0=ALU.is_ge, op1=ALU.subtract)
                seq.stage("vector", qz, waits=((sm_q.s, qv),))
                def stq(e, i=i, b8=b8):
                    srcb = b8[:] if i < DT else b8[:].rearrange(
                        "p (a c) -> p a c", a=4)
                    return e.dma_start(out=slab_out8(i), in_=srcb)
                seq.dma("gpsimd", stq, sm_8, 1)
                stored[i] = seq.n
            t8_done = sm_8.v

            def ldwx(e):
                return [
                    e.dma_start(out=s_qa[:, 0:N].rearrange(
                        "p (t c) -> p t c", t=8),
                        in_=wxT[0:1024, 0:1024].rearrange(
                            "(t p) c -> p t c", p=128)),
                    e.dma_start(out=s_qb[:, 0:N].rearrange(
                        "p (t c) -> p t c", t=8),
                        in_=wxT[1024:2048, 0:1024].rearrange(
                            "(t p) c -> p t c", p=128)),
                ]
            seq.dma("sync", ldwx, sm_q, 2)
            qv = sm_q.v

            def cvwx(e):
                e.tensor_copy(out=s_wx[:, 0:8 * 1024], in_=s_qa[:, 0:N])
                return e.tensor_copy(out=s_wx[:, 8 * 1024:DT * 1024],
                                     in_=s_qb[:, 0:N])
            seq.stage("vector", cvwx, waits=((sm_q.s, qv),))

            def ldwio(e):
                return [
                    e.dma_start(out=s_qa[:, 0:DT * S].rearrange(
                        "p (t c) -> p t c", t=DT),
                        in_=wiT.rearrange("(t p) c -> p t c", p=128)),
                    e.dma_start(out=s_qb[:, 0:ST * D].rearrange(
                        "p (t c) -> p t c", t=ST),
                        in_=woT.rearrange("(t p) c -> p t c", p=128)),
                ]
            seq.dma("sync", ldwio, sm_q, 2)
            qv = sm_q.v

            def cvwio(e):
                e.tensor_copy(out=s_wi[:], in_=s_qa[:, 0:DT * S])
                return e.tensor_copy(out=s_wo[:], in_=s_qb[:, 0:ST * D])
            quant_tail = seq.stage("vector", cvwio, waits=((sm_q.s, qv),))

            if "quant" in tp:
                def tq(e):
                    return e.dma_start(out=tp["quant"][:, 0:4], in_=s_scal[:, 0:4])
                seq.dma("sync", tq, sm_o, 1)
            if trunc == "quant":
                def tdbg(e):
                    return e.dma_start(out=out[128:256, 0:32], in_=s_abs[:])
                seq.dma("sync", tdbg, sm_o, 1)
            seq.flush()
        trunc_point("quant")

        with ExitStack() as mctx:
            msb = lambda nm, shp, dt=F32: mctx.enter_context(
                nc.sbuf_tensor(nm, shp, dt))
            s_xbs = [msb(f"s_xb{i}", [128, DT * 520], BF16) for i in range(2)]
            s_xc = msb("s_xc", [128, DT * 512], BF16)
            s_B = msb("s_B", [128, ST * 512], BF16)
            s_C = msb("s_C", [128, ST * 512], BF16)
            s_dc = msb("s_dc", [128, 512])
            s_dA = msb("s_dA", [128, ST * 512], BF16)
            s_Bu = msb("s_Bu", [128, ST * 512], BF16)
            s_h = msb("s_h", [128, ST * 512])
            s_hm = msb("s_hm", [128, ST * 512], BF16)
            s_mu = msb("s_mu", [128, 512])
            s_va = msb("s_va", [128, 512])
            s_rs = msb("s_rs", [128, 512])
            s_tm = msb("s_tm", [128, 512])
            s_x1f = msb("s_x1f", [128, DT * 512], BF16)
            s_sq = s_x1f
            s_x1t = msb("s_x1t", [128, MT * D], BF16)

            PS = [mctx.enter_context(nc.psum_tensor(f"p_s{i}", [128, 512], F32))
                  for i in range(4)]
            p_mu = mctx.enter_context(nc.psum_tensor("p_mu", [128, 512], F32))
            p_vr = mctx.enter_context(nc.psum_tensor("p_vr", [128, 512], F32))
            p_tb = mctx.enter_context(nc.psum_tensor("p_tb", [128, 1024], BF16))
            p_tb2 = mctx.enter_context(
                nc.psum_tensor("p_tb2", [128, 1024], BF16))

            def chunk_src(i):
                if i < NPC:
                    return x_pre[:, i * 512:i * 512 + 516]
                j = i - NPC
                return x_own[:, j * 512:j * 512 + 516]

            x_used = {}
            pv = {}
            for ci in range(NPC + NLC):
                own = ci >= NPC
                oc = ci - NPC
                cur = {}
                s_xb = s_xbs[ci % 2]

                def ldx(e, ci=ci, s_xb=s_xb):
                    return e.dma_start(
                        out=s_xb[:].rearrange("p (t c) -> p t c",
                                              t=DT)[:, :, 4:520],
                        in_=chunk_src(ci).rearrange("(t p) c -> p t c", p=128))
                seq.dma("gpsimd", ldx, sm_x, 1,
                        dep=x_used.get(ci - 2, quant_tail))
                xv = sm_x.v

                def conv(e, s_xb=s_xb):
                    for da in range(0, DT, 2):
                        pair = (da, da + 1)
                        for dt_ in pair:
                            xo = dt_ * 520 + 5
                            e.tensor_scalar(
                                out=s_xc[:, dt_ * 512:(dt_ + 1) * 512],
                                in0=s_xb[:, xo:xo + 512],
                                scalar1=s_cw[:, dt_ * 4:dt_ * 4 + 1],
                                scalar2=s_cb[:, dt_:dt_ + 1],
                                op0=ALU.mult, op1=ALU.add)
                        for k in range(1, 4):
                            for dt_ in pair:
                                xo = dt_ * 520 + 5
                                acc = s_xc[:, dt_ * 512:(dt_ + 1) * 512]
                                ins = e.scalar_tensor_tensor(
                                    out=acc,
                                    in0=s_xb[:, xo + k:xo + k + 512],
                                    scalar=s_cw[:, dt_ * 4 + k:
                                                 dt_ * 4 + k + 1],
                                    in1=acc, op0=ALU.mult, op1=ALU.add)
                    return ins
                cur["conv"] = seq.stage(
                    "vector", conv, waits=((sm_x.s, xv),),
                    after=[pv.get("projXS", quant_tail), pv.get("evq3", 0)])
                if ci == 0:
                    trunc_point("mc")

                def projB(e, s_xb=s_xb):
                    ins = None
                    for j in range(ST):
                        for dt_ in range(DT):
                            ins = e.matmul(
                                PS[j][:],
                                s_wx[:, dt_ * 1024 + j * 128:
                                     dt_ * 1024 + (j + 1) * 128],
                                s_xb[:, dt_ * 520 + 8:dt_ * 520 + 520],
                                start=(dt_ == 0), stop=(dt_ == DT - 1))
                    return ins
                cur["projB"] = seq.stage(
                    "tensor", projB, waits=((sm_x.s, xv),),
                    after=[pv.get("evXS", quant_tail), pv.get("evC", 0),
                           pv.get("evq3", 0)])
                if ci == 0:
                    trunc_point("mpb")
                def projD(e, s_xb=s_xb):
                    ins = None
                    for dt_ in range(DT):
                        ins = e.matmul(
                            p_mu[:], s_wdrep[:, dt_ * 128:(dt_ + 1) * 128],
                            s_xb[:, dt_ * 520 + 8:dt_ * 520 + 520],
                            start=(dt_ == 0), stop=(dt_ == DT - 1))
                    return ins
                cur["projD"] = seq.stage(
                    "tensor", projD, waits=((sm_x.s, xv),),
                    after=[pv.get("sp1", quant_tail), pv.get("ln1a", 0)])

                def sp1(e):
                    e.activation(out=s_tm[:], in_=p_mu[:], func=AF.Abs)
                    e.activation(out=s_va[:], in_=p_mu[:], func=AF.Relu)
                    e.activation(out=s_tm[:], in_=s_tm[:], func=AF.Exp,
                                 scale=-1.0)
                    sjk(e)
                    return e.activation(out=s_tm[:], in_=s_tm[:], func=AF.Ln,
                                        bias=1.0)
                cur["sp1"] = seq.stage(
                    "scalar", sp1,
                    after=[cur["projD"], pv.get("ln1a", 0),
                           pv.get("sp2", 0)])

                def sp2(e):
                    return e.tensor_tensor(out=s_dc[:], in0=s_va[:],
                                           in1=s_tm[:], op=ALU.add)
                cur["sp2"] = seq.stage("vector", sp2, after=[cur["sp1"]])

                def dax(e):
                    ins = None
                    for j in range(ST):
                        ins = e.activation(
                            out=s_dA[:, j * 512:(j + 1) * 512], in_=s_dc[:],
                            func=AF.Exp, scale=s_A[:, j:j + 1])
                    return ins
                cur["dax"] = seq.stage("scalar", dax, after=[cur["sp2"]])

                def evB(e):
                    ins = None
                    for j in range(ST):
                        ins = e.tensor_copy(out=s_B[:, j * 512:(j + 1) * 512],
                                            in_=PS[j][:])
                    return ins
                cur["evB"] = seq.stage(
                    "vector", evB,
                    after=[cur["projB"], pv.get("buscan", 0)])
                if ci == 0:
                    trunc_point("mb")

                def projXS(e):
                    ins = None
                    for j in range(ST):
                        for dt_ in range(DT):
                            ins = e.matmul(
                                PS[j][:],
                                s_wi[:, dt_ * 512 + j * 128:
                                     dt_ * 512 + (j + 1) * 128],
                                s_xc[:, dt_ * 512:(dt_ + 1) * 512],
                                start=(dt_ == 0), stop=(dt_ == DT - 1))
                    return ins
                cur["projXS"] = seq.stage(
                    "tensor", projXS, after=[cur["conv"], cur["evB"]])

                def evXS(e):
                    ins = None
                    for j in range(ST):
                        ins = e.tensor_tensor(
                            out=s_Bu[:, j * 512:(j + 1) * 512], in0=PS[j][:],
                            in1=s_B[:, j * 512:(j + 1) * 512], op=ALU.mult)
                    return ins
                cur["evXS"] = seq.stage(
                    "vector", evXS,
                    after=[cur["projXS"], pv.get("buscan", 0)])

                if own:
                    def projC(e, s_xb=s_xb):
                        ins = None
                        for j in range(ST):
                            for dt_ in range(DT):
                                ins = e.matmul(
                                    PS[j][:],
                                    s_wx[:, dt_ * 1024 + 512 + j * 128:
                                         dt_ * 1024 + 512 + (j + 1) * 128],
                                    s_xb[:, dt_ * 520 + 8:dt_ * 520 + 520],
                                    start=(dt_ == 0), stop=(dt_ == DT - 1))
                        return ins
                    cur["projC"] = seq.stage("tensor", projC,
                                             after=[cur["evXS"]])

                    def evC(e):
                        ins = None
                        for j in range(ST):
                            ins = e.tensor_copy(
                                out=s_C[:, j * 512:(j + 1) * 512],
                                in_=PS[j][:])
                        return ins
                    cur["evC"] = seq.stage(
                        "vector", evC,
                        after=[cur["projC"], pv.get("buscan", 0)])


                def buscan(e, own=own):
                    for j in range(ST):
                        o = j * 512
                        e.tensor_tensor(out=s_Bu[:, o:o + 512],
                                        in0=s_Bu[:, o:o + 512], in1=s_dc[:],
                                        op=ALU.mult)
                    for j in range(ST):
                        o = j * 512
                        e.tensor_tensor_scan(
                            out=s_h[:, o:o + 512], data0=s_dA[:, o:o + 512],
                            data1=s_Bu[:, o:o + 512],
                            initial=s_carry[:, j:j + 1],
                            op0=ALU.mult, op1=ALU.add)
                    for j in range(ST):
                        o = j * 512
                        ins = e.tensor_copy(out=s_carry[:, j:j + 1],
                                            in_=s_h[:, o + 511:o + 512])
                    if own:
                        for j in range(ST):
                            o = j * 512
                            ins = e.tensor_tensor(out=s_hm[:, o:o + 512],
                                                  in0=s_h[:, o:o + 512],
                                                  in1=s_C[:, o:o + 512],
                                                  op=ALU.mult)
                    return ins
                bus_after = [cur["dax"], cur["evB"], cur["evXS"]]
                if own:
                    bus_after.append(cur["evC"])
                cur["buscan"] = seq.stage("vector", buscan,
                                          after=bus_after)
                if ci == 0:
                    trunc_point("ms")
                if ci == 2:
                    trunc_point("ms3")

                if not own:
                    x_used[ci] = cur["buscan"]
                    pv = cur
                    continue

                prev_evq = cur["evC"]
                for quad in range(4):
                    def opq(e, quad=quad):
                        ins = None
                        for dt_ in range(quad * 4, quad * 4 + 4):
                            pb = PS[dt_ % 4]
                            for j in range(ST):
                                ins = e.matmul(
                                    pb[:],
                                    s_wo[:, j * D + dt_ * 128:
                                         j * D + (dt_ + 1) * 128],
                                    s_hm[:, j * 512:(j + 1) * 512],
                                    start=(j == 0), stop=(j == ST - 1))
                        return ins
                    cur[f"opq{quad}"] = seq.stage(
                        "tensor", opq,
                        after=[cur["buscan"], prev_evq])

                    def evq(e, quad=quad, s_xb=s_xb):
                        for dt_ in range(quad * 4, quad * 4 + 4):
                            pb = PS[dt_ % 4]
                            xo = dt_ * 520 + 8
                            e.scalar_tensor_tensor(
                                out=s_xb[:, xo:xo + 512], in0=pb[:], scalar=1.0,
                                in1=s_xb[:, xo:xo + 512],
                                op0=ALU.mult, op1=ALU.add)
                        ins = None
                        for dt_ in range(quad * 4, quad * 4 + 4):
                            xo = dt_ * 520 + 8
                            ins = e.scalar_tensor_tensor(
                                out=s_xb[:, xo:xo + 512],
                                in0=s_xc[:, dt_ * 512:(dt_ + 1) * 512],
                                scalar=s_D[:, dt_:dt_ + 1],
                                in1=s_xb[:, xo:xo + 512],
                                op0=ALU.mult, op1=ALU.add)
                        return ins
                    prev_evq = seq.stage("vector", evq,
                                         after=[cur[f"opq{quad}"]])
                    cur[f"evq{quad}"] = prev_evq

                def sqs(e, s_xb=s_xb):
                    ins = None
                    for dt_ in range(DT):
                        xo = dt_ * 520 + 8
                        ins = e.activation(
                            out=s_sq[:, dt_ * 512:(dt_ + 1) * 512],
                            in_=s_xb[:, xo:xo + 512], func=AF.Square)
                    return ins
                cur["sqs"] = seq.stage(
                    "scalar", sqs,
                    after=[cur["evq3"], pv.get("evt7", 0)],
                    waits=((sm_1.s, sm_1.v),))

                def stat_mm(e, s_xb=s_xb):
                    ins = None
                    for dt_ in range(DT):
                        xo = dt_ * 520 + 8
                        e.matmul(p_mu[:], s_ones[:], s_xb[:, xo:xo + 512],
                                 start=(dt_ == 0), stop=(dt_ == DT - 1))
                    for dt_ in range(DT):
                        ins = e.matmul(p_vr[:], s_ones[:],
                                       s_sq[:, dt_ * 512:(dt_ + 1) * 512],
                                       start=(dt_ == 0), stop=(dt_ == DT - 1))
                    return ins
                cur["stat_mm"] = seq.stage("tensor", stat_mm,
                                           after=[cur["sqs"]])

                def ln1a(e):
                    e.tensor_scalar(out=s_mu[:], in0=p_mu[:], scalar1=1.0 / D,
                                    scalar2=None, op0=ALU.mult)
                    e.tensor_scalar(out=s_va[:], in0=p_vr[:], scalar1=1.0 / D,
                                    scalar2=LNEPS, op0=ALU.mult, op1=ALU.add)
                    e.tensor_tensor(out=s_tm[:], in0=s_mu[:], in1=s_mu[:],
                                    op=ALU.mult)
                    vjk(e)
                    return e.tensor_tensor(out=s_va[:], in0=s_va[:],
                                           in1=s_tm[:], op=ALU.subtract)
                cur["ln1a"] = seq.stage("vector", ln1a,
                                        after=[cur["stat_mm"]])

                def ln1b(e):
                    e.activation(out=s_rs[:], in_=s_va[:], func=AF.Ln)
                    sjk(e)
                    return e.activation(out=s_rs[:], in_=s_rs[:], func=AF.Exp,
                                        scale=-0.5)
                cur["ln1b"] = seq.stage("scalar", ln1b, after=[cur["ln1a"]])

                def ln1c(e, s_xb=s_xb):
                    ins = None
                    for da in range(0, DT, 2):
                        for dt_ in (da, da + 1):
                            xo = dt_ * 520 + 8
                            o = dt_ * 512
                            e.tensor_tensor(out=s_x1f[:, o:o + 512],
                                            in0=s_xb[:, xo:xo + 512],
                                            in1=s_mu[:], op=ALU.subtract)
                        for dt_ in (da, da + 1):
                            o = dt_ * 512
                            ins = e.tensor_tensor(out=s_x1f[:, o:o + 512],
                                                  in0=s_x1f[:, o:o + 512],
                                                  in1=s_rs[:], op=ALU.mult)
                    return ins
                cur["ln1c"] = seq.stage("vector", ln1c, after=[cur["ln1b"]])
                x_used[ci] = cur["ln1c"]
                if ci == NPC:
                    trunc_point("ml")

                evt_ids = {-2: pv.get("evt6", 0), -1: pv.get("evt7", 0)}
                for m in range(MT):
                    for half in range(2):
                        k = m * 2 + half
                        ptb = p_tb if k % 2 == 0 else p_tb2

                        def trx(e, m=m, half=half, ptb=ptb):
                            ins = None
                            for bi in range(8):
                                dt_ = half * 8 + bi
                                ins = e.transpose(
                                    ptb[:, bi * 128:(bi + 1) * 128],
                                    s_x1f[:, dt_ * 512 + m * 128:
                                          dt_ * 512 + (m + 1) * 128],
                                    s_eye[:])
                            return ins
                        tid = seq.stage(
                            "tensor", trx,
                            after=[cur["ln1c"], evt_ids.get(k - 2, 0)])

                        def evt(e, m=m, half=half, ptb=ptb):
                            return e.tensor_copy(
                                out=s_x1t[:, m * D + half * 1024:
                                          m * D + (half + 1) * 1024],
                                in_=ptb[:])
                        w_ = ((sm_1.s, sm_1.v),) if k == 0 else ()
                        evt_ids[k] = seq.stage("vector", evt, after=[tid],
                                               waits=w_)
                cur["evt6"] = evt_ids[6]
                cur["evt7"] = evt_ids[7]

                def stx1(e, oc=oc):
                    r = [
                        e.dma_start(
                            out=x1fm_h[oc, :, :].rearrange(
                                "(t p) c -> p t c", p=128),
                            in_=s_x1f[:].rearrange("p (t c) -> p t c", t=DT)),
                        e.dma_start(
                            out=x1tk_h[oc * 512:(oc + 1) * 512, :].rearrange(
                                "(m p) c -> p m c", p=128),
                            in_=s_x1t[:].rearrange("p (m c) -> p m c", m=MT)),
                    ]
                    if "x1" in tp:
                        r.append(e.dma_start(
                            out=tp["x1"][oc * 512:(oc + 1) * 512, :].rearrange(
                                "(m p) c -> p m c", p=128),
                            in_=s_x1t[:].rearrange("p (m c) -> p m c", m=MT)))
                    return r
                seq.dma("sync", stx1, sm_1, 3 if "x1" in tp else 2)
                pv = cur

            if "carry" in tp:
                def tcar(e):
                    return e.dma_start(out=tp["carry"][:], in_=s_carry[:])
                seq.dma("sync", tcar, sm_o, 1)
            x1_all = sm_1.v
            seq.flush()
        wscope.close()
        trunc_point("mamba")

        with ExitStack() as fctx:
            fsb = lambda nm, shp, dt=F32: fctx.enter_context(
                nc.sbuf_tensor(nm, shp, dt))
            s_za = [fsb(f"s_z{m}", [128, N]) for m in range(MT)]
            s_zoneA = fsb("s_zoneA", [128, N], BF16)
            s_zoneB = fsb("s_zoneB", [128, 2 * N], BF16)
            s_x1k = fsb("s_x1k", [128, MT * D], BF16)
            s_bs = fsb("s_bs", [128, 2 * 512], BF16)
            s_zp = fsb("s_zp", [128, MT * 16])
            s_pk = fsb("s_pk", [128, MT * 16])
            s_w8 = fsb("s_w8", [128, 8])
            s_sg = fsb("s_sg", [128, 8])

            def r2m(m):
                return s_za[m][:].bitcast(BF16)[:, N:N + D]

            def ffn_init(e):
                e.memset(s_pk[:], 0.0)
                return e.memset(s_zp[:], 0.0)
            g_prev_psum = seq.stage("vector", ffn_init)
            g_prev_za = 0
            g_prev_mm2 = 0
            g_prev_lnap = 0
            g_prev_sov = {}
            s_x1g = s_zoneA
            s_hidt = s_zoneA
            s_to1 = s_zoneA
            s_ts = (s_zoneB[:, 0:N], s_zoneB[:, N:2 * N])
            s_wv = s_zoneB[:].bitcast(F32)
            s_to0 = s_zoneB[:, 0:N]
            s_r3 = s_zoneB[:].bitcast(F32)[:, 0:D]
            s_o2 = s_zoneB[:].bitcast(F32)[:, D:2 * D]


            for g in range(NG):
                with ExitStack() as gctx:
                    zscope = ExitStack()
                    p_z = [zscope.enter_context(nc.psum_tensor(
                        f"p_z_{g}_{i}", [128, 512], F32)) for i in range(8)]

                    def ldx1g(e, g=g):
                        return e.dma_start(
                            out=s_x1g[:].rearrange("p (t c) -> p t c", t=DT),
                            in_=x1fm_h[g, :, :].rearrange(
                                "(t p) c -> p t c", p=128))
                    seq.dma("sync", ldx1g, sm_1, 1,
                            dep=g_prev_mm2, waits=((sm_1.s, x1_all),))
                    x1v = sm_1.v

                    def ldx1k(e, g=g):
                        return e.dma_start(
                            out=s_x1k[:].rearrange("p (m c) -> p m c", m=MT),
                            in_=x1tk_h[g * 512:(g + 1) * 512, :].rearrange(
                                "(m p) c -> p m c", p=128))
                    seq.dma("sync", ldx1k, sm_1, 1,
                            dep=g_prev_lnap, waits=((sm_1.s, x1_all),))
                    x1kv = sm_1.v

                    ts_used = {}
                    ev1_ids = {}
                    for nch in range(NCH):
                        buf = s_ts[nch % 2]
                        def ldt(e, nch=nch, buf=buf):
                            return [e.dma_start(
                                out=buf.rearrange("p (t c) -> p t c", t=DT),
                                in_=t8syn[:, nch * 512:(nch + 1) * 512]
                                .rearrange("(t p) c -> p t c", p=128)),
                                e.dma_start(
                                    out=s_bs[:, (nch % 2) * 512:
                                             (nch % 2) * 512 + 512],
                                    in_=dr["bsyn_r"][0:128,
                                                     nch * 512:(nch + 1) * 512])]
                        ldt_sov = g_prev_sov.get(2 + nch % 2, sm_o.v) \
                            if nch <= 1 else sm_o.v
                        seq.dma("gpsimd", ldt, sm_t, 2,
                                dep=ts_used.get(nch - 2, None),
                                waits=((sm_8.s, t8_done),
                                       (sm_o.s, ldt_sov),
                                       (sm_1.s, x1_all)))
                        tv = sm_t.v

                        par = 4 * (nch % 2)

                        def mm1(e, nch=nch, buf=buf, par=par):
                            ins = None
                            for m in range(MT):
                                for dt_ in range(DT):
                                    ins = e.matmul(
                                        p_z[par + m][:],
                                        s_x1g[:, dt_ * 512 + m * 128:
                                              dt_ * 512 + (m + 1) * 128],
                                        buf[:, dt_ * 512:(dt_ + 1) * 512],
                                        start=(dt_ == 0),
                                        stop=(dt_ == DT - 1))
                            return ins
                        w = [(sm_t.s, tv)]
                        if nch == 0:
                            w.append((sm_1.s, x1v))
                        mm1_after = [ev1_ids.get(nch - 2, 0)]
                        if nch <= 1:
                            mm1_after.append(g_prev_psum)
                        ts_used[nch] = seq.stage("tensor", mm1, waits=w,
                                                 after=mm1_after)

                        def ev1(e, nch=nch, par=par):
                            ins = None
                            for m in range(MT):
                                ins = e.scalar_tensor_tensor(
                                    out=s_za[m][:, nch * 512:(nch + 1) * 512],
                                    in0=p_z[par + m][:], scalar=s_scal[:, 0:1],
                                    in1=s_bs[:, (nch % 2) * 512:
                                             (nch % 2) * 512 + 512],
                                    op0=ALU.mult, op1=ALU.add,
                                    accum_out=s_zp[:, m * 16 + nch:
                                                   m * 16 + nch + 1])
                            return ins
                        ev1_after = [ts_used[nch]]
                        if nch == 0:
                            ev1_after.append(g_prev_za)
                        ev1_ids[nch] = seq.stage("vector", ev1,
                                                 after=ev1_after)

                    if "z" in tp and g == 2:
                        def tz(e):
                            return [e.dma_start(
                                out=tp["z"][m * 128:(m + 1) * 128, :],
                                in_=s_za[m][:]) for m in range(MT)]
                        seq.dma("sync", tz, sm_o, MT)

                    zscope.close()
                    htscope = ExitStack()
                    p_ht = htscope.enter_context(
                        nc.psum_tensor(f"p_ht_{g}", [128, 1024], BF16))
                    p_ht2 = htscope.enter_context(
                        nc.psum_tensor(f"p_ht2_{g}", [128, 1024], BF16))

                    def col(m, j):
                        return s_pk[:, m * 16 + j:m * 16 + j + 1]

                    def tk0a(e):
                        ins = None
                        for m in range(MT):
                            ins = e.activation(out=s_zoneA[:], in_=s_za[m][:],
                                               func=AF.Square,
                                               accum_out=col(m, 13))
                        return ins
                    id_tk0a = seq.stage("scalar", tk0a,
                                        after=[ev1_ids[NCH - 1]])

                    def tk0b(e):
                        for m in range(MT):
                            e.tensor_reduce(out=col(m, 0),
                                            in_=s_zp[:, m * 16:m * 16 + 16],
                                            axis=AX.X, op=ALU.add)
                        for m in range(MT):
                            e.tensor_scalar(out=col(m, 0), in0=col(m, 0),
                                            scalar1=1.0 / N, scalar2=None,
                                            op0=ALU.mult)
                        for m in range(MT):
                            e.tensor_tensor(out=col(m, 11), in0=col(m, 0),
                                            in1=col(m, 0), op=ALU.mult)
                        ins = None
                        for m in range(MT):
                            ins = e.scalar_tensor_tensor(
                                out=col(m, 12), in0=col(m, 13),
                                scalar=1.0 / N, in1=col(m, 11),
                                op0=ALU.mult, op1=ALU.subtract)
                        return ins
                    id_tk0b = seq.stage("vector", tk0b, after=[id_tk0a])

                    def tk0c(e):
                        for m in range(MT):
                            e.activation(out=col(m, 12), in_=col(m, 12),
                                         func=AF.Ln)
                        ins = None
                        for m in range(MT):
                            ins = e.activation(out=col(m, 12), in_=col(m, 12),
                                               func=AF.Exp, scale=0.5)
                        return ins
                    id_tk0c = seq.stage("scalar", tk0c, after=[id_tk0b])

                    def tk0d(e):
                        ins = None
                        for m in range(MT):
                            for c_, coef in ((8, SIG_SPAN), (1, BR_LO),
                                             (2, BR_HI), (5, LAD0)):
                                e.scalar_tensor_tensor(
                                    out=col(m, c_), in0=col(m, 12),
                                    scalar=coef, in1=col(m, 0),
                                    op0=ALU.mult, op1=ALU.add)
                            e.memset(col(m, 3), 8192.0)
                            e.memset(col(m, 4), 0.0)
                            e.memset(col(m, 9), 0.0)
                            ins = e.memset(col(m, 10), -1.0)
                        return ins
                    id_upd = seq.stage("vector", tk0d, after=[id_tk0c])

                    for pi in range(NPROBE):
                        def cnt_v(e):
                            ins = None
                            for m in range(2):
                                ins = e.tensor_scalar(
                                    out=s_zoneA[:], in0=s_za[m][:],
                                    scalar1=col(m, 5),
                                    scalar2=None, op0=ALU.is_ge, op1=ALU.add,
                                    accum_out=col(m, 6))
                            return ins
                        id_cv = seq.stage("vector", cnt_v, after=[id_upd])

                        def cnt_s(e):
                            ins = None
                            for m in range(2, MT):
                                ins = e.activation(
                                    out=s_zoneA[:], in_=s_za[m][:],
                                    func=AF.Sign, scale=-1.0,
                                    bias=col(m, 5),
                                    accum_out=col(m, 6))
                            return ins
                        id_cs = seq.stage("scalar", cnt_s, after=[id_upd])

                        def upd(e, pi=pi, last=(pi == NPROBE - 1)):
                            for m in range(2, MT):
                                e.tensor_scalar(out=col(m, 6), in0=col(m, 6),
                                                scalar1=-0.5,
                                                scalar2=float(N // 2),
                                                op0=ALU.mult, op1=ALU.add)
                            for m in range(MT):
                                e.tensor_scalar(out=col(m, 7), in0=col(m, 6),
                                                scalar1=float(TOPK) - 0.5,
                                                scalar2=None, op0=ALU.is_ge)
                            for dst, src in ((1, 5), (3, 6)):
                                for m in range(MT):
                                    e.tensor_tensor(out=col(m, 11),
                                                    in0=col(m, src),
                                                    in1=col(m, dst),
                                                    op=ALU.subtract)
                                for m in range(MT):
                                    e.tensor_tensor(out=col(m, 11),
                                                    in0=col(m, 11),
                                                    in1=col(m, 7), op=ALU.mult)
                                for m in range(MT):
                                    e.tensor_tensor(out=col(m, dst),
                                                    in0=col(m, dst),
                                                    in1=col(m, 11), op=ALU.add)
                            for m in range(MT):
                                e.tensor_scalar(out=col(m, 7), in0=col(m, 7),
                                                scalar1=-1.0, scalar2=1.0,
                                                op0=ALU.mult, op1=ALU.add)
                            for dst, src in ((2, 5), (4, 6)):
                                for m in range(MT):
                                    e.tensor_tensor(out=col(m, 11),
                                                    in0=col(m, src),
                                                    in1=col(m, dst),
                                                    op=ALU.subtract)
                                for m in range(MT):
                                    e.tensor_tensor(out=col(m, 11),
                                                    in0=col(m, 11),
                                                    in1=col(m, 7), op=ALU.mult)
                                for m in range(MT):
                                    e.tensor_tensor(out=col(m, dst),
                                                    in0=col(m, dst),
                                                    in1=col(m, 11), op=ALU.add)
                            for m in range(MT):
                                e.tensor_scalar(out=col(m, 7), in0=col(m, 6),
                                                scalar1=float(TOPK) - 0.5,
                                                scalar2=None, op0=ALU.is_ge)
                            for m in range(MT):
                                e.tensor_scalar(out=col(m, 11), in0=col(m, 6),
                                                scalar1=float(TOPK + 7),
                                                scalar2=None, op0=ALU.is_le)
                            for m in range(MT):
                                e.tensor_tensor(out=col(m, 7), in0=col(m, 7),
                                                in1=col(m, 11), op=ALU.mult)
                            for dst, src in ((9, 5), (10, 6)):
                                for m in range(MT):
                                    e.tensor_tensor(out=col(m, 11),
                                                    in0=col(m, src),
                                                    in1=col(m, dst),
                                                    op=ALU.subtract)
                                for m in range(MT):
                                    e.tensor_tensor(out=col(m, 11),
                                                    in0=col(m, 11),
                                                    in1=col(m, 7), op=ALU.mult)
                                for m in range(MT):
                                    e.tensor_tensor(out=col(m, dst),
                                                    in0=col(m, dst),
                                                    in1=col(m, 11), op=ALU.add)
                            if pi == 0:
                                for m in range(MT):
                                    e.scalar_tensor_tensor(
                                        out=col(m, 5), in0=col(m, 12),
                                        scalar=LAD1, in1=col(m, 0),
                                        op0=ALU.mult, op1=ALU.add)
                            elif pi < NPROBE - 1 - N_FP:
                                for m in range(MT):
                                    e.tensor_tensor(out=col(m, 5),
                                                    in0=col(m, 1),
                                                    in1=col(m, 2), op=ALU.add)
                                for m in range(MT):
                                    e.tensor_scalar(out=col(m, 5),
                                                    in0=col(m, 5), scalar1=0.5,
                                                    scalar2=None, op0=ALU.mult)
                            elif not last:
                                for m in range(MT):
                                    e.tensor_tensor(out=col(m, 7),
                                                    in0=col(m, 3),
                                                    in1=col(m, 4),
                                                    op=ALU.subtract)
                                for m in range(MT):
                                    e.tensor_scalar(out=col(m, 7),
                                                    in0=col(m, 7),
                                                    scalar1=1e-6, scalar2=None,
                                                    op0=ALU.add)
                                for m in range(MT):
                                    e.reciprocal(out=col(m, 7), in_=col(m, 7))
                                for m in range(MT):
                                    e.tensor_scalar(out=col(m, 11),
                                                    in0=col(m, 3),
                                                    scalar1=-AIM, scalar2=None,
                                                    op0=ALU.add)
                                for m in range(MT):
                                    e.tensor_tensor(out=col(m, 11),
                                                    in0=col(m, 11),
                                                    in1=col(m, 7), op=ALU.mult)
                                for m in range(MT):
                                    e.tensor_tensor(out=col(m, 7),
                                                    in0=col(m, 2),
                                                    in1=col(m, 1),
                                                    op=ALU.subtract)
                                for m in range(MT):
                                    e.tensor_tensor(out=col(m, 11),
                                                    in0=col(m, 11),
                                                    in1=col(m, 7), op=ALU.mult)
                                for m in range(MT):
                                    e.tensor_tensor(out=col(m, 5),
                                                    in0=col(m, 1),
                                                    in1=col(m, 11), op=ALU.add)
                            return vjk(e)
                        id_upd = seq.stage("vector", upd,
                                           after=[id_cs, id_cv])

                    def fin(e):
                        for m in range(MT):
                            e.tensor_scalar(out=col(m, 7), in0=col(m, 10),
                                            scalar1=0.0, scalar2=None,
                                            op0=ALU.is_ge)
                        for dst, src in ((1, 9), (3, 10)):
                            for m in range(MT):
                                e.tensor_tensor(out=col(m, 11),
                                                in0=col(m, src),
                                                in1=col(m, dst),
                                                op=ALU.subtract)
                            for m in range(MT):
                                e.tensor_tensor(out=col(m, 11), in0=col(m, 11),
                                                in1=col(m, 7), op=ALU.mult)
                            for m in range(MT):
                                e.tensor_tensor(out=col(m, dst),
                                                in0=col(m, dst),
                                                in1=col(m, 11), op=ALU.add)
                        for m in range(MT):
                            e.tensor_scalar(out=col(m, 6), in0=col(m, 3),
                                            scalar1=float(-TOPK), scalar2=0.0,
                                            op0=ALU.add, op1=ALU.max)
                        for m in range(MT):
                            e.tensor_scalar(out=col(m, 6), in0=col(m, 6),
                                            scalar1=7.0, scalar2=None,
                                            op0=ALU.min)
                        for m in range(MT):
                            e.tensor_scalar(out=col(m, 14), in0=col(m, 6),
                                            scalar1=-0.25, scalar2=None,
                                            op0=ALU.add)
                        ins = None
                        for m in range(MT):
                            ins = e.tensor_scalar(out=col(m, 15),
                                                  in0=col(m, 6),
                                                  scalar1=0.6, scalar2=None,
                                                  op0=ALU.add)
                        return ins
                    seq.stage("vector", fin)

                    def tcm(e):
                        ins = None
                        for m in range(MT):
                            ins = e.tensor_scalar(
                                out=col(m, 7), in0=col(m, 1), scalar1=-1e-5,
                                scalar2=0.0, op0=ALU.add, op1=ALU.max)
                        return ins
                    seq.stage("vector", tcm)

                    evh_ids = {}
                    for m in range(MT):
                        zs = s_za[m][:]
                        hTm = s_za[m][:].bitcast(BF16)[:, 0:N]
                        pkb = m * 16

                        def hid(e, pkb=pkb, zs=zs):
                            return e.scalar_tensor_tensor(
                                out=s_hidt[:], in0=zs,
                                scalar=s_pk[:, pkb + 7:pkb + 8], in1=zs,
                                op0=ALU.is_ge, op1=ALU.mult)
                        id_hid = seq.stage("vector", hid)

                        for half in range(8):
                            kg = m * 8 + half
                            pht = p_ht if kg % 2 == 0 else p_ht2

                            def trh(e, half=half, pht=pht):
                                ins = None
                                for bi in range(8):
                                    nt = half * 8 + bi
                                    ins = e.transpose(
                                        pht[:, bi * 128:(bi + 1) * 128],
                                        s_hidt[:, nt * 128:(nt + 1) * 128],
                                        s_eye[:])
                                return ins
                            id_trh = seq.stage(
                                "tensor", trh,
                                after=[id_hid, evh_ids.get(kg - 2, 0)])

                            def evh(e, m=m, half=half, hTm=hTm, pht=pht):
                                ins = None
                                for bi in range(8):
                                    nt = half * 8 + bi
                                    ins = e.tensor_copy(
                                        out=hTm[:, nt * 128:(nt + 1) * 128],
                                        in_=pht[:, bi * 128:(bi + 1) * 128])
                                return ins
                            evh_ids[kg] = seq.stage("vector", evh,
                                                    after=[id_trh])

                    if "thr" in tp and g == 2:
                        def tthr(e):
                            return e.dma_start(out=tp["thr"][:],
                                               in_=s_pk[:])
                        seq.dma("sync", tthr, sm_o, 1)

                    htscope.close()
                    p_o = [gctx.enter_context(nc.psum_tensor(
                        f"p_o_{g}_{i}", [128, 512], F32)) for i in range(8)]
                    to_used = {}
                    ev2_ids = {}
                    for dh in range(2):
                        for ntq in range(NT // 4):
                            buf = s_to0 if ntq % 2 == 0 else s_to1

                            def ldo(e, ntq=ntq, dh=dh, buf=buf):
                                return e.dma_start(
                                    out=buf[:, 0:4096].rearrange(
                                        "p (a c) -> p a c", a=4),
                                    in_=t8out[ntq * 512:(ntq + 1) * 512,
                                              dh * 1024:(dh + 1) * 1024]
                                    .rearrange("(a p) c -> p a c", p=128))
                            ldep = to_used.get((dh, ntq - 2), None)
                            if dh == 0 and ntq == 0:
                                ldep = ts_used[NCH - 1]
                            seq.dma("gpsimd", ldo, sm_t, 1, dep=ldep)
                            tv = sm_t.v

                            def mm2(e, dh=dh, ntq=ntq, buf=buf):
                                ins = None
                                for m in range(MT):
                                    hTm = s_za[m][:].bitcast(BF16)
                                    for a in range(4):
                                        nt = ntq * 4 + a
                                        first = (ntq == 0 and a == 0)
                                        last = (ntq == NT // 4 - 1 and a == 3)
                                        for dq in range(2):
                                            ins = e.matmul(
                                                p_o[m * 2 + dq][:],
                                                hTm[:, nt * 128:
                                                    (nt + 1) * 128],
                                                buf[:, a * 1024 + dq * 512:
                                                    a * 1024 + (dq + 1) * 512],
                                                start=first, stop=last)
                                return ins
                            to_used[(dh, ntq)] = seq.stage(
                                "tensor", mm2, waits=((sm_t.s, tv),))

                        def ev2(e, dh=dh):
                            ins = None
                            for m in range(MT):
                                r2 = r2m(m)
                                for dq in range(2):
                                    do_ = dh * 1024 + dq * 512
                                    ins = e.scalar_tensor_tensor(
                                        out=r2[:, do_:do_ + 512],
                                        in0=p_o[m * 2 + dq][:],
                                        scalar=s_scal[:, 1:2],
                                        in1=s_x1k[:, m * D + do_:
                                                  m * D + do_ + 512],
                                        op0=ALU.mult, op1=ALU.add)
                            return ins
                        ev2_ids[dh] = seq.stage(
                            "vector", ev2, waits=((sm_1.s, x1kv),))

                    zBf = s_zoneB[:].bitcast(F32)
                    s_lnd = s_zoneA[:].bitcast(F32)[:, 0:D]

                    def r3q(m):
                        return zBf[:, (m % 2) * 2 * D:(m % 2) * 2 * D + D]

                    def o2q(m):
                        return zBf[:, (m % 2) * 2 * D + D:
                                   (m % 2) * 2 * D + 2 * D]

                    ln_sov = sm_o.v
                    sov = {}
                    lni_ids = {}
                    ln2sq_ids = {}
                    lnap_ids = {}
                    for m in range(MT):
                        def lni(e, m=m):
                            e.tensor_tensor(out=r3q(m), in0=r2m(m),
                                            in1=s_bout[:], op=ALU.add)
                            vjk(e)
                            return e.tensor_scalar(
                                out=s_lnd, in0=r3q(m),
                                scalar1=1.0, scalar2=None, op0=ALU.mult,
                                op1=ALU.add,
                                accum_out=s_pk[:, m * 16:m * 16 + 1])
                        lni_ids[m] = seq.stage(
                            "vector", lni, waits=((sm_o.s, ln_sov),),
                            after=[ln2sq_ids.get(m - 2, 0),
                                   ev2_ids[1]])

                        def lnsq(e, m=m):
                            return e.activation(
                                out=s_lnd, in_=r3q(m), func=AF.Square,
                                accum_out=s_pk[:, m * 16 + 1:m * 16 + 2])
                        id_lnsq = seq.stage("scalar", lnsq,
                                            after=[lni_ids[m]])

                        def lnab(e, m=m):
                            P = s_pk
                            mu = P[:, m * 16:m * 16 + 1]
                            sq = P[:, m * 16 + 1:m * 16 + 2]
                            va = P[:, m * 16 + 2:m * 16 + 3]
                            tq_ = P[:, m * 16 + 8:m * 16 + 9]
                            e.tensor_scalar(out=mu, in0=mu, scalar1=1.0 / D,
                                            scalar2=None, op0=ALU.mult)
                            e.tensor_scalar(out=va, in0=sq, scalar1=1.0 / D,
                                            scalar2=LNEPS, op0=ALU.mult,
                                            op1=ALU.add)
                            e.tensor_tensor(out=tq_, in0=mu, in1=mu,
                                            op=ALU.mult)
                            vjk(e)
                            return e.tensor_tensor(out=va, in0=va, in1=tq_,
                                                   op=ALU.subtract)
                        id_lnab = seq.stage("vector", lnab, after=[id_lnsq])

                        def lnrs(e, m=m):
                            P = s_pk
                            va = P[:, m * 16 + 2:m * 16 + 3]
                            rs = P[:, m * 16 + 3:m * 16 + 4]
                            e.activation(out=rs, in_=va, func=AF.Ln)
                            sjk(e)
                            return e.activation(out=rs, in_=rs, func=AF.Exp,
                                                scale=-0.5)
                        id_lnrs = seq.stage("scalar", lnrs, after=[id_lnab])

                        def lnap(e, m=m):
                            P = s_pk
                            mu = P[:, m * 16:m * 16 + 1]
                            rs = P[:, m * 16 + 3:m * 16 + 4]
                            e.tensor_scalar(out=r3q(m), in0=r3q(m),
                                            scalar1=mu, scalar2=rs,
                                            op0=ALU.subtract, op1=ALU.mult)
                            vjk(e)
                            e.tensor_tensor(out=r3q(m), in0=r3q(m),
                                            in1=s_gam[:], op=ALU.mult)
                            vjk(e)
                            e.tensor_tensor(out=r3q(m), in0=r3q(m),
                                            in1=s_bet[:], op=ALU.add)
                            vjk(e)
                            e.tensor_tensor(out=r3q(m), in0=r3q(m),
                                            in1=s_x1k[:, m * D:(m + 1) * D],
                                            op=ALU.add)
                            vjk(e)
                            return e.tensor_scalar(
                                out=s_lnd, in0=r3q(m), scalar1=1.0,
                                scalar2=None, op0=ALU.mult, op1=ALU.add,
                                accum_out=s_pk[:, m * 16 + 4:m * 16 + 5])
                        id_lnap = seq.stage("vector", lnap, after=[id_lnrs])
                        lnap_ids[m] = id_lnap

                        def ln2sq(e, m=m):
                            return e.activation(
                                out=s_lnd, in_=r3q(m), func=AF.Square,
                                accum_out=s_pk[:, m * 16 + 5:m * 16 + 6])
                        ln2sq_ids[m] = seq.stage("scalar", ln2sq,
                                                 after=[id_lnap])

                        def ln2ab(e, m=m):
                            P = s_pk
                            mu = P[:, m * 16 + 4:m * 16 + 5]
                            sq = P[:, m * 16 + 5:m * 16 + 6]
                            va = P[:, m * 16 + 6:m * 16 + 7]
                            tq_ = P[:, m * 16 + 9:m * 16 + 10]
                            e.tensor_scalar(out=mu, in0=mu, scalar1=1.0 / D,
                                            scalar2=None, op0=ALU.mult)
                            e.tensor_scalar(out=va, in0=sq, scalar1=1.0 / D,
                                            scalar2=LNEPS, op0=ALU.mult,
                                            op1=ALU.add)
                            e.tensor_tensor(out=tq_, in0=mu, in1=mu,
                                            op=ALU.mult)
                            vjk(e)
                            return e.tensor_tensor(out=va, in0=va, in1=tq_,
                                                   op=ALU.subtract)
                        id_ln2ab = seq.stage("vector", ln2ab,
                                             after=[ln2sq_ids[m]])

                        def ln2rs(e, m=m):
                            P = s_pk
                            va = P[:, m * 16 + 6:m * 16 + 7]
                            rs = P[:, m * 16 + 7:m * 16 + 8]
                            e.activation(out=rs, in_=va, func=AF.Ln)
                            sjk(e)
                            return e.activation(out=rs, in_=rs, func=AF.Exp,
                                                scale=-0.5)
                        id_ln2rs = seq.stage("scalar", ln2rs,
                                             after=[id_ln2ab])

                        def ln2ap(e, m=m):
                            P = s_pk
                            mu = P[:, m * 16 + 4:m * 16 + 5]
                            rs = P[:, m * 16 + 7:m * 16 + 8]
                            return e.tensor_scalar(out=o2q(m), in0=r3q(m),
                                                   scalar1=mu, scalar2=rs,
                                                   op0=ALU.subtract,
                                                   op1=ALU.mult)
                        id_ln2ap = seq.stage(
                            "vector", ln2ap, after=[id_ln2rs],
                            waits=((sm_o.s, sov.get(m - 2, ln_sov)),))

                        def sto(e, g=g, m=m):
                            r0 = g * 512 + m * 128
                            return e.dma_start(out=out[r0:r0 + 128, :],
                                               in_=o2q(m))
                        seq.dma("sync", sto, sm_o, 1, dep=id_ln2ap)
                        sov[m] = sm_o.v

                    g_prev_psum = ev2_ids[1]
                    g_prev_za = lni_ids[3]
                    g_prev_mm2 = to_used[(1, NT // 4 - 1)]
                    g_prev_lnap = lnap_ids[3]
                    g_prev_sov = {2: sov[2], 3: sov[3]}
                    seq.flush()
                trunc_point(f"g{g}")

            def final(e):
                e.wait_ge(sm_o.s, sm_o.v)
                return None
            seq.stage("sync", final, inc=False)
            seq.flush()

    return nc, dr, tp



def stage_inputs(inputs):
    x = np.ascontiguousarray(np.asarray(inputs["L0_attn_x"], dtype=np.float32))
    Wx = np.asarray(inputs["L0_attn_Wxproj"], dtype=np.float32)
    Wi = np.asarray(inputs["L0_attn_Winproj"], dtype=np.float32)
    Wo = np.asarray(inputs["L0_attn_Woutproj"], dtype=np.float32)
    cwn = np.asarray(inputs["L0_attn_conv_w"], dtype=np.float32)
    cbn = np.asarray(inputs["L0_attn_conv_b"], dtype=np.float32)
    Alog = np.asarray(inputs["L0_attn_A_log"], dtype=np.float32)
    Dv = np.asarray(inputs["L0_attn_D"], dtype=np.float32)
    Wsyn = np.asarray(inputs["L0_ffn_W_syn"], dtype=np.float32)
    bsyn = np.asarray(inputs["L0_ffn_b_syn"], dtype=np.float32)
    Wout = np.asarray(inputs["L0_ffn_W_out"], dtype=np.float32)
    bout = np.asarray(inputs["L0_ffn_b_out"], dtype=np.float32)
    gam = np.asarray(inputs["L0_ffn_gamma"], dtype=np.float32)
    bet = np.asarray(inputs["L0_ffn_beta"], dtype=np.float32)

    shared = {
        "wxT": np.ascontiguousarray(Wx.T),
        "wiT": np.ascontiguousarray(Wi.T),
        "woT": np.ascontiguousarray(Wo.T),
        "cwv": np.ascontiguousarray(
            cwn[:, 0, :].reshape(DT, 128, 4).transpose(1, 0, 2).reshape(128, DT * 4)),
        "cbv": np.ascontiguousarray(cbn.reshape(DT, 128).T),
        "alog": np.ascontiguousarray(Alog.reshape(ST, 128).T),
        "dvec": np.ascontiguousarray(Dv.reshape(DT, 128).T),
        "wsynT": np.ascontiguousarray(Wsyn.T),
        "woutT": np.ascontiguousarray(Wout.T),
        "bsyn_r": np.ascontiguousarray(np.broadcast_to(bsyn[None, :], (128, N))),
        "bout_r": np.ascontiguousarray(np.broadcast_to(bout[None, :], (128, D))),
        "gam_r": np.ascontiguousarray(np.broadcast_to(gam[None, :], (128, D))),
        "bet_r": np.ascontiguousarray(np.broadcast_to(bet[None, :], (128, D))),
    }

    in_maps = []
    for c in range(8):
        seqi, idx = divmod(c, 4)
        start = idx * CT
        xo = np.zeros((CT + 4, D), np.float32)
        lo = start - 4
        src0 = max(lo, 0)
        xo[src0 - lo:, :] = x[seqi, src0:start + CT, :]
        xp = np.zeros((PRE + 4, D), np.float32)
        lo2 = start - PRE - 4
        src0 = max(lo2, 0)
        if start > 0:
            xp[src0 - lo2:, :] = x[seqi, src0:start, :]
        m = dict(shared)
        m["x_own"] = np.ascontiguousarray(xo.T)
        m["x_pre"] = np.ascontiguousarray(xp.T)
        in_maps.append(m)
    return in_maps


_NC_CACHE = {}


def kernel(**inputs):
    key = ()
    if key not in _NC_CACHE:
        _NC_CACHE[key] = build_nc()
    nc, dr, tp = _NC_CACHE[key]
    in_maps = stage_inputs(inputs)
    res = run_bass_kernel_spmd(nc, in_maps, core_ids=list(range(8)))
    outs = [np.asarray(res.results[c]["out"], dtype=np.float32)
            for c in range(8)]
    return np.concatenate(outs, axis=0).reshape(B_, L_, D)


# version 21
# speedup vs baseline: 1.0396x; 1.0115x over previous
import numpy as np
from contextlib import ExitStack

import concourse.bass as bass
import concourse.mybir as mybir
from concourse.bass_utils import run_bass_kernel_spmd

F32 = mybir.dt.float32
BF16 = mybir.dt.bfloat16
FP8 = mybir.dt.float8e4
I32 = mybir.dt.int32
AF = mybir.ActivationFunctionType
ALU = mybir.AluOpType
AX = mybir.AxisListType

B_, L_, D, S, N = 2, 8192, 2048, 512, 8192
CT, PRE = 2048, 6144
TOPK = 819
LNEPS = 1e-5
NG, G = 4, 512
MT, DT, ST, NT = 4, 16, 4, 64
NPC, NLC, NCH = 12, 4, 16
AIM = float(TOPK) + 3.0
SIG_SPAN = 30.0
BR_LO = 0.8
BR_HI = 2.0
LAD0 = 1.38
LAD1 = 1.15
NPROBE = 7
N_FP = 3


ENGS = ("tensor", "vector", "scalar", "gpsimd", "sync")


class Seq:

    def __init__(self, nc, ctx):
        self.nc = nc
        self.chains = {e: ctx.enter_context(nc.semaphore(f"ch_{e}"))
                       for e in ENGS}
        self.counts = {e: 0 for e in ENGS}
        self.where = {}
        self.n = 0
        self.stages = []

    def stage(self, eng, fn, dep=None, after=None, waits=(), inc=True,
              dma=None):
        if after is None:
            after = [self.n if dep is None else dep]
        elif dep is not None:
            after = list(after) + [dep]
        cw = {}
        for sid in after:
            if not sid:
                continue
            deng, didx = self.where[sid]
            cw[deng] = max(cw.get(deng, 0), didx)
        chw = tuple((self.chains[k], v) for k, v in cw.items())
        if dma is not None:
            dma[0].v += 16 * dma[1]
        self.stages.append((eng, chw, tuple(waits), fn, inc, dma))
        if inc:
            self.counts[eng] += 1
            self.n += 1
            self.where[self.n] = (eng, self.counts[eng])
            return self.n
        return None

    def dma(self, eng, fn, sem, n, dep=None, after=None, waits=()):
        return self.stage(eng, fn, dep=dep, after=after, waits=waits,
                          inc=False, dma=(sem, n))

    def flush(self):
        stages = self.stages
        self.stages = []
        if not stages:
            return
        nc = self.nc
        used = sorted({s[0] for s in stages})
        with nc.Block() as blk:
            for name in used:
                def mk(name):
                    def run(e):
                        for eng, chw, waits, fn, inc, dma in stages:
                            if eng != name:
                                continue
                            for sem, val in chw:
                                e.wait_ge(sem, val)
                            for sem, val in waits:
                                e.wait_ge(sem, val)
                            r = fn(e)
                            if dma is not None:
                                insts = r if isinstance(r, (list, tuple)) else \
                                    ([] if r is None else [r])
                                assert len(insts) == dma[1], \
                                    f"dma stage returned {len(insts)} != {dma[1]}"
                                for ins in insts:
                                    ins.then_inc(dma[0].s, 16)
                            elif inc:
                                assert r is not None and \
                                    not isinstance(r, (list, tuple)), \
                                    "stage must return its last instruction"
                                r.then_inc(self.chains[name], 1)
                    return run
                getattr(blk, name)(mk(name))


class Dsem:
    def __init__(self, ctx, nc, name):
        self.s = ctx.enter_context(nc.semaphore(name))
        self.v = 0


class _TruncDone(Exception):
    pass


def build_nc(debug_taps=(), trunc=None):
    nc = bass.Bass("TRN2", target_bir_lowering=False, debug=False,
                   detect_race_conditions=False)
    taps = set(debug_taps)
    dr = {}

    def din(name, shape, dt=F32):
        dr[name] = nc.dram_tensor(name, shape, dt, kind="ExternalInput").ap()
        return dr[name]

    def dint(name, shape, dt):
        dr[name] = nc.dram_tensor(name, shape, dt, kind="Internal").ap()
        return dr[name]

    def dout(name, shape, dt=F32):
        dr[name] = nc.dram_tensor(name, shape, dt, kind="ExternalOutput").ap()
        return dr[name]

    x_own = din("x_own", [D, CT + 4])
    x_pre = din("x_pre", [D, PRE + 4])
    wxT = din("wxT", [D, 1025])
    wiT = din("wiT", [D, S])
    woT = din("woT", [S, D])
    din("cwv", [128, DT * 4])
    din("cbv", [128, DT])
    din("alog", [128, ST])
    din("dvec", [128, DT])
    wsynT = din("wsynT", [D, N])
    woutT = din("woutT", [N, D])
    din("bsyn_r", [128, N])
    din("bout_r", [128, D])
    din("gam_r", [128, D])
    din("bet_r", [128, D])

    t8syn = dint("t8syn", [D, N], FP8)
    t8out = dint("t8out", [N, D], FP8)
    x1fm_h = dint("x1fm_h", [NG, D, G], BF16)
    x1tk_h = dint("x1tk_h", [CT, D], BF16)

    out = dout("out", [CT, D])
    tp = {}
    for tname, shp, tdt in [("x1", [CT, D], BF16), ("z", [G, N], F32),
                            ("thr", [128, 64], F32), ("quant", [128, 16], F32),
                            ("carry", [128, ST], F32), ("dbg", [128, 64], F32)]:
        if tname in taps:
            tp[tname] = dout("tap_" + tname, shp, tdt)

    class _Suppress:
        def __enter__(self):
            return self

        def __exit__(self, et, ev, tb):
            return et is _TruncDone

    ctx = ExitStack()
    with ctx:
        ctx.enter_context(_Suppress())

        def sb(name, shape, dt=F32):
            return ctx.enter_context(nc.sbuf_tensor(name, shape, dt))

        seq = Seq(nc, ctx)

        def trunc_point(name):
            if trunc != name:
                return
            def tf(e):
                return e.dma_start(out=out[0:128, 0:32], in_=s_scal[:])
            seq.dma("sync", tf, sm_o, 1)
            sems = [sm_w, sm_g, sm_x, sm_q, sm_8, sm_t, sm_1, sm_o]
            def tw(e):
                for sm in sems:
                    e.wait_ge(sm.s, sm.v)
                return None
            seq.stage("sync", tw, inc=False)
            seq.flush()
            raise _TruncDone()

        s_bout = sb("s_bout", [128, D], BF16)
        s_gam = sb("s_gam", [128, D], BF16)
        s_bet = sb("s_bet", [128, D], BF16)
        s_ones = sb("s_ones", [128, 128], BF16)
        s_eye = sb("s_eye", [128, 128], BF16)
        s_io8 = sb("s_io8", [128, 8])
        s_scal = sb("s_scal", [128, 32])
        s_jk = sb("s_jk", [128, 8])

        def vjk(e):
            return e.tensor_scalar(out=s_jk[:, 0:1], in0=s_jk[:, 2:3],
                                   scalar1=1.0, scalar2=None, op0=ALU.mult)

        def sjk(e):
            return e.activation(out=s_jk[:, 1:2], in_=s_jk[:, 1:2],
                                func=AF.Copy)

        sm_w = Dsem(ctx, nc, "sm_w")
        sm_g = Dsem(ctx, nc, "sm_g")
        sm_x = Dsem(ctx, nc, "sm_x")
        sm_q = Dsem(ctx, nc, "sm_q")
        sm_8 = Dsem(ctx, nc, "sm_8")
        sm_t = Dsem(ctx, nc, "sm_t")
        sm_1 = Dsem(ctx, nc, "sm_1")
        sm_o = Dsem(ctx, nc, "sm_o")

        wscope = ExitStack()
        s_wx = wscope.enter_context(nc.sbuf_tensor("s_wx", [128, DT * 1024], BF16))
        s_wi = wscope.enter_context(nc.sbuf_tensor("s_wi", [128, DT * S], BF16))
        s_wo = wscope.enter_context(nc.sbuf_tensor("s_wo", [128, ST * D], BF16))
        s_wdrep = wscope.enter_context(
            nc.sbuf_tensor("s_wdrep", [128, DT * 128], BF16))
        s_cw = wscope.enter_context(nc.sbuf_tensor("s_cw", [128, DT * 4], F32))
        s_cb = wscope.enter_context(nc.sbuf_tensor("s_cb", [128, DT], F32))
        s_A = wscope.enter_context(nc.sbuf_tensor("s_A", [128, ST], F32))
        s_D = wscope.enter_context(nc.sbuf_tensor("s_D", [128, DT], F32))
        s_carry = wscope.enter_context(nc.sbuf_tensor("s_carry", [128, ST], F32))
        s_on32 = wscope.enter_context(nc.sbuf_tensor("s_on32", [128, 128], F32))

        def st0_loads(e):
            r = []
            for t, dst in [("cwv", s_cw), ("cbv", s_cb), ("dvec", s_D)]:
                r.append(e.dma_start(out=dst[:], in_=dr[t][:]))
            r.append(e.dma_start(out=s_scal[:, 8:8 + ST], in_=dr["alog"][:]))
            r.append(e.dma_start(
                out=s_scal[:, 16:16 + DT].rearrange("p (t o) -> p t o", o=1),
                in_=wxT[:, 1024:1025].rearrange("(t p) o -> p t o", p=128),
                allow_slow_non_contiguous=True))
            return r
        seq.dma("sync", st0_loads, sm_w, 5)

        def st0_casts(e):
            return [e.dma_start(out=dst[:], in_=dr[t][:])
                    for t, dst in [("bout_r", s_bout), ("gam_r", s_gam),
                                   ("bet_r", s_bet)]]
        seq.dma("gpsimd", st0_casts, sm_g, 3)
        w_small = sm_w.v

        def st0_iota(e):
            e.iota(s_eye[:], pattern=[[-1, 128]], channel_multiplier=1,
                   allow_small_or_imprecise_dtypes=True)
            return e.iota(s_io8[:], pattern=[[1, 8]], channel_multiplier=0,
                          allow_small_or_imprecise_dtypes=True)
        seq.stage("gpsimd", st0_iota)

        def st0_v(e):
            e.memset(s_ones[:], 1.0)
            e.memset(s_on32[:], 1.0)
            e.memset(s_jk[:], 0.0)
            e.memset(s_carry[:], 0.0)
            e.tensor_scalar(out=s_eye[:], in0=s_eye[:], scalar1=0.0,
                            scalar2=None, op0=ALU.is_equal)
            ins = None
            for dt_ in range(DT):
                ins = e.tensor_scalar(
                    out=s_wdrep[:, dt_ * 128:(dt_ + 1) * 128], in0=s_ones[:],
                    scalar1=s_scal[:, 16 + dt_:17 + dt_], scalar2=None,
                    op0=ALU.mult)
            return ins
        seq.stage("vector", st0_v, waits=((sm_w.s, w_small), (sm_g.s, sm_g.v)))

        def st0_s(e):
            return e.activation(out=s_A[:], in_=s_scal[:, 8:8 + ST],
                                func=AF.Exp)
        seq.stage("scalar", st0_s)

        def st0_v2(e):
            return e.tensor_scalar(out=s_A[:], in0=s_A[:], scalar1=-1.0,
                                   scalar2=None, op0=ALU.mult)
        seq.stage("vector", st0_v2)
        seq.flush()
        trunc_point("t0")

        NSL = DT + 16

        def slab_in(i):
            if i < DT:
                return wsynT[i * 128:(i + 1) * 128, :]
            j = i - DT
            return woutT[j * 512:(j + 1) * 512, :].rearrange(
                "(a p) c -> p a c", p=128)

        def slab_out8(i):
            if i < DT:
                return t8syn[i * 128:(i + 1) * 128, :]
            j = i - DT
            return t8out[j * 512:(j + 1) * 512, :].rearrange(
                "(a p) c -> p a c", p=128)

        with ExitStack() as qctx:
            qsb = lambda nm, shp, dt=F32: qctx.enter_context(
                nc.sbuf_tensor(nm, shp, dt))
            s_qa = qsb("s_qa", [128, N])
            s_qb = qsb("s_qb", [128, N])
            s_qi = qsb("s_qi", [128, N], I32)
            s_qo = s_qi[:].bitcast(F32)
            s_q8a = qsb("s_q8a", [128, N], FP8)
            s_q8b = qsb("s_q8b", [128, N], FP8)
            s_abs = qsb("s_abs", [128, 32])
            p_red = qctx.enter_context(nc.psum_tensor("p_red", [128, 32], F32))

            consumed = {}
            for i in range(NSL):
                buf = s_qa if i % 2 == 0 else s_qb
                def ld(e, i=i, buf=buf):
                    dst = buf[:] if i < DT else buf[:].rearrange(
                        "p (a c) -> p a c", a=4)
                    return e.dma_start(out=dst, in_=slab_in(i))
                seq.dma("sync", ld, sm_q, 1, dep=consumed.get(i - 2, 0))
                qv = sm_q.v
                def ab(e, i=i, buf=buf):
                    return e.activation(out=s_qo, in_=buf[:], func=AF.Abs,
                                        accum_out=s_abs[:, i:i + 1])
                consumed[i] = seq.stage("scalar", ab, waits=((sm_q.s, qv),))

            def red_mm(e):
                return e.matmul(p_red[:], s_on32[:], s_abs[:],
                                start=True, stop=True)
            seq.stage("tensor", red_mm)

            def red_v(e):
                e.tensor_reduce(out=s_scal[:, 0:1], in_=p_red[:, 0:DT],
                                axis=AX.X, op=ALU.add)
                e.tensor_reduce(out=s_scal[:, 1:2], in_=p_red[:, DT:NSL],
                                axis=AX.X, op=ALU.add)
                for c in range(2):
                    e.tensor_scalar(out=s_scal[:, c:c + 1],
                                    in0=s_scal[:, c:c + 1],
                                    scalar1=1.0 / (D * N), scalar2=None,
                                    op0=ALU.mult)
                for c in range(2):
                    e.tensor_scalar(out=s_scal[:, 5 + c:6 + c],
                                    in0=s_scal[:, c:c + 1],
                                    scalar1=1e-8, scalar2=None, op0=ALU.add)
                for c in range(2):
                    e.tensor_scalar(out=s_scal[:, 12 + c:13 + c],
                                    in0=s_scal[:, 5 + c:6 + c],
                                    scalar1=0.5, scalar2=None, op0=ALU.mult)
                for c in range(2):
                    e.tensor_scalar(out=s_scal[:, 14 + c:15 + c],
                                    in0=s_scal[:, 5 + c:6 + c],
                                    scalar1=-0.5, scalar2=None, op0=ALU.mult)
                return e.reciprocal(out=s_scal[:, 2:4], in_=s_scal[:, 5:7])
            seq.stage("vector", red_v)

            stored = {}
            for i in range(NSL):
                buf = s_qa if i % 2 == 0 else s_qb
                b8 = s_q8a if i % 2 == 0 else s_q8b
                def ld(e, i=i, buf=buf):
                    dst = buf[:] if i < DT else buf[:].rearrange(
                        "p (a c) -> p a c", a=4)
                    return e.dma_start(out=dst, in_=slab_in(i))
                seq.dma("sync", ld, sm_q, 1, dep=stored.get(i - 2, 0))
                qv = sm_q.v
                def qz(e, i=i, buf=buf, b8=b8):
                    co = 0 if i < DT else 1
                    e.tensor_scalar(out=s_qo, in0=buf[:],
                                    scalar1=s_scal[:, 14 + co:15 + co],
                                    scalar2=None, op0=ALU.is_le)
                    vjk(e)
                    return e.scalar_tensor_tensor(
                        out=b8[:], in0=buf[:],
                        scalar=s_scal[:, 12 + co:13 + co],
                        in1=s_qo, op0=ALU.is_ge, op1=ALU.subtract)
                seq.stage("vector", qz, waits=((sm_q.s, qv),))
                def stq(e, i=i, b8=b8):
                    srcb = b8[:] if i < DT else b8[:].rearrange(
                        "p (a c) -> p a c", a=4)
                    return e.dma_start(out=slab_out8(i), in_=srcb)
                seq.dma("gpsimd", stq, sm_8, 1)
                stored[i] = seq.n
            t8_done = sm_8.v

            def ldwx(e):
                return [
                    e.dma_start(out=s_qa[:, 0:N].rearrange(
                        "p (t c) -> p t c", t=8),
                        in_=wxT[0:1024, 0:1024].rearrange(
                            "(t p) c -> p t c", p=128)),
                    e.dma_start(out=s_qb[:, 0:N].rearrange(
                        "p (t c) -> p t c", t=8),
                        in_=wxT[1024:2048, 0:1024].rearrange(
                            "(t p) c -> p t c", p=128)),
                ]
            seq.dma("sync", ldwx, sm_q, 2)
            qv = sm_q.v

            def cvwx(e):
                e.tensor_copy(out=s_wx[:, 0:8 * 1024], in_=s_qa[:, 0:N])
                return e.tensor_copy(out=s_wx[:, 8 * 1024:DT * 1024],
                                     in_=s_qb[:, 0:N])
            seq.stage("vector", cvwx, waits=((sm_q.s, qv),))

            def ldwio(e):
                return [
                    e.dma_start(out=s_qa[:, 0:DT * S].rearrange(
                        "p (t c) -> p t c", t=DT),
                        in_=wiT.rearrange("(t p) c -> p t c", p=128)),
                    e.dma_start(out=s_qb[:, 0:ST * D].rearrange(
                        "p (t c) -> p t c", t=ST),
                        in_=woT.rearrange("(t p) c -> p t c", p=128)),
                ]
            seq.dma("sync", ldwio, sm_q, 2)
            qv = sm_q.v

            def cvwio(e):
                e.tensor_copy(out=s_wi[:], in_=s_qa[:, 0:DT * S])
                return e.tensor_copy(out=s_wo[:], in_=s_qb[:, 0:ST * D])
            quant_tail = seq.stage("vector", cvwio, waits=((sm_q.s, qv),))

            if "quant" in tp:
                def tq(e):
                    return e.dma_start(out=tp["quant"][:, 0:4], in_=s_scal[:, 0:4])
                seq.dma("sync", tq, sm_o, 1)
            if trunc == "quant":
                def tdbg(e):
                    return e.dma_start(out=out[128:256, 0:32], in_=s_abs[:])
                seq.dma("sync", tdbg, sm_o, 1)
            seq.flush()
        trunc_point("quant")

        with ExitStack() as mctx:
            msb = lambda nm, shp, dt=F32: mctx.enter_context(
                nc.sbuf_tensor(nm, shp, dt))
            s_xbs = [msb(f"s_xb{i}", [128, DT * 520], BF16) for i in range(2)]
            s_xc = msb("s_xc", [128, DT * 512], BF16)
            s_B = msb("s_B", [128, ST * 512], BF16)
            s_C = msb("s_C", [128, ST * 512], BF16)
            s_dc = msb("s_dc", [128, 512])
            s_dA = msb("s_dA", [128, ST * 512], BF16)
            s_Bu = msb("s_Bu", [128, ST * 512], BF16)
            s_h = msb("s_h", [128, ST * 512])
            s_hm = msb("s_hm", [128, ST * 512], BF16)
            s_mu = msb("s_mu", [128, 512])
            s_va = msb("s_va", [128, 512])
            s_rs = msb("s_rs", [128, 512])
            s_tm = msb("s_tm", [128, 512])
            s_x1f = msb("s_x1f", [128, DT * 512], BF16)
            s_sq = s_x1f
            s_x1t = msb("s_x1t", [128, MT * D], BF16)

            PS = [mctx.enter_context(nc.psum_tensor(f"p_s{i}", [128, 512], F32))
                  for i in range(4)]
            p_mu = mctx.enter_context(nc.psum_tensor("p_mu", [128, 512], F32))
            p_vr = mctx.enter_context(nc.psum_tensor("p_vr", [128, 512], F32))
            p_tb = mctx.enter_context(nc.psum_tensor("p_tb", [128, 1024], BF16))
            p_tb2 = mctx.enter_context(
                nc.psum_tensor("p_tb2", [128, 1024], BF16))

            def chunk_src(i):
                if i < NPC:
                    return x_pre[:, i * 512:i * 512 + 516]
                j = i - NPC
                return x_own[:, j * 512:j * 512 + 516]

            x_used = {}
            pv = {}
            for ci in range(NPC + NLC):
                own = ci >= NPC
                oc = ci - NPC
                cur = {}
                s_xb = s_xbs[ci % 2]

                def ldx(e, ci=ci, s_xb=s_xb):
                    return e.dma_start(
                        out=s_xb[:].rearrange("p (t c) -> p t c",
                                              t=DT)[:, :, 4:520],
                        in_=chunk_src(ci).rearrange("(t p) c -> p t c", p=128))
                seq.dma("gpsimd", ldx, sm_x, 1,
                        dep=x_used.get(ci - 2, quant_tail))
                xv = sm_x.v

                def conv(e, s_xb=s_xb):
                    for da in range(0, DT, 2):
                        pair = (da, da + 1)
                        for dt_ in pair:
                            xo = dt_ * 520 + 5
                            e.tensor_scalar(
                                out=s_xc[:, dt_ * 512:(dt_ + 1) * 512],
                                in0=s_xb[:, xo:xo + 512],
                                scalar1=s_cw[:, dt_ * 4:dt_ * 4 + 1],
                                scalar2=s_cb[:, dt_:dt_ + 1],
                                op0=ALU.mult, op1=ALU.add)
                        for k in range(1, 4):
                            for dt_ in pair:
                                xo = dt_ * 520 + 5
                                acc = s_xc[:, dt_ * 512:(dt_ + 1) * 512]
                                ins = e.scalar_tensor_tensor(
                                    out=acc,
                                    in0=s_xb[:, xo + k:xo + k + 512],
                                    scalar=s_cw[:, dt_ * 4 + k:
                                                 dt_ * 4 + k + 1],
                                    in1=acc, op0=ALU.mult, op1=ALU.add)
                    return ins
                cur["conv"] = seq.stage(
                    "vector", conv, waits=((sm_x.s, xv),),
                    after=[pv.get("projXS", quant_tail), pv.get("evq3", 0),
                           pv.get("ln1c", 0)])
                if ci == 0:
                    trunc_point("mc")

                def projB(e, s_xb=s_xb):
                    ins = None
                    for j in range(ST):
                        for dt_ in range(DT):
                            ins = e.matmul(
                                PS[j][:],
                                s_wx[:, dt_ * 1024 + j * 128:
                                     dt_ * 1024 + (j + 1) * 128],
                                s_xb[:, dt_ * 520 + 8:dt_ * 520 + 520],
                                start=(dt_ == 0), stop=(dt_ == DT - 1))
                    return ins
                cur["projB"] = seq.stage(
                    "tensor", projB, waits=((sm_x.s, xv),),
                    after=[pv.get("evXS", quant_tail), pv.get("evC", 0),
                           pv.get("evq3", 0)])
                if ci == 0:
                    trunc_point("mpb")
                def projD(e, s_xb=s_xb):
                    ins = None
                    for dt_ in range(DT):
                        ins = e.matmul(
                            p_mu[:], s_wdrep[:, dt_ * 128:(dt_ + 1) * 128],
                            s_xb[:, dt_ * 520 + 8:dt_ * 520 + 520],
                            start=(dt_ == 0), stop=(dt_ == DT - 1))
                    return ins
                cur["projD"] = seq.stage(
                    "tensor", projD, waits=((sm_x.s, xv),),
                    after=[pv.get("sp1", quant_tail), pv.get("ln1a", 0)])

                def sp1(e):
                    e.activation(out=s_tm[:], in_=p_mu[:], func=AF.Abs)
                    e.activation(out=s_va[:], in_=p_mu[:], func=AF.Relu)
                    e.activation(out=s_tm[:], in_=s_tm[:], func=AF.Exp,
                                 scale=-1.0)
                    sjk(e)
                    return e.activation(out=s_tm[:], in_=s_tm[:], func=AF.Ln,
                                        bias=1.0)
                cur["sp1"] = seq.stage(
                    "scalar", sp1,
                    after=[cur["projD"], pv.get("ln1a", 0),
                           pv.get("sp2", 0)])

                def sp2(e):
                    return e.tensor_tensor(out=s_dc[:], in0=s_va[:],
                                           in1=s_tm[:], op=ALU.add)
                cur["sp2"] = seq.stage("vector", sp2, after=[cur["sp1"]])

                def dax(e):
                    ins = None
                    for j in range(ST):
                        ins = e.activation(
                            out=s_dA[:, j * 512:(j + 1) * 512], in_=s_dc[:],
                            func=AF.Exp, scale=s_A[:, j:j + 1])
                    return ins
                cur["dax"] = seq.stage("scalar", dax, after=[cur["sp2"]])

                def evB(e):
                    ins = None
                    for j in range(ST):
                        ins = e.tensor_copy(out=s_B[:, j * 512:(j + 1) * 512],
                                            in_=PS[j][:])
                    return ins
                cur["evB"] = seq.stage(
                    "vector", evB,
                    after=[cur["projB"], pv.get("buscan", 0)])
                if ci == 0:
                    trunc_point("mb")

                def projXS(e):
                    ins = None
                    for j in range(ST):
                        for dt_ in range(DT):
                            ins = e.matmul(
                                PS[j][:],
                                s_wi[:, dt_ * 512 + j * 128:
                                     dt_ * 512 + (j + 1) * 128],
                                s_xc[:, dt_ * 512:(dt_ + 1) * 512],
                                start=(dt_ == 0), stop=(dt_ == DT - 1))
                    return ins
                cur["projXS"] = seq.stage(
                    "tensor", projXS, after=[cur["conv"], cur["evB"]])

                def evXS(e):
                    ins = None
                    for j in range(ST):
                        ins = e.tensor_tensor(
                            out=s_Bu[:, j * 512:(j + 1) * 512], in0=PS[j][:],
                            in1=s_B[:, j * 512:(j + 1) * 512], op=ALU.mult)
                    return ins
                cur["evXS"] = seq.stage(
                    "vector", evXS,
                    after=[cur["projXS"], pv.get("buscan", 0)])

                if own:
                    def projC(e, s_xb=s_xb):
                        ins = None
                        for j in range(ST):
                            for dt_ in range(DT):
                                ins = e.matmul(
                                    PS[j][:],
                                    s_wx[:, dt_ * 1024 + 512 + j * 128:
                                         dt_ * 1024 + 512 + (j + 1) * 128],
                                    s_xb[:, dt_ * 520 + 8:dt_ * 520 + 520],
                                    start=(dt_ == 0), stop=(dt_ == DT - 1))
                        return ins
                    cur["projC"] = seq.stage("tensor", projC,
                                             after=[cur["evXS"]])

                    def evC(e):
                        ins = None
                        for j in range(ST):
                            ins = e.tensor_copy(
                                out=s_C[:, j * 512:(j + 1) * 512],
                                in_=PS[j][:])
                        return ins
                    cur["evC"] = seq.stage(
                        "vector", evC,
                        after=[cur["projC"], pv.get("buscan", 0)])


                def buscan(e, own=own):
                    for j in range(ST):
                        o = j * 512
                        e.tensor_tensor(out=s_Bu[:, o:o + 512],
                                        in0=s_Bu[:, o:o + 512], in1=s_dc[:],
                                        op=ALU.mult)
                    for j in range(ST):
                        o = j * 512
                        e.tensor_tensor_scan(
                            out=s_h[:, o:o + 512], data0=s_dA[:, o:o + 512],
                            data1=s_Bu[:, o:o + 512],
                            initial=s_carry[:, j:j + 1],
                            op0=ALU.mult, op1=ALU.add)
                    for j in range(ST):
                        o = j * 512
                        ins = e.tensor_copy(out=s_carry[:, j:j + 1],
                                            in_=s_h[:, o + 511:o + 512])
                    if own:
                        for j in range(ST):
                            o = j * 512
                            ins = e.tensor_tensor(out=s_hm[:, o:o + 512],
                                                  in0=s_h[:, o:o + 512],
                                                  in1=s_C[:, o:o + 512],
                                                  op=ALU.mult)
                    return ins
                bus_after = [cur["dax"], cur["evB"], cur["evXS"]]
                if own:
                    bus_after.append(cur["evC"])
                cur["buscan"] = seq.stage("vector", buscan,
                                          after=bus_after)
                if ci == 0:
                    trunc_point("ms")
                if ci == 2:
                    trunc_point("ms3")

                if not own:
                    x_used[ci] = cur["buscan"]
                    pv = cur
                    continue

                prev_evq = cur["evC"]
                for quad in range(4):
                    def opq(e, quad=quad):
                        ins = None
                        for dt_ in range(quad * 4, quad * 4 + 4):
                            pb = PS[dt_ % 4]
                            for j in range(ST):
                                ins = e.matmul(
                                    pb[:],
                                    s_wo[:, j * D + dt_ * 128:
                                         j * D + (dt_ + 1) * 128],
                                    s_hm[:, j * 512:(j + 1) * 512],
                                    start=(j == 0), stop=(j == ST - 1))
                        return ins
                    cur[f"opq{quad}"] = seq.stage(
                        "tensor", opq,
                        after=[cur["buscan"], prev_evq])

                    def evq(e, quad=quad, s_xb=s_xb):
                        for dt_ in range(quad * 4, quad * 4 + 4):
                            pb = PS[dt_ % 4]
                            xo = dt_ * 520 + 8
                            e.scalar_tensor_tensor(
                                out=s_xb[:, xo:xo + 512], in0=pb[:], scalar=1.0,
                                in1=s_xb[:, xo:xo + 512],
                                op0=ALU.mult, op1=ALU.add)
                        ins = None
                        for dt_ in range(quad * 4, quad * 4 + 4):
                            xo = dt_ * 520 + 8
                            ins = e.scalar_tensor_tensor(
                                out=s_xb[:, xo:xo + 512],
                                in0=s_xc[:, dt_ * 512:(dt_ + 1) * 512],
                                scalar=s_D[:, dt_:dt_ + 1],
                                in1=s_xb[:, xo:xo + 512],
                                op0=ALU.mult, op1=ALU.add)
                        return ins
                    prev_evq = seq.stage("vector", evq,
                                         after=[cur[f"opq{quad}"]])
                    cur[f"evq{quad}"] = prev_evq

                def sqs(e, s_xb=s_xb):
                    ins = None
                    for dt_ in range(DT):
                        xo = dt_ * 520 + 8
                        ins = e.activation(
                            out=s_sq[:, dt_ * 512:(dt_ + 1) * 512],
                            in_=s_xb[:, xo:xo + 512], func=AF.Square)
                    return ins
                cur["sqs"] = seq.stage(
                    "scalar", sqs,
                    after=[cur["evq3"], pv.get("evt7", 0)],
                    waits=((sm_1.s, sm_1.v),))

                def stat_mm(e, s_xb=s_xb):
                    ins = None
                    for dt_ in range(DT):
                        xo = dt_ * 520 + 8
                        e.matmul(p_mu[:], s_ones[:], s_xb[:, xo:xo + 512],
                                 start=(dt_ == 0), stop=(dt_ == DT - 1))
                    for dt_ in range(DT):
                        ins = e.matmul(p_vr[:], s_ones[:],
                                       s_sq[:, dt_ * 512:(dt_ + 1) * 512],
                                       start=(dt_ == 0), stop=(dt_ == DT - 1))
                    return ins
                cur["stat_mm"] = seq.stage("tensor", stat_mm,
                                           after=[cur["sqs"]])

                def ln1a(e):
                    e.tensor_scalar(out=s_mu[:], in0=p_mu[:], scalar1=1.0 / D,
                                    scalar2=None, op0=ALU.mult)
                    e.tensor_scalar(out=s_va[:], in0=p_vr[:], scalar1=1.0 / D,
                                    scalar2=LNEPS, op0=ALU.mult, op1=ALU.add)
                    e.tensor_tensor(out=s_tm[:], in0=s_mu[:], in1=s_mu[:],
                                    op=ALU.mult)
                    vjk(e)
                    return e.tensor_tensor(out=s_va[:], in0=s_va[:],
                                           in1=s_tm[:], op=ALU.subtract)
                cur["ln1a"] = seq.stage("vector", ln1a,
                                        after=[cur["stat_mm"]])

                def ln1b(e):
                    e.activation(out=s_rs[:], in_=s_va[:], func=AF.Ln)
                    sjk(e)
                    return e.activation(out=s_rs[:], in_=s_rs[:], func=AF.Exp,
                                        scale=-0.5)
                cur["ln1b"] = seq.stage("scalar", ln1b, after=[cur["ln1a"]])

                def ln1c(e, s_xb=s_xb):
                    ins = None
                    for da in range(0, DT, 2):
                        for dt_ in (da, da + 1):
                            xo = dt_ * 520 + 8
                            o = dt_ * 512
                            e.tensor_tensor(out=s_x1f[:, o:o + 512],
                                            in0=s_xb[:, xo:xo + 512],
                                            in1=s_mu[:], op=ALU.subtract)
                        for dt_ in (da, da + 1):
                            o = dt_ * 512
                            ins = e.tensor_tensor(out=s_x1f[:, o:o + 512],
                                                  in0=s_x1f[:, o:o + 512],
                                                  in1=s_rs[:], op=ALU.mult)
                    return ins
                cur["ln1c"] = seq.stage("vector", ln1c, after=[cur["ln1b"]])
                x_used[ci] = cur["ln1c"]
                if ci == NPC:
                    trunc_point("ml")

                evt_ids = {-2: pv.get("evt6", 0), -1: pv.get("evt7", 0)}
                for m in range(MT):
                    for half in range(2):
                        k = m * 2 + half
                        ptb = p_tb if k % 2 == 0 else p_tb2

                        def trx(e, m=m, half=half, ptb=ptb):
                            ins = None
                            for bi in range(8):
                                dt_ = half * 8 + bi
                                ins = e.transpose(
                                    ptb[:, bi * 128:(bi + 1) * 128],
                                    s_x1f[:, dt_ * 512 + m * 128:
                                          dt_ * 512 + (m + 1) * 128],
                                    s_eye[:])
                            return ins
                        tid = seq.stage(
                            "tensor", trx,
                            after=[cur["ln1c"], evt_ids.get(k - 2, 0)])

                        def evt(e, m=m, half=half, ptb=ptb):
                            return e.tensor_copy(
                                out=s_x1t[:, m * D + half * 1024:
                                          m * D + (half + 1) * 1024],
                                in_=ptb[:])
                        w_ = ((sm_1.s, sm_1.v),) if k == 0 else ()
                        evt_ids[k] = seq.stage("vector", evt, after=[tid],
                                               waits=w_)
                cur["evt6"] = evt_ids[6]
                cur["evt7"] = evt_ids[7]

                def stx1(e, oc=oc):
                    r = [
                        e.dma_start(
                            out=x1fm_h[oc, :, :].rearrange(
                                "(t p) c -> p t c", p=128),
                            in_=s_x1f[:].rearrange("p (t c) -> p t c", t=DT)),
                        e.dma_start(
                            out=x1tk_h[oc * 512:(oc + 1) * 512, :].rearrange(
                                "(m p) c -> p m c", p=128),
                            in_=s_x1t[:].rearrange("p (m c) -> p m c", m=MT)),
                    ]
                    if "x1" in tp:
                        r.append(e.dma_start(
                            out=tp["x1"][oc * 512:(oc + 1) * 512, :].rearrange(
                                "(m p) c -> p m c", p=128),
                            in_=s_x1t[:].rearrange("p (m c) -> p m c", m=MT)))
                    return r
                seq.dma("sync", stx1, sm_1, 3 if "x1" in tp else 2)
                pv = cur

            if "carry" in tp:
                def tcar(e):
                    return e.dma_start(out=tp["carry"][:], in_=s_carry[:])
                seq.dma("sync", tcar, sm_o, 1)
            x1_all = sm_1.v
            seq.flush()
        wscope.close()
        trunc_point("mamba")

        with ExitStack() as fctx:
            fsb = lambda nm, shp, dt=F32: fctx.enter_context(
                nc.sbuf_tensor(nm, shp, dt))
            s_za = [fsb(f"s_z{m}", [128, N]) for m in range(MT)]
            s_zoneA = fsb("s_zoneA", [128, N], BF16)
            s_zoneB = fsb("s_zoneB", [128, 2 * N], BF16)
            s_x1k = fsb("s_x1k", [128, MT * D], BF16)
            s_bs = fsb("s_bs", [128, 2 * 512], BF16)
            s_zp = fsb("s_zp", [128, MT * 16])
            s_pk = fsb("s_pk", [128, MT * 16])
            s_w8 = fsb("s_w8", [128, 8])
            s_sg = fsb("s_sg", [128, 8])

            def r2m(m):
                return s_za[m][:].bitcast(BF16)[:, N:N + D]

            def ffn_init(e):
                e.memset(s_pk[:], 0.0)
                return e.memset(s_zp[:], 0.0)
            g_prev_psum = seq.stage("vector", ffn_init)
            g_prev_za = 0
            g_prev_mm2 = 0
            g_prev_lnap = 0
            g_prev_sov = {}
            s_x1g = s_zoneA
            s_hidt = s_zoneA
            s_to1 = s_zoneA
            s_ts = (s_zoneB[:, 0:N], s_zoneB[:, N:2 * N])
            s_wv = s_zoneB[:].bitcast(F32)
            s_to0 = s_zoneB[:, 0:N]
            s_r3 = s_zoneB[:].bitcast(F32)[:, 0:D]
            s_o2 = s_zoneB[:].bitcast(F32)[:, D:2 * D]


            for g in range(NG):
                with ExitStack() as gctx:
                    zscope = ExitStack()
                    p_z = [zscope.enter_context(nc.psum_tensor(
                        f"p_z_{g}_{i}", [128, 512], F32)) for i in range(8)]

                    def ldx1g(e, g=g):
                        return e.dma_start(
                            out=s_x1g[:].rearrange("p (t c) -> p t c", t=DT),
                            in_=x1fm_h[g, :, :].rearrange(
                                "(t p) c -> p t c", p=128))
                    seq.dma("sync", ldx1g, sm_1, 1,
                            dep=g_prev_mm2, waits=((sm_1.s, x1_all),))
                    x1v = sm_1.v

                    def ldx1k(e, g=g):
                        return e.dma_start(
                            out=s_x1k[:].rearrange("p (m c) -> p m c", m=MT),
                            in_=x1tk_h[g * 512:(g + 1) * 512, :].rearrange(
                                "(m p) c -> p m c", p=128))
                    seq.dma("sync", ldx1k, sm_1, 1,
                            dep=g_prev_lnap, waits=((sm_1.s, x1_all),))
                    x1kv = sm_1.v

                    ts_used = {}
                    ev1_ids = {}
                    for nch in range(NCH):
                        buf = s_ts[nch % 2]
                        def ldt(e, nch=nch, buf=buf):
                            return [e.dma_start(
                                out=buf.rearrange("p (t c) -> p t c", t=DT),
                                in_=t8syn[:, nch * 512:(nch + 1) * 512]
                                .rearrange("(t p) c -> p t c", p=128)),
                                e.dma_start(
                                    out=s_bs[:, (nch % 2) * 512:
                                             (nch % 2) * 512 + 512],
                                    in_=dr["bsyn_r"][0:128,
                                                     nch * 512:(nch + 1) * 512])]
                        ldt_sov = g_prev_sov.get(2 + nch % 2, sm_o.v) \
                            if nch <= 1 else sm_o.v
                        seq.dma("gpsimd", ldt, sm_t, 2,
                                dep=ts_used.get(nch - 2, None),
                                waits=((sm_8.s, t8_done),
                                       (sm_o.s, ldt_sov),
                                       (sm_1.s, x1_all)))
                        tv = sm_t.v

                        par = 4 * (nch % 2)

                        def mm1(e, nch=nch, buf=buf, par=par):
                            ins = None
                            for m in range(MT):
                                for dt_ in range(DT):
                                    ins = e.matmul(
                                        p_z[par + m][:],
                                        s_x1g[:, dt_ * 512 + m * 128:
                                              dt_ * 512 + (m + 1) * 128],
                                        buf[:, dt_ * 512:(dt_ + 1) * 512],
                                        start=(dt_ == 0),
                                        stop=(dt_ == DT - 1))
                            return ins
                        w = [(sm_t.s, tv)]
                        if nch == 0:
                            w.append((sm_1.s, x1v))
                        mm1_after = [ev1_ids.get(nch - 2, 0)]
                        if nch <= 1:
                            mm1_after.append(g_prev_psum)
                        ts_used[nch] = seq.stage("tensor", mm1, waits=w,
                                                 after=mm1_after)

                        def ev1(e, nch=nch, par=par):
                            ins = None
                            for m in range(MT):
                                ins = e.scalar_tensor_tensor(
                                    out=s_za[m][:, nch * 512:(nch + 1) * 512],
                                    in0=p_z[par + m][:], scalar=s_scal[:, 0:1],
                                    in1=s_bs[:, (nch % 2) * 512:
                                             (nch % 2) * 512 + 512],
                                    op0=ALU.mult, op1=ALU.add,
                                    accum_out=s_zp[:, m * 16 + nch:
                                                   m * 16 + nch + 1])
                            return ins
                        ev1_after = [ts_used[nch]]
                        if nch == 0:
                            ev1_after.append(g_prev_za)
                        ev1_ids[nch] = seq.stage("vector", ev1,
                                                 after=ev1_after)

                    if "z" in tp and g == 2:
                        def tz(e):
                            return [e.dma_start(
                                out=tp["z"][m * 128:(m + 1) * 128, :],
                                in_=s_za[m][:]) for m in range(MT)]
                        seq.dma("sync", tz, sm_o, MT)

                    zscope.close()
                    htscope = ExitStack()
                    p_ht = htscope.enter_context(
                        nc.psum_tensor(f"p_ht_{g}", [128, 1024], BF16))
                    p_ht2 = htscope.enter_context(
                        nc.psum_tensor(f"p_ht2_{g}", [128, 1024], BF16))

                    def col(m, j):
                        return s_pk[:, m * 16 + j:m * 16 + j + 1]

                    def tk0a(e):
                        ins = None
                        for m in range(MT):
                            ins = e.activation(out=s_zoneA[:], in_=s_za[m][:],
                                               func=AF.Square,
                                               accum_out=col(m, 13))
                        return ins
                    id_tk0a = seq.stage("scalar", tk0a,
                                        after=[ev1_ids[NCH - 1]])

                    def tk0b(e):
                        for m in range(MT):
                            e.tensor_reduce(out=col(m, 0),
                                            in_=s_zp[:, m * 16:m * 16 + 16],
                                            axis=AX.X, op=ALU.add)
                        for m in range(MT):
                            e.tensor_scalar(out=col(m, 0), in0=col(m, 0),
                                            scalar1=1.0 / N, scalar2=None,
                                            op0=ALU.mult)
                        for m in range(MT):
                            e.tensor_tensor(out=col(m, 11), in0=col(m, 0),
                                            in1=col(m, 0), op=ALU.mult)
                        ins = None
                        for m in range(MT):
                            ins = e.scalar_tensor_tensor(
                                out=col(m, 12), in0=col(m, 13),
                                scalar=1.0 / N, in1=col(m, 11),
                                op0=ALU.mult, op1=ALU.subtract)
                        return ins
                    id_tk0b = seq.stage("vector", tk0b, after=[id_tk0a])

                    def tk0c(e):
                        for m in range(MT):
                            e.activation(out=col(m, 12), in_=col(m, 12),
                                         func=AF.Ln)
                        ins = None
                        for m in range(MT):
                            ins = e.activation(out=col(m, 12), in_=col(m, 12),
                                               func=AF.Exp, scale=0.5)
                        return ins
                    id_tk0c = seq.stage("scalar", tk0c, after=[id_tk0b])

                    def tk0d(e):
                        ins = None
                        for m in range(MT):
                            for c_, coef in ((8, SIG_SPAN), (1, BR_LO),
                                             (2, BR_HI), (5, LAD0)):
                                e.scalar_tensor_tensor(
                                    out=col(m, c_), in0=col(m, 12),
                                    scalar=coef, in1=col(m, 0),
                                    op0=ALU.mult, op1=ALU.add)
                            e.memset(col(m, 3), 8192.0)
                            e.memset(col(m, 4), 0.0)
                            e.memset(col(m, 9), 0.0)
                            ins = e.memset(col(m, 10), -1.0)
                        return ins
                    id_upd = seq.stage("vector", tk0d, after=[id_tk0c])

                    for pi in range(NPROBE):
                        def cnt_v(e):
                            ins = None
                            for m in range(2):
                                ins = e.tensor_scalar(
                                    out=s_zoneA[:], in0=s_za[m][:],
                                    scalar1=col(m, 5),
                                    scalar2=None, op0=ALU.is_ge, op1=ALU.add,
                                    accum_out=col(m, 6))
                            return ins
                        id_cv = seq.stage("vector", cnt_v, after=[id_upd])

                        def cnt_s(e):
                            ins = None
                            for m in range(2, MT):
                                ins = e.activation(
                                    out=s_zoneA[:], in_=s_za[m][:],
                                    func=AF.Sign, scale=-1.0,
                                    bias=col(m, 5),
                                    accum_out=col(m, 6))
                            return ins
                        id_cs = seq.stage("scalar", cnt_s, after=[id_upd])

                        def upd(e, pi=pi, last=(pi == NPROBE - 1)):
                            P4 = s_pk[:].rearrange("p (m c) -> p m c", c=16)

                            def c4(j):
                                return P4[:, :, j:j + 1]
                            for m in range(2, MT):
                                e.tensor_scalar(out=col(m, 6), in0=col(m, 6),
                                                scalar1=-0.5,
                                                scalar2=float(N // 2),
                                                op0=ALU.mult, op1=ALU.add)
                            vjk(e)
                            e.tensor_scalar(out=c4(7), in0=c4(6),
                                            scalar1=float(TOPK) - 0.5,
                                            scalar2=None, op0=ALU.is_ge)
                            e.tensor_tensor(out=c4(11), in0=c4(5), in1=c4(1),
                                            op=ALU.subtract)
                            e.tensor_tensor(out=c4(14), in0=c4(6), in1=c4(3),
                                            op=ALU.subtract)
                            e.tensor_tensor(out=c4(11), in0=c4(11), in1=c4(7),
                                            op=ALU.mult)
                            e.tensor_tensor(out=c4(14), in0=c4(14), in1=c4(7),
                                            op=ALU.mult)
                            e.tensor_tensor(out=c4(1), in0=c4(1), in1=c4(11),
                                            op=ALU.add)
                            e.tensor_tensor(out=c4(3), in0=c4(3), in1=c4(14),
                                            op=ALU.add)
                            e.tensor_scalar(out=c4(7), in0=c4(7), scalar1=-1.0,
                                            scalar2=1.0, op0=ALU.mult,
                                            op1=ALU.add)
                            e.tensor_tensor(out=c4(11), in0=c4(5), in1=c4(2),
                                            op=ALU.subtract)
                            e.tensor_tensor(out=c4(14), in0=c4(6), in1=c4(4),
                                            op=ALU.subtract)
                            e.tensor_tensor(out=c4(11), in0=c4(11), in1=c4(7),
                                            op=ALU.mult)
                            e.tensor_tensor(out=c4(14), in0=c4(14), in1=c4(7),
                                            op=ALU.mult)
                            e.tensor_tensor(out=c4(2), in0=c4(2), in1=c4(11),
                                            op=ALU.add)
                            e.tensor_tensor(out=c4(4), in0=c4(4), in1=c4(14),
                                            op=ALU.add)
                            e.tensor_scalar(out=c4(7), in0=c4(6),
                                            scalar1=float(TOPK) - 0.5,
                                            scalar2=None, op0=ALU.is_ge)
                            e.tensor_scalar(out=c4(11), in0=c4(6),
                                            scalar1=float(TOPK + 7),
                                            scalar2=None, op0=ALU.is_le)
                            vjk(e)
                            e.tensor_tensor(out=c4(7), in0=c4(7), in1=c4(11),
                                            op=ALU.mult)
                            e.tensor_tensor(out=c4(14), in0=c4(5), in1=c4(9),
                                            op=ALU.subtract)
                            e.tensor_tensor(out=c4(11), in0=c4(6), in1=c4(10),
                                            op=ALU.subtract)
                            e.tensor_tensor(out=c4(14), in0=c4(14), in1=c4(7),
                                            op=ALU.mult)
                            e.tensor_tensor(out=c4(11), in0=c4(11), in1=c4(7),
                                            op=ALU.mult)
                            e.tensor_tensor(out=c4(9), in0=c4(9), in1=c4(14),
                                            op=ALU.add)
                            e.tensor_tensor(out=c4(10), in0=c4(10), in1=c4(11),
                                            op=ALU.add)
                            if pi == 0:
                                e.scalar_tensor_tensor(
                                    out=c4(5), in0=c4(12), scalar=LAD1,
                                    in1=c4(0), op0=ALU.mult, op1=ALU.add)
                            elif pi < NPROBE - 1 - N_FP:
                                e.tensor_tensor(out=c4(5), in0=c4(1),
                                                in1=c4(2), op=ALU.add)
                                vjk(e)
                                e.tensor_scalar(out=c4(5), in0=c4(5),
                                                scalar1=0.5, scalar2=None,
                                                op0=ALU.mult)
                            elif not last:
                                e.tensor_tensor(out=c4(7), in0=c4(3),
                                                in1=c4(4), op=ALU.subtract)
                                e.tensor_scalar(out=c4(11), in0=c4(3),
                                                scalar1=-AIM, scalar2=None,
                                                op0=ALU.add)
                                e.tensor_scalar(out=c4(7), in0=c4(7),
                                                scalar1=1e-6, scalar2=None,
                                                op0=ALU.add)
                                e.tensor_tensor(out=c4(14), in0=c4(2),
                                                in1=c4(1), op=ALU.subtract)
                                e.reciprocal(out=c4(7), in_=c4(7))
                                vjk(e)
                                e.tensor_tensor(out=c4(11), in0=c4(11),
                                                in1=c4(7), op=ALU.mult)
                                vjk(e)
                                e.tensor_tensor(out=c4(11), in0=c4(11),
                                                in1=c4(14), op=ALU.mult)
                                vjk(e)
                                e.tensor_tensor(out=c4(5), in0=c4(1),
                                                in1=c4(11), op=ALU.add)
                            return vjk(e)
                        id_upd = seq.stage("vector", upd,
                                           after=[id_cs, id_cv])

                    def fin(e):
                        for m in range(MT):
                            e.tensor_scalar(out=col(m, 7), in0=col(m, 10),
                                            scalar1=0.0, scalar2=None,
                                            op0=ALU.is_ge)
                        for dst, src in ((1, 9), (3, 10)):
                            for m in range(MT):
                                e.tensor_tensor(out=col(m, 11),
                                                in0=col(m, src),
                                                in1=col(m, dst),
                                                op=ALU.subtract)
                            for m in range(MT):
                                e.tensor_tensor(out=col(m, 11), in0=col(m, 11),
                                                in1=col(m, 7), op=ALU.mult)
                            for m in range(MT):
                                e.tensor_tensor(out=col(m, dst),
                                                in0=col(m, dst),
                                                in1=col(m, 11), op=ALU.add)
                        for m in range(MT):
                            e.tensor_scalar(out=col(m, 6), in0=col(m, 3),
                                            scalar1=float(-TOPK), scalar2=0.0,
                                            op0=ALU.add, op1=ALU.max)
                        for m in range(MT):
                            e.tensor_scalar(out=col(m, 6), in0=col(m, 6),
                                            scalar1=7.0, scalar2=None,
                                            op0=ALU.min)
                        for m in range(MT):
                            e.tensor_scalar(out=col(m, 14), in0=col(m, 6),
                                            scalar1=-0.25, scalar2=None,
                                            op0=ALU.add)
                        ins = None
                        for m in range(MT):
                            ins = e.tensor_scalar(out=col(m, 15),
                                                  in0=col(m, 6),
                                                  scalar1=0.6, scalar2=None,
                                                  op0=ALU.add)
                        return ins
                    seq.stage("vector", fin)

                    def tcm(e):
                        ins = None
                        for m in range(MT):
                            ins = e.tensor_scalar(
                                out=col(m, 7), in0=col(m, 1), scalar1=-1e-5,
                                scalar2=0.0, op0=ALU.add, op1=ALU.max)
                        return ins
                    seq.stage("vector", tcm)

                    evh_ids = {}
                    for m in range(MT):
                        zs = s_za[m][:]
                        hTm = s_za[m][:].bitcast(BF16)[:, 0:N]
                        pkb = m * 16

                        def hid(e, pkb=pkb, zs=zs):
                            return e.scalar_tensor_tensor(
                                out=s_hidt[:], in0=zs,
                                scalar=s_pk[:, pkb + 7:pkb + 8], in1=zs,
                                op0=ALU.is_ge, op1=ALU.mult)
                        id_hid = seq.stage("vector", hid)

                        for half in range(8):
                            kg = m * 8 + half
                            pht = p_ht if kg % 2 == 0 else p_ht2

                            def trh(e, half=half, pht=pht):
                                ins = None
                                for bi in range(8):
                                    nt = half * 8 + bi
                                    ins = e.transpose(
                                        pht[:, bi * 128:(bi + 1) * 128],
                                        s_hidt[:, nt * 128:(nt + 1) * 128],
                                        s_eye[:])
                                return ins
                            id_trh = seq.stage(
                                "tensor", trh,
                                after=[id_hid, evh_ids.get(kg - 2, 0)])

                            def evh(e, m=m, half=half, hTm=hTm, pht=pht):
                                ins = None
                                for bi in range(8):
                                    nt = half * 8 + bi
                                    ins = e.tensor_copy(
                                        out=hTm[:, nt * 128:(nt + 1) * 128],
                                        in_=pht[:, bi * 128:(bi + 1) * 128])
                                return ins
                            evh_ids[kg] = seq.stage("vector", evh,
                                                    after=[id_trh])

                    if "thr" in tp and g == 2:
                        def tthr(e):
                            return e.dma_start(out=tp["thr"][:],
                                               in_=s_pk[:])
                        seq.dma("sync", tthr, sm_o, 1)

                    htscope.close()
                    p_o = [gctx.enter_context(nc.psum_tensor(
                        f"p_o_{g}_{i}", [128, 512], F32)) for i in range(8)]
                    to_used = {}
                    ev2_ids = {}
                    for dh in range(2):
                        for ntq in range(NT // 4):
                            buf = s_to0 if ntq % 2 == 0 else s_to1

                            def ldo(e, ntq=ntq, dh=dh, buf=buf):
                                return e.dma_start(
                                    out=buf[:, 0:4096].rearrange(
                                        "p (a c) -> p a c", a=4),
                                    in_=t8out[ntq * 512:(ntq + 1) * 512,
                                              dh * 1024:(dh + 1) * 1024]
                                    .rearrange("(a p) c -> p a c", p=128))
                            ldep = to_used.get((dh, ntq - 2), None)
                            if dh == 0 and ntq == 0:
                                ldep = ts_used[NCH - 1]
                            seq.dma("gpsimd", ldo, sm_t, 1, dep=ldep)
                            tv = sm_t.v

                            def mm2(e, dh=dh, ntq=ntq, buf=buf):
                                ins = None
                                for m in range(MT):
                                    hTm = s_za[m][:].bitcast(BF16)
                                    for a in range(4):
                                        nt = ntq * 4 + a
                                        first = (ntq == 0 and a == 0)
                                        last = (ntq == NT // 4 - 1 and a == 3)
                                        for dq in range(2):
                                            ins = e.matmul(
                                                p_o[m * 2 + dq][:],
                                                hTm[:, nt * 128:
                                                    (nt + 1) * 128],
                                                buf[:, a * 1024 + dq * 512:
                                                    a * 1024 + (dq + 1) * 512],
                                                start=first, stop=last)
                                return ins
                            to_used[(dh, ntq)] = seq.stage(
                                "tensor", mm2, waits=((sm_t.s, tv),))

                        def ev2(e, dh=dh):
                            ins = None
                            for m in range(MT):
                                r2 = r2m(m)
                                for dq in range(2):
                                    do_ = dh * 1024 + dq * 512
                                    ins = e.scalar_tensor_tensor(
                                        out=r2[:, do_:do_ + 512],
                                        in0=p_o[m * 2 + dq][:],
                                        scalar=s_scal[:, 1:2],
                                        in1=s_x1k[:, m * D + do_:
                                                  m * D + do_ + 512],
                                        op0=ALU.mult, op1=ALU.add)
                            return ins
                        ev2_ids[dh] = seq.stage(
                            "vector", ev2, waits=((sm_1.s, x1kv),))

                    zBf = s_zoneB[:].bitcast(F32)
                    s_lnd = s_zoneA[:].bitcast(F32)[:, 0:D]

                    def r3q(m):
                        return zBf[:, (m % 2) * 2 * D:(m % 2) * 2 * D + D]

                    def o2q(m):
                        return zBf[:, (m % 2) * 2 * D + D:
                                   (m % 2) * 2 * D + 2 * D]

                    ln_sov = sm_o.v
                    sov = {}
                    lni_ids = {}
                    ln2sq_ids = {}
                    lnap_ids = {}
                    for m in range(MT):
                        def lni(e, m=m):
                            e.tensor_tensor(out=r3q(m), in0=r2m(m),
                                            in1=s_bout[:], op=ALU.add)
                            vjk(e)
                            return e.tensor_scalar(
                                out=s_lnd, in0=r3q(m),
                                scalar1=1.0, scalar2=None, op0=ALU.mult,
                                op1=ALU.add,
                                accum_out=s_pk[:, m * 16:m * 16 + 1])
                        lni_ids[m] = seq.stage(
                            "vector", lni, waits=((sm_o.s, ln_sov),),
                            after=[ln2sq_ids.get(m - 2, 0),
                                   ev2_ids[1]])

                        def lnsq(e, m=m):
                            return e.activation(
                                out=s_lnd, in_=r3q(m), func=AF.Square,
                                accum_out=s_pk[:, m * 16 + 1:m * 16 + 2])
                        id_lnsq = seq.stage("scalar", lnsq,
                                            after=[lni_ids[m]])

                        def lnab(e, m=m):
                            P = s_pk
                            mu = P[:, m * 16:m * 16 + 1]
                            sq = P[:, m * 16 + 1:m * 16 + 2]
                            va = P[:, m * 16 + 2:m * 16 + 3]
                            tq_ = P[:, m * 16 + 8:m * 16 + 9]
                            e.tensor_scalar(out=mu, in0=mu, scalar1=1.0 / D,
                                            scalar2=None, op0=ALU.mult)
                            e.tensor_scalar(out=va, in0=sq, scalar1=1.0 / D,
                                            scalar2=LNEPS, op0=ALU.mult,
                                            op1=ALU.add)
                            e.tensor_tensor(out=tq_, in0=mu, in1=mu,
                                            op=ALU.mult)
                            vjk(e)
                            return e.tensor_tensor(out=va, in0=va, in1=tq_,
                                                   op=ALU.subtract)
                        id_lnab = seq.stage("vector", lnab, after=[id_lnsq])

                        def lnrs(e, m=m):
                            P = s_pk
                            va = P[:, m * 16 + 2:m * 16 + 3]
                            rs = P[:, m * 16 + 3:m * 16 + 4]
                            e.activation(out=rs, in_=va, func=AF.Ln)
                            sjk(e)
                            return e.activation(out=rs, in_=rs, func=AF.Exp,
                                                scale=-0.5)
                        id_lnrs = seq.stage("scalar", lnrs, after=[id_lnab])

                        def lnap(e, m=m):
                            P = s_pk
                            mu = P[:, m * 16:m * 16 + 1]
                            rs = P[:, m * 16 + 3:m * 16 + 4]
                            e.tensor_scalar(out=r3q(m), in0=r3q(m),
                                            scalar1=mu, scalar2=rs,
                                            op0=ALU.subtract, op1=ALU.mult)
                            vjk(e)
                            e.tensor_tensor(out=r3q(m), in0=r3q(m),
                                            in1=s_gam[:], op=ALU.mult)
                            vjk(e)
                            e.tensor_tensor(out=r3q(m), in0=r3q(m),
                                            in1=s_bet[:], op=ALU.add)
                            vjk(e)
                            e.tensor_tensor(out=r3q(m), in0=r3q(m),
                                            in1=s_x1k[:, m * D:(m + 1) * D],
                                            op=ALU.add)
                            vjk(e)
                            return e.tensor_scalar(
                                out=s_lnd, in0=r3q(m), scalar1=1.0,
                                scalar2=None, op0=ALU.mult, op1=ALU.add,
                                accum_out=s_pk[:, m * 16 + 4:m * 16 + 5])
                        id_lnap = seq.stage("vector", lnap, after=[id_lnrs])
                        lnap_ids[m] = id_lnap

                        def ln2sq(e, m=m):
                            return e.activation(
                                out=s_lnd, in_=r3q(m), func=AF.Square,
                                accum_out=s_pk[:, m * 16 + 5:m * 16 + 6])
                        ln2sq_ids[m] = seq.stage("scalar", ln2sq,
                                                 after=[id_lnap])

                        def ln2ab(e, m=m):
                            P = s_pk
                            mu = P[:, m * 16 + 4:m * 16 + 5]
                            sq = P[:, m * 16 + 5:m * 16 + 6]
                            va = P[:, m * 16 + 6:m * 16 + 7]
                            tq_ = P[:, m * 16 + 9:m * 16 + 10]
                            e.tensor_scalar(out=mu, in0=mu, scalar1=1.0 / D,
                                            scalar2=None, op0=ALU.mult)
                            e.tensor_scalar(out=va, in0=sq, scalar1=1.0 / D,
                                            scalar2=LNEPS, op0=ALU.mult,
                                            op1=ALU.add)
                            e.tensor_tensor(out=tq_, in0=mu, in1=mu,
                                            op=ALU.mult)
                            vjk(e)
                            return e.tensor_tensor(out=va, in0=va, in1=tq_,
                                                   op=ALU.subtract)
                        id_ln2ab = seq.stage("vector", ln2ab,
                                             after=[ln2sq_ids[m]])

                        def ln2rs(e, m=m):
                            P = s_pk
                            va = P[:, m * 16 + 6:m * 16 + 7]
                            rs = P[:, m * 16 + 7:m * 16 + 8]
                            e.activation(out=rs, in_=va, func=AF.Ln)
                            sjk(e)
                            return e.activation(out=rs, in_=rs, func=AF.Exp,
                                                scale=-0.5)
                        id_ln2rs = seq.stage("scalar", ln2rs,
                                             after=[id_ln2ab])

                        def ln2ap(e, m=m):
                            P = s_pk
                            mu = P[:, m * 16 + 4:m * 16 + 5]
                            rs = P[:, m * 16 + 7:m * 16 + 8]
                            return e.tensor_scalar(out=o2q(m), in0=r3q(m),
                                                   scalar1=mu, scalar2=rs,
                                                   op0=ALU.subtract,
                                                   op1=ALU.mult)
                        id_ln2ap = seq.stage(
                            "vector", ln2ap, after=[id_ln2rs],
                            waits=((sm_o.s, sov.get(m - 2, ln_sov)),))

                        def sto(e, g=g, m=m):
                            r0 = g * 512 + m * 128
                            return e.dma_start(out=out[r0:r0 + 128, :],
                                               in_=o2q(m))
                        seq.dma("sync", sto, sm_o, 1, dep=id_ln2ap)
                        sov[m] = sm_o.v

                    g_prev_psum = ev2_ids[1]
                    g_prev_za = lni_ids[3]
                    g_prev_mm2 = to_used[(1, NT // 4 - 1)]
                    g_prev_lnap = lnap_ids[3]
                    g_prev_sov = {2: sov[2], 3: sov[3]}
                    seq.flush()
                trunc_point(f"g{g}")

            def final(e):
                e.wait_ge(sm_o.s, sm_o.v)
                return None
            seq.stage("sync", final, inc=False)
            seq.flush()

    return nc, dr, tp



def stage_inputs(inputs):
    x = np.ascontiguousarray(np.asarray(inputs["L0_attn_x"], dtype=np.float32))
    Wx = np.asarray(inputs["L0_attn_Wxproj"], dtype=np.float32)
    Wi = np.asarray(inputs["L0_attn_Winproj"], dtype=np.float32)
    Wo = np.asarray(inputs["L0_attn_Woutproj"], dtype=np.float32)
    cwn = np.asarray(inputs["L0_attn_conv_w"], dtype=np.float32)
    cbn = np.asarray(inputs["L0_attn_conv_b"], dtype=np.float32)
    Alog = np.asarray(inputs["L0_attn_A_log"], dtype=np.float32)
    Dv = np.asarray(inputs["L0_attn_D"], dtype=np.float32)
    Wsyn = np.asarray(inputs["L0_ffn_W_syn"], dtype=np.float32)
    bsyn = np.asarray(inputs["L0_ffn_b_syn"], dtype=np.float32)
    Wout = np.asarray(inputs["L0_ffn_W_out"], dtype=np.float32)
    bout = np.asarray(inputs["L0_ffn_b_out"], dtype=np.float32)
    gam = np.asarray(inputs["L0_ffn_gamma"], dtype=np.float32)
    bet = np.asarray(inputs["L0_ffn_beta"], dtype=np.float32)

    shared = {
        "wxT": np.ascontiguousarray(Wx.T),
        "wiT": np.ascontiguousarray(Wi.T),
        "woT": np.ascontiguousarray(Wo.T),
        "cwv": np.ascontiguousarray(
            cwn[:, 0, :].reshape(DT, 128, 4).transpose(1, 0, 2).reshape(128, DT * 4)),
        "cbv": np.ascontiguousarray(cbn.reshape(DT, 128).T),
        "alog": np.ascontiguousarray(Alog.reshape(ST, 128).T),
        "dvec": np.ascontiguousarray(Dv.reshape(DT, 128).T),
        "wsynT": np.ascontiguousarray(Wsyn.T),
        "woutT": np.ascontiguousarray(Wout.T),
        "bsyn_r": np.ascontiguousarray(np.broadcast_to(bsyn[None, :], (128, N))),
        "bout_r": np.ascontiguousarray(np.broadcast_to(bout[None, :], (128, D))),
        "gam_r": np.ascontiguousarray(np.broadcast_to(gam[None, :], (128, D))),
        "bet_r": np.ascontiguousarray(np.broadcast_to(bet[None, :], (128, D))),
    }

    in_maps = []
    for c in range(8):
        seqi, idx = divmod(c, 4)
        start = idx * CT
        xo = np.zeros((CT + 4, D), np.float32)
        lo = start - 4
        src0 = max(lo, 0)
        xo[src0 - lo:, :] = x[seqi, src0:start + CT, :]
        xp = np.zeros((PRE + 4, D), np.float32)
        lo2 = start - PRE - 4
        src0 = max(lo2, 0)
        if start > 0:
            xp[src0 - lo2:, :] = x[seqi, src0:start, :]
        m = dict(shared)
        m["x_own"] = np.ascontiguousarray(xo.T)
        m["x_pre"] = np.ascontiguousarray(xp.T)
        in_maps.append(m)
    return in_maps


_NC_CACHE = {}


def kernel(**inputs):
    key = ()
    if key not in _NC_CACHE:
        _NC_CACHE[key] = build_nc()
    nc, dr, tp = _NC_CACHE[key]
    in_maps = stage_inputs(inputs)
    res = run_bass_kernel_spmd(nc, in_maps, core_ids=list(range(8)))
    outs = [np.asarray(res.results[c]["out"], dtype=np.float32)
            for c in range(8)]
    return np.concatenate(outs, axis=0).reshape(B_, L_, D)


# version 22 (final)
# speedup vs baseline: 1.0467x; 1.0069x over previous
import numpy as np
from contextlib import ExitStack

import concourse.bass as bass
import concourse.mybir as mybir
from concourse.bass_utils import run_bass_kernel_spmd

F32 = mybir.dt.float32
BF16 = mybir.dt.bfloat16
FP8 = mybir.dt.float8e4
I32 = mybir.dt.int32
AF = mybir.ActivationFunctionType
ALU = mybir.AluOpType
AX = mybir.AxisListType

B_, L_, D, S, N = 2, 8192, 2048, 512, 8192
CT, PRE = 2048, 6144
TOPK = 819
LNEPS = 1e-5
NG, G = 4, 512
MT, DT, ST, NT = 4, 16, 4, 64
NPC, NLC, NCH = 12, 4, 16
AIM = float(TOPK) + 3.0
SIG_SPAN = 30.0
BR_LO = 0.8
BR_HI = 2.0
LAD0 = 1.38
LAD1 = 1.15
NPROBE = 6
N_FP = 4


ENGS = ("tensor", "vector", "scalar", "gpsimd", "sync")


class Seq:

    def __init__(self, nc, ctx):
        self.nc = nc
        self.chains = {e: ctx.enter_context(nc.semaphore(f"ch_{e}"))
                       for e in ENGS}
        self.counts = {e: 0 for e in ENGS}
        self.where = {}
        self.n = 0
        self.stages = []

    def stage(self, eng, fn, dep=None, after=None, waits=(), inc=True,
              dma=None):
        if after is None:
            after = [self.n if dep is None else dep]
        elif dep is not None:
            after = list(after) + [dep]
        cw = {}
        for sid in after:
            if not sid:
                continue
            deng, didx = self.where[sid]
            cw[deng] = max(cw.get(deng, 0), didx)
        chw = tuple((self.chains[k], v) for k, v in cw.items())
        if dma is not None:
            dma[0].v += 16 * dma[1]
        self.stages.append((eng, chw, tuple(waits), fn, inc, dma))
        if inc:
            self.counts[eng] += 1
            self.n += 1
            self.where[self.n] = (eng, self.counts[eng])
            return self.n
        return None

    def dma(self, eng, fn, sem, n, dep=None, after=None, waits=()):
        return self.stage(eng, fn, dep=dep, after=after, waits=waits,
                          inc=False, dma=(sem, n))

    def flush(self):
        stages = self.stages
        self.stages = []
        if not stages:
            return
        nc = self.nc
        used = sorted({s[0] for s in stages})
        with nc.Block() as blk:
            for name in used:
                def mk(name):
                    def run(e):
                        for eng, chw, waits, fn, inc, dma in stages:
                            if eng != name:
                                continue
                            for sem, val in chw:
                                e.wait_ge(sem, val)
                            for sem, val in waits:
                                e.wait_ge(sem, val)
                            r = fn(e)
                            if dma is not None:
                                insts = r if isinstance(r, (list, tuple)) else \
                                    ([] if r is None else [r])
                                assert len(insts) == dma[1], \
                                    f"dma stage returned {len(insts)} != {dma[1]}"
                                for ins in insts:
                                    ins.then_inc(dma[0].s, 16)
                            elif inc:
                                assert r is not None and \
                                    not isinstance(r, (list, tuple)), \
                                    "stage must return its last instruction"
                                r.then_inc(self.chains[name], 1)
                    return run
                getattr(blk, name)(mk(name))


class Dsem:
    def __init__(self, ctx, nc, name):
        self.s = ctx.enter_context(nc.semaphore(name))
        self.v = 0


class _TruncDone(Exception):
    pass


def build_nc(debug_taps=(), trunc=None):
    nc = bass.Bass("TRN2", target_bir_lowering=False, debug=False,
                   detect_race_conditions=False)
    taps = set(debug_taps)
    dr = {}

    def din(name, shape, dt=F32):
        dr[name] = nc.dram_tensor(name, shape, dt, kind="ExternalInput").ap()
        return dr[name]

    def dint(name, shape, dt):
        dr[name] = nc.dram_tensor(name, shape, dt, kind="Internal").ap()
        return dr[name]

    def dout(name, shape, dt=F32):
        dr[name] = nc.dram_tensor(name, shape, dt, kind="ExternalOutput").ap()
        return dr[name]

    x_own = din("x_own", [D, CT + 4])
    x_pre = din("x_pre", [D, PRE + 4])
    wxT = din("wxT", [D, 1025])
    wiT = din("wiT", [D, S])
    woT = din("woT", [S, D])
    din("cwv", [128, DT * 4])
    din("cbv", [128, DT])
    din("alog", [128, ST])
    din("dvec", [128, DT])
    wsynT = din("wsynT", [D, N])
    woutT = din("woutT", [N, D])
    din("bsyn_r", [128, N])
    din("bout_r", [128, D])
    din("gam_r", [128, D])
    din("bet_r", [128, D])

    t8syn = dint("t8syn", [D, N], FP8)
    t8out = dint("t8out", [N, D], FP8)
    x1fm_h = dint("x1fm_h", [NG, D, G], BF16)
    x1tk_h = dint("x1tk_h", [CT, D], BF16)

    out = dout("out", [CT, D])
    tp = {}
    for tname, shp, tdt in [("x1", [CT, D], BF16), ("z", [G, N], F32),
                            ("thr", [128, 64], F32), ("quant", [128, 16], F32),
                            ("carry", [128, ST], F32), ("dbg", [128, 64], F32)]:
        if tname in taps:
            tp[tname] = dout("tap_" + tname, shp, tdt)

    class _Suppress:
        def __enter__(self):
            return self

        def __exit__(self, et, ev, tb):
            return et is _TruncDone

    ctx = ExitStack()
    with ctx:
        ctx.enter_context(_Suppress())

        def sb(name, shape, dt=F32):
            return ctx.enter_context(nc.sbuf_tensor(name, shape, dt))

        seq = Seq(nc, ctx)

        def trunc_point(name):
            if trunc != name:
                return
            def tf(e):
                return e.dma_start(out=out[0:128, 0:32], in_=s_scal[:])
            seq.dma("sync", tf, sm_o, 1)
            sems = [sm_w, sm_g, sm_x, sm_q, sm_8, sm_t, sm_1, sm_o]
            def tw(e):
                for sm in sems:
                    e.wait_ge(sm.s, sm.v)
                return None
            seq.stage("sync", tw, inc=False)
            seq.flush()
            raise _TruncDone()

        s_bout = sb("s_bout", [128, D], BF16)
        s_gam = sb("s_gam", [128, D], BF16)
        s_bet = sb("s_bet", [128, D], BF16)
        s_ones = sb("s_ones", [128, 128], BF16)
        s_eye = sb("s_eye", [128, 128], BF16)
        s_io8 = sb("s_io8", [128, 8])
        s_scal = sb("s_scal", [128, 32])
        s_jk = sb("s_jk", [128, 8])

        def vjk(e):
            return e.tensor_scalar(out=s_jk[:, 0:1], in0=s_jk[:, 2:3],
                                   scalar1=1.0, scalar2=None, op0=ALU.mult)

        def sjk(e):
            return e.activation(out=s_jk[:, 1:2], in_=s_jk[:, 1:2],
                                func=AF.Copy)

        sm_w = Dsem(ctx, nc, "sm_w")
        sm_g = Dsem(ctx, nc, "sm_g")
        sm_x = Dsem(ctx, nc, "sm_x")
        sm_q = Dsem(ctx, nc, "sm_q")
        sm_8 = Dsem(ctx, nc, "sm_8")
        sm_t = Dsem(ctx, nc, "sm_t")
        sm_1 = Dsem(ctx, nc, "sm_1")
        sm_o = Dsem(ctx, nc, "sm_o")

        wscope = ExitStack()
        s_wx = wscope.enter_context(nc.sbuf_tensor("s_wx", [128, DT * 1024], BF16))
        s_wi = wscope.enter_context(nc.sbuf_tensor("s_wi", [128, DT * S], BF16))
        s_wo = wscope.enter_context(nc.sbuf_tensor("s_wo", [128, ST * D], BF16))
        s_wdrep = wscope.enter_context(
            nc.sbuf_tensor("s_wdrep", [128, DT * 128], BF16))
        s_cw = wscope.enter_context(nc.sbuf_tensor("s_cw", [128, DT * 4], F32))
        s_cb = wscope.enter_context(nc.sbuf_tensor("s_cb", [128, DT], F32))
        s_A = wscope.enter_context(nc.sbuf_tensor("s_A", [128, ST], F32))
        s_D = wscope.enter_context(nc.sbuf_tensor("s_D", [128, DT], F32))
        s_carry = wscope.enter_context(nc.sbuf_tensor("s_carry", [128, ST], F32))
        s_on32 = wscope.enter_context(nc.sbuf_tensor("s_on32", [128, 128], F32))

        def st0_loads(e):
            r = []
            for t, dst in [("cwv", s_cw), ("cbv", s_cb), ("dvec", s_D)]:
                r.append(e.dma_start(out=dst[:], in_=dr[t][:]))
            r.append(e.dma_start(out=s_scal[:, 8:8 + ST], in_=dr["alog"][:]))
            r.append(e.dma_start(
                out=s_scal[:, 16:16 + DT].rearrange("p (t o) -> p t o", o=1),
                in_=wxT[:, 1024:1025].rearrange("(t p) o -> p t o", p=128),
                allow_slow_non_contiguous=True))
            return r
        seq.dma("sync", st0_loads, sm_w, 5)

        def st0_casts(e):
            return [e.dma_start(out=dst[:], in_=dr[t][:])
                    for t, dst in [("bout_r", s_bout), ("gam_r", s_gam),
                                   ("bet_r", s_bet)]]
        seq.dma("gpsimd", st0_casts, sm_g, 3)
        w_small = sm_w.v

        def st0_iota(e):
            e.iota(s_eye[:], pattern=[[-1, 128]], channel_multiplier=1,
                   allow_small_or_imprecise_dtypes=True)
            return e.iota(s_io8[:], pattern=[[1, 8]], channel_multiplier=0,
                          allow_small_or_imprecise_dtypes=True)
        seq.stage("gpsimd", st0_iota)

        def st0_v(e):
            e.memset(s_ones[:], 1.0)
            e.memset(s_on32[:], 1.0)
            e.memset(s_jk[:], 0.0)
            e.memset(s_carry[:], 0.0)
            e.tensor_scalar(out=s_eye[:], in0=s_eye[:], scalar1=0.0,
                            scalar2=None, op0=ALU.is_equal)
            ins = None
            for dt_ in range(DT):
                ins = e.tensor_scalar(
                    out=s_wdrep[:, dt_ * 128:(dt_ + 1) * 128], in0=s_ones[:],
                    scalar1=s_scal[:, 16 + dt_:17 + dt_], scalar2=None,
                    op0=ALU.mult)
            return ins
        seq.stage("vector", st0_v, waits=((sm_w.s, w_small), (sm_g.s, sm_g.v)))

        def st0_s(e):
            return e.activation(out=s_A[:], in_=s_scal[:, 8:8 + ST],
                                func=AF.Exp)
        seq.stage("scalar", st0_s)

        def st0_v2(e):
            return e.tensor_scalar(out=s_A[:], in0=s_A[:], scalar1=-1.0,
                                   scalar2=None, op0=ALU.mult)
        seq.stage("vector", st0_v2)
        seq.flush()
        trunc_point("t0")

        NSL = DT + 16

        def slab_in(i):
            if i < DT:
                return wsynT[i * 128:(i + 1) * 128, :]
            j = i - DT
            return woutT[j * 512:(j + 1) * 512, :].rearrange(
                "(a p) c -> p a c", p=128)

        def slab_out8(i):
            if i < DT:
                return t8syn[i * 128:(i + 1) * 128, :]
            j = i - DT
            return t8out[j * 512:(j + 1) * 512, :].rearrange(
                "(a p) c -> p a c", p=128)

        with ExitStack() as qctx:
            qsb = lambda nm, shp, dt=F32: qctx.enter_context(
                nc.sbuf_tensor(nm, shp, dt))
            s_qa = qsb("s_qa", [128, N])
            s_qb = qsb("s_qb", [128, N])
            s_qi = qsb("s_qi", [128, N], I32)
            s_qo = s_qi[:].bitcast(F32)
            s_q8a = qsb("s_q8a", [128, N], FP8)
            s_q8b = qsb("s_q8b", [128, N], FP8)
            s_abs = qsb("s_abs", [128, 32])
            p_red = qctx.enter_context(nc.psum_tensor("p_red", [128, 32], F32))

            consumed = {}
            for i in range(NSL):
                buf = s_qa if i % 2 == 0 else s_qb
                def ld(e, i=i, buf=buf):
                    dst = buf[:] if i < DT else buf[:].rearrange(
                        "p (a c) -> p a c", a=4)
                    return e.dma_start(out=dst, in_=slab_in(i))
                seq.dma("sync", ld, sm_q, 1, dep=consumed.get(i - 2, 0))
                qv = sm_q.v
                def ab(e, i=i, buf=buf):
                    return e.activation(out=s_qo, in_=buf[:], func=AF.Abs,
                                        accum_out=s_abs[:, i:i + 1])
                consumed[i] = seq.stage("scalar", ab, waits=((sm_q.s, qv),))

            def red_mm(e):
                return e.matmul(p_red[:], s_on32[:], s_abs[:],
                                start=True, stop=True)
            seq.stage("tensor", red_mm)

            def red_v(e):
                e.tensor_reduce(out=s_scal[:, 0:1], in_=p_red[:, 0:DT],
                                axis=AX.X, op=ALU.add)
                e.tensor_reduce(out=s_scal[:, 1:2], in_=p_red[:, DT:NSL],
                                axis=AX.X, op=ALU.add)
                for c in range(2):
                    e.tensor_scalar(out=s_scal[:, c:c + 1],
                                    in0=s_scal[:, c:c + 1],
                                    scalar1=1.0 / (D * N), scalar2=None,
                                    op0=ALU.mult)
                for c in range(2):
                    e.tensor_scalar(out=s_scal[:, 5 + c:6 + c],
                                    in0=s_scal[:, c:c + 1],
                                    scalar1=1e-8, scalar2=None, op0=ALU.add)
                for c in range(2):
                    e.tensor_scalar(out=s_scal[:, 12 + c:13 + c],
                                    in0=s_scal[:, 5 + c:6 + c],
                                    scalar1=0.5, scalar2=None, op0=ALU.mult)
                for c in range(2):
                    e.tensor_scalar(out=s_scal[:, 14 + c:15 + c],
                                    in0=s_scal[:, 5 + c:6 + c],
                                    scalar1=-0.5, scalar2=None, op0=ALU.mult)
                return e.reciprocal(out=s_scal[:, 2:4], in_=s_scal[:, 5:7])
            seq.stage("vector", red_v)

            stored = {}
            for i in range(NSL):
                buf = s_qa if i % 2 == 0 else s_qb
                b8 = s_q8a if i % 2 == 0 else s_q8b
                def ld(e, i=i, buf=buf):
                    dst = buf[:] if i < DT else buf[:].rearrange(
                        "p (a c) -> p a c", a=4)
                    return e.dma_start(out=dst, in_=slab_in(i))
                seq.dma("sync", ld, sm_q, 1, dep=stored.get(i - 2, 0))
                qv = sm_q.v
                def qz(e, i=i, buf=buf, b8=b8):
                    co = 0 if i < DT else 1
                    e.tensor_scalar(out=s_qo, in0=buf[:],
                                    scalar1=s_scal[:, 14 + co:15 + co],
                                    scalar2=None, op0=ALU.is_le)
                    vjk(e)
                    return e.scalar_tensor_tensor(
                        out=b8[:], in0=buf[:],
                        scalar=s_scal[:, 12 + co:13 + co],
                        in1=s_qo, op0=ALU.is_ge, op1=ALU.subtract)
                seq.stage("vector", qz, waits=((sm_q.s, qv),))
                def stq(e, i=i, b8=b8):
                    srcb = b8[:] if i < DT else b8[:].rearrange(
                        "p (a c) -> p a c", a=4)
                    return e.dma_start(out=slab_out8(i), in_=srcb)
                seq.dma("gpsimd", stq, sm_8, 1)
                stored[i] = seq.n
            t8_done = sm_8.v

            def ldwx(e):
                return [
                    e.dma_start(out=s_qa[:, 0:N].rearrange(
                        "p (t c) -> p t c", t=8),
                        in_=wxT[0:1024, 0:1024].rearrange(
                            "(t p) c -> p t c", p=128)),
                    e.dma_start(out=s_qb[:, 0:N].rearrange(
                        "p (t c) -> p t c", t=8),
                        in_=wxT[1024:2048, 0:1024].rearrange(
                            "(t p) c -> p t c", p=128)),
                ]
            seq.dma("sync", ldwx, sm_q, 2)
            qv = sm_q.v

            def cvwx(e):
                e.tensor_copy(out=s_wx[:, 0:8 * 1024], in_=s_qa[:, 0:N])
                return e.tensor_copy(out=s_wx[:, 8 * 1024:DT * 1024],
                                     in_=s_qb[:, 0:N])
            seq.stage("vector", cvwx, waits=((sm_q.s, qv),))

            def ldwio(e):
                return [
                    e.dma_start(out=s_qa[:, 0:DT * S].rearrange(
                        "p (t c) -> p t c", t=DT),
                        in_=wiT.rearrange("(t p) c -> p t c", p=128)),
                    e.dma_start(out=s_qb[:, 0:ST * D].rearrange(
                        "p (t c) -> p t c", t=ST),
                        in_=woT.rearrange("(t p) c -> p t c", p=128)),
                ]
            seq.dma("sync", ldwio, sm_q, 2)
            qv = sm_q.v

            def cvwio(e):
                e.tensor_copy(out=s_wi[:], in_=s_qa[:, 0:DT * S])
                return e.tensor_copy(out=s_wo[:], in_=s_qb[:, 0:ST * D])
            quant_tail = seq.stage("vector", cvwio, waits=((sm_q.s, qv),))

            if "quant" in tp:
                def tq(e):
                    return e.dma_start(out=tp["quant"][:, 0:4], in_=s_scal[:, 0:4])
                seq.dma("sync", tq, sm_o, 1)
            if trunc == "quant":
                def tdbg(e):
                    return e.dma_start(out=out[128:256, 0:32], in_=s_abs[:])
                seq.dma("sync", tdbg, sm_o, 1)
            seq.flush()
        trunc_point("quant")

        with ExitStack() as mctx:
            msb = lambda nm, shp, dt=F32: mctx.enter_context(
                nc.sbuf_tensor(nm, shp, dt))
            s_xbs = [msb(f"s_xb{i}", [128, DT * 520], BF16) for i in range(2)]
            s_xc = msb("s_xc", [128, DT * 512], BF16)
            s_B = msb("s_B", [128, ST * 512], BF16)
            s_C = msb("s_C", [128, ST * 512], BF16)
            s_dc = msb("s_dc", [128, 512])
            s_dA = msb("s_dA", [128, ST * 512], BF16)
            s_Bu = msb("s_Bu", [128, ST * 512], BF16)
            s_h = msb("s_h", [128, ST * 512])
            s_hm = msb("s_hm", [128, ST * 512], BF16)
            s_mu = msb("s_mu", [128, 512])
            s_va = msb("s_va", [128, 512])
            s_rs = msb("s_rs", [128, 512])
            s_tm = msb("s_tm", [128, 512])
            s_x1f = msb("s_x1f", [128, DT * 512], BF16)
            s_sq = s_x1f
            s_x1t = msb("s_x1t", [128, MT * D], BF16)

            PS = [mctx.enter_context(nc.psum_tensor(f"p_s{i}", [128, 512], F32))
                  for i in range(4)]
            p_mu = mctx.enter_context(nc.psum_tensor("p_mu", [128, 512], F32))
            p_vr = mctx.enter_context(nc.psum_tensor("p_vr", [128, 512], F32))
            p_tb = mctx.enter_context(nc.psum_tensor("p_tb", [128, 1024], BF16))
            p_tb2 = mctx.enter_context(
                nc.psum_tensor("p_tb2", [128, 1024], BF16))

            def chunk_src(i):
                if i < NPC:
                    return x_pre[:, i * 512:i * 512 + 516]
                j = i - NPC
                return x_own[:, j * 512:j * 512 + 516]

            x_used = {}
            pv = {}
            for ci in range(NPC + NLC):
                own = ci >= NPC
                oc = ci - NPC
                cur = {}
                s_xb = s_xbs[ci % 2]

                def ldx(e, ci=ci, s_xb=s_xb):
                    return e.dma_start(
                        out=s_xb[:].rearrange("p (t c) -> p t c",
                                              t=DT)[:, :, 4:520],
                        in_=chunk_src(ci).rearrange("(t p) c -> p t c", p=128))
                seq.dma("gpsimd", ldx, sm_x, 1,
                        dep=x_used.get(ci - 2, quant_tail))
                xv = sm_x.v

                def conv(e, s_xb=s_xb):
                    for da in range(0, DT, 2):
                        pair = (da, da + 1)
                        for dt_ in pair:
                            xo = dt_ * 520 + 5
                            e.tensor_scalar(
                                out=s_xc[:, dt_ * 512:(dt_ + 1) * 512],
                                in0=s_xb[:, xo:xo + 512],
                                scalar1=s_cw[:, dt_ * 4:dt_ * 4 + 1],
                                scalar2=s_cb[:, dt_:dt_ + 1],
                                op0=ALU.mult, op1=ALU.add)
                        for k in range(1, 4):
                            for dt_ in pair:
                                xo = dt_ * 520 + 5
                                acc = s_xc[:, dt_ * 512:(dt_ + 1) * 512]
                                ins = e.scalar_tensor_tensor(
                                    out=acc,
                                    in0=s_xb[:, xo + k:xo + k + 512],
                                    scalar=s_cw[:, dt_ * 4 + k:
                                                 dt_ * 4 + k + 1],
                                    in1=acc, op0=ALU.mult, op1=ALU.add)
                    return ins
                cur["conv"] = seq.stage(
                    "vector", conv, waits=((sm_x.s, xv),),
                    after=[pv.get("projXS", quant_tail), pv.get("evq3", 0),
                           pv.get("ln1c", 0)])
                if ci == 0:
                    trunc_point("mc")

                def projB(e, s_xb=s_xb):
                    ins = None
                    for j in range(ST):
                        for dt_ in range(DT):
                            ins = e.matmul(
                                PS[j][:],
                                s_wx[:, dt_ * 1024 + j * 128:
                                     dt_ * 1024 + (j + 1) * 128],
                                s_xb[:, dt_ * 520 + 8:dt_ * 520 + 520],
                                start=(dt_ == 0), stop=(dt_ == DT - 1))
                    return ins
                cur["projB"] = seq.stage(
                    "tensor", projB, waits=((sm_x.s, xv),),
                    after=[pv.get("evXS", quant_tail), pv.get("evC", 0),
                           pv.get("evq3", 0)])
                if ci == 0:
                    trunc_point("mpb")
                def projD(e, s_xb=s_xb):
                    ins = None
                    for dt_ in range(DT):
                        ins = e.matmul(
                            p_mu[:], s_wdrep[:, dt_ * 128:(dt_ + 1) * 128],
                            s_xb[:, dt_ * 520 + 8:dt_ * 520 + 520],
                            start=(dt_ == 0), stop=(dt_ == DT - 1))
                    return ins
                cur["projD"] = seq.stage(
                    "tensor", projD, waits=((sm_x.s, xv),),
                    after=[pv.get("sp1", quant_tail), pv.get("ln1a", 0)])

                def sp1(e):
                    e.activation(out=s_tm[:], in_=p_mu[:], func=AF.Abs)
                    e.activation(out=s_va[:], in_=p_mu[:], func=AF.Relu)
                    e.activation(out=s_tm[:], in_=s_tm[:], func=AF.Exp,
                                 scale=-1.0)
                    sjk(e)
                    return e.activation(out=s_tm[:], in_=s_tm[:], func=AF.Ln,
                                        bias=1.0)
                cur["sp1"] = seq.stage(
                    "scalar", sp1,
                    after=[cur["projD"], pv.get("ln1a", 0),
                           pv.get("sp2", 0)])

                def sp2(e):
                    return e.tensor_tensor(out=s_dc[:], in0=s_va[:],
                                           in1=s_tm[:], op=ALU.add)
                cur["sp2"] = seq.stage("vector", sp2, after=[cur["sp1"]])

                def dax(e):
                    ins = None
                    for j in range(ST):
                        ins = e.activation(
                            out=s_dA[:, j * 512:(j + 1) * 512], in_=s_dc[:],
                            func=AF.Exp, scale=s_A[:, j:j + 1])
                    return ins
                cur["dax"] = seq.stage("scalar", dax, after=[cur["sp2"]])

                def evB(e):
                    ins = None
                    for j in range(ST):
                        ins = e.tensor_copy(out=s_B[:, j * 512:(j + 1) * 512],
                                            in_=PS[j][:])
                    return ins
                cur["evB"] = seq.stage(
                    "vector", evB,
                    after=[cur["projB"], pv.get("buscan", 0)])
                if ci == 0:
                    trunc_point("mb")

                def projXS(e):
                    ins = None
                    for j in range(ST):
                        for dt_ in range(DT):
                            ins = e.matmul(
                                PS[j][:],
                                s_wi[:, dt_ * 512 + j * 128:
                                     dt_ * 512 + (j + 1) * 128],
                                s_xc[:, dt_ * 512:(dt_ + 1) * 512],
                                start=(dt_ == 0), stop=(dt_ == DT - 1))
                    return ins
                cur["projXS"] = seq.stage(
                    "tensor", projXS, after=[cur["conv"], cur["evB"]])

                def evXS(e):
                    ins = None
                    for j in range(ST):
                        ins = e.tensor_tensor(
                            out=s_Bu[:, j * 512:(j + 1) * 512], in0=PS[j][:],
                            in1=s_B[:, j * 512:(j + 1) * 512], op=ALU.mult)
                    return ins
                cur["evXS"] = seq.stage(
                    "vector", evXS,
                    after=[cur["projXS"], pv.get("buscan", 0)])

                if own:
                    def projC(e, s_xb=s_xb):
                        ins = None
                        for j in range(ST):
                            for dt_ in range(DT):
                                ins = e.matmul(
                                    PS[j][:],
                                    s_wx[:, dt_ * 1024 + 512 + j * 128:
                                         dt_ * 1024 + 512 + (j + 1) * 128],
                                    s_xb[:, dt_ * 520 + 8:dt_ * 520 + 520],
                                    start=(dt_ == 0), stop=(dt_ == DT - 1))
                        return ins
                    cur["projC"] = seq.stage("tensor", projC,
                                             after=[cur["evXS"]])

                    def evC(e):
                        ins = None
                        for j in range(ST):
                            ins = e.tensor_copy(
                                out=s_C[:, j * 512:(j + 1) * 512],
                                in_=PS[j][:])
                        return ins
                    cur["evC"] = seq.stage(
                        "vector", evC,
                        after=[cur["projC"], pv.get("buscan", 0)])


                def buscan(e, own=own):
                    for j in range(ST):
                        o = j * 512
                        e.tensor_tensor(out=s_Bu[:, o:o + 512],
                                        in0=s_Bu[:, o:o + 512], in1=s_dc[:],
                                        op=ALU.mult)
                    for j in range(ST):
                        o = j * 512
                        e.tensor_tensor_scan(
                            out=s_h[:, o:o + 512], data0=s_dA[:, o:o + 512],
                            data1=s_Bu[:, o:o + 512],
                            initial=s_carry[:, j:j + 1],
                            op0=ALU.mult, op1=ALU.add)
                    for j in range(ST):
                        o = j * 512
                        ins = e.tensor_copy(out=s_carry[:, j:j + 1],
                                            in_=s_h[:, o + 511:o + 512])
                    if own:
                        for j in range(ST):
                            o = j * 512
                            ins = e.tensor_tensor(out=s_hm[:, o:o + 512],
                                                  in0=s_h[:, o:o + 512],
                                                  in1=s_C[:, o:o + 512],
                                                  op=ALU.mult)
                    return ins
                bus_after = [cur["dax"], cur["evB"], cur["evXS"]]
                if own:
                    bus_after.append(cur["evC"])
                cur["buscan"] = seq.stage("vector", buscan,
                                          after=bus_after)
                if ci == 0:
                    trunc_point("ms")
                if ci == 2:
                    trunc_point("ms3")

                if not own:
                    x_used[ci] = cur["buscan"]
                    pv = cur
                    continue

                prev_evq = cur["evC"]
                for quad in range(4):
                    def opq(e, quad=quad):
                        ins = None
                        for dt_ in range(quad * 4, quad * 4 + 4):
                            pb = PS[dt_ % 4]
                            for j in range(ST):
                                ins = e.matmul(
                                    pb[:],
                                    s_wo[:, j * D + dt_ * 128:
                                         j * D + (dt_ + 1) * 128],
                                    s_hm[:, j * 512:(j + 1) * 512],
                                    start=(j == 0), stop=(j == ST - 1))
                        return ins
                    cur[f"opq{quad}"] = seq.stage(
                        "tensor", opq,
                        after=[cur["buscan"], prev_evq])

                    def evq(e, quad=quad, s_xb=s_xb):
                        for dt_ in range(quad * 4, quad * 4 + 4):
                            pb = PS[dt_ % 4]
                            xo = dt_ * 520 + 8
                            e.scalar_tensor_tensor(
                                out=s_xb[:, xo:xo + 512], in0=pb[:], scalar=1.0,
                                in1=s_xb[:, xo:xo + 512],
                                op0=ALU.mult, op1=ALU.add)
                        ins = None
                        for dt_ in range(quad * 4, quad * 4 + 4):
                            xo = dt_ * 520 + 8
                            ins = e.scalar_tensor_tensor(
                                out=s_xb[:, xo:xo + 512],
                                in0=s_xc[:, dt_ * 512:(dt_ + 1) * 512],
                                scalar=s_D[:, dt_:dt_ + 1],
                                in1=s_xb[:, xo:xo + 512],
                                op0=ALU.mult, op1=ALU.add)
                        return ins
                    prev_evq = seq.stage("vector", evq,
                                         after=[cur[f"opq{quad}"]])
                    cur[f"evq{quad}"] = prev_evq

                def sqs(e, s_xb=s_xb):
                    ins = None
                    for dt_ in range(DT):
                        xo = dt_ * 520 + 8
                        ins = e.activation(
                            out=s_sq[:, dt_ * 512:(dt_ + 1) * 512],
                            in_=s_xb[:, xo:xo + 512], func=AF.Square)
                    return ins
                cur["sqs"] = seq.stage(
                    "scalar", sqs,
                    after=[cur["evq3"], pv.get("evt7", 0)],
                    waits=((sm_1.s, sm_1.v),))

                def stat_mm(e, s_xb=s_xb):
                    ins = None
                    for dt_ in range(DT):
                        xo = dt_ * 520 + 8
                        e.matmul(p_mu[:], s_ones[:], s_xb[:, xo:xo + 512],
                                 start=(dt_ == 0), stop=(dt_ == DT - 1))
                    for dt_ in range(DT):
                        ins = e.matmul(p_vr[:], s_ones[:],
                                       s_sq[:, dt_ * 512:(dt_ + 1) * 512],
                                       start=(dt_ == 0), stop=(dt_ == DT - 1))
                    return ins
                cur["stat_mm"] = seq.stage("tensor", stat_mm,
                                           after=[cur["sqs"]])

                def ln1a(e):
                    e.tensor_scalar(out=s_mu[:], in0=p_mu[:], scalar1=1.0 / D,
                                    scalar2=None, op0=ALU.mult)
                    e.tensor_scalar(out=s_va[:], in0=p_vr[:], scalar1=1.0 / D,
                                    scalar2=LNEPS, op0=ALU.mult, op1=ALU.add)
                    e.tensor_tensor(out=s_tm[:], in0=s_mu[:], in1=s_mu[:],
                                    op=ALU.mult)
                    vjk(e)
                    return e.tensor_tensor(out=s_va[:], in0=s_va[:],
                                           in1=s_tm[:], op=ALU.subtract)
                cur["ln1a"] = seq.stage("vector", ln1a,
                                        after=[cur["stat_mm"]])

                def ln1b(e):
                    e.activation(out=s_rs[:], in_=s_va[:], func=AF.Ln)
                    sjk(e)
                    return e.activation(out=s_rs[:], in_=s_rs[:], func=AF.Exp,
                                        scale=-0.5)
                cur["ln1b"] = seq.stage("scalar", ln1b, after=[cur["ln1a"]])

                def ln1c(e, s_xb=s_xb):
                    ins = None
                    for da in range(0, DT, 2):
                        for dt_ in (da, da + 1):
                            xo = dt_ * 520 + 8
                            o = dt_ * 512
                            e.tensor_tensor(out=s_x1f[:, o:o + 512],
                                            in0=s_xb[:, xo:xo + 512],
                                            in1=s_mu[:], op=ALU.subtract)
                        for dt_ in (da, da + 1):
                            o = dt_ * 512
                            ins = e.tensor_tensor(out=s_x1f[:, o:o + 512],
                                                  in0=s_x1f[:, o:o + 512],
                                                  in1=s_rs[:], op=ALU.mult)
                    return ins
                cur["ln1c"] = seq.stage("vector", ln1c, after=[cur["ln1b"]])
                x_used[ci] = cur["ln1c"]
                if ci == NPC:
                    trunc_point("ml")

                evt_ids = {-2: pv.get("evt6", 0), -1: pv.get("evt7", 0)}
                for m in range(MT):
                    for half in range(2):
                        k = m * 2 + half
                        ptb = p_tb if k % 2 == 0 else p_tb2

                        def trx(e, m=m, half=half, ptb=ptb):
                            ins = None
                            for bi in range(8):
                                dt_ = half * 8 + bi
                                ins = e.transpose(
                                    ptb[:, bi * 128:(bi + 1) * 128],
                                    s_x1f[:, dt_ * 512 + m * 128:
                                          dt_ * 512 + (m + 1) * 128],
                                    s_eye[:])
                            return ins
                        tid = seq.stage(
                            "tensor", trx,
                            after=[cur["ln1c"], evt_ids.get(k - 2, 0)])

                        def evt(e, m=m, half=half, ptb=ptb):
                            return e.tensor_copy(
                                out=s_x1t[:, m * D + half * 1024:
                                          m * D + (half + 1) * 1024],
                                in_=ptb[:])
                        w_ = ((sm_1.s, sm_1.v),) if k == 0 else ()
                        evt_ids[k] = seq.stage("vector", evt, after=[tid],
                                               waits=w_)
                cur["evt6"] = evt_ids[6]
                cur["evt7"] = evt_ids[7]

                def stx1(e, oc=oc):
                    r = [
                        e.dma_start(
                            out=x1fm_h[oc, :, :].rearrange(
                                "(t p) c -> p t c", p=128),
                            in_=s_x1f[:].rearrange("p (t c) -> p t c", t=DT)),
                        e.dma_start(
                            out=x1tk_h[oc * 512:(oc + 1) * 512, :].rearrange(
                                "(m p) c -> p m c", p=128),
                            in_=s_x1t[:].rearrange("p (m c) -> p m c", m=MT)),
                    ]
                    if "x1" in tp:
                        r.append(e.dma_start(
                            out=tp["x1"][oc * 512:(oc + 1) * 512, :].rearrange(
                                "(m p) c -> p m c", p=128),
                            in_=s_x1t[:].rearrange("p (m c) -> p m c", m=MT)))
                    return r
                seq.dma("sync", stx1, sm_1, 3 if "x1" in tp else 2)
                pv = cur

            if "carry" in tp:
                def tcar(e):
                    return e.dma_start(out=tp["carry"][:], in_=s_carry[:])
                seq.dma("sync", tcar, sm_o, 1)
            x1_all = sm_1.v
            seq.flush()
        wscope.close()
        trunc_point("mamba")

        with ExitStack() as fctx:
            fsb = lambda nm, shp, dt=F32: fctx.enter_context(
                nc.sbuf_tensor(nm, shp, dt))
            s_za = [fsb(f"s_z{m}", [128, N]) for m in range(MT)]
            s_zoneA = fsb("s_zoneA", [128, N], BF16)
            s_zoneB = fsb("s_zoneB", [128, 2 * N], BF16)
            s_x1k = fsb("s_x1k", [128, MT * D], BF16)
            s_bs = fsb("s_bs", [128, 2 * 512], BF16)
            s_zp = fsb("s_zp", [128, MT * 16])
            s_pk = fsb("s_pk", [128, MT * 16])
            s_w8 = fsb("s_w8", [128, 8])
            s_sg = fsb("s_sg", [128, 8])

            def r2m(m):
                return s_za[m][:].bitcast(BF16)[:, N:N + D]

            def ffn_init(e):
                e.memset(s_pk[:], 0.0)
                return e.memset(s_zp[:], 0.0)
            g_prev_psum = seq.stage("vector", ffn_init)
            g_prev_za = 0
            g_prev_mm2 = 0
            g_prev_lnap = 0
            g_prev_sov = {}
            s_x1g = s_zoneA
            s_hidt = s_zoneA
            s_to1 = s_zoneA
            s_ts = (s_zoneB[:, 0:N], s_zoneB[:, N:2 * N])
            s_wv = s_zoneB[:].bitcast(F32)
            s_to0 = s_zoneB[:, 0:N]
            s_r3 = s_zoneB[:].bitcast(F32)[:, 0:D]
            s_o2 = s_zoneB[:].bitcast(F32)[:, D:2 * D]


            for g in range(NG):
                with ExitStack() as gctx:
                    zscope = ExitStack()
                    p_z = [zscope.enter_context(nc.psum_tensor(
                        f"p_z_{g}_{i}", [128, 512], F32)) for i in range(8)]

                    def ldx1g(e, g=g):
                        return e.dma_start(
                            out=s_x1g[:].rearrange("p (t c) -> p t c", t=DT),
                            in_=x1fm_h[g, :, :].rearrange(
                                "(t p) c -> p t c", p=128))
                    seq.dma("sync", ldx1g, sm_1, 1,
                            dep=g_prev_mm2, waits=((sm_1.s, x1_all),))
                    x1v = sm_1.v

                    def ldx1k(e, g=g):
                        return e.dma_start(
                            out=s_x1k[:].rearrange("p (m c) -> p m c", m=MT),
                            in_=x1tk_h[g * 512:(g + 1) * 512, :].rearrange(
                                "(m p) c -> p m c", p=128))
                    seq.dma("sync", ldx1k, sm_1, 1,
                            dep=g_prev_lnap, waits=((sm_1.s, x1_all),))
                    x1kv = sm_1.v

                    ts_used = {}
                    ev1_ids = {}
                    for nch in range(NCH):
                        buf = s_ts[nch % 2]
                        def ldt(e, nch=nch, buf=buf):
                            return [e.dma_start(
                                out=buf.rearrange("p (t c) -> p t c", t=DT),
                                in_=t8syn[:, nch * 512:(nch + 1) * 512]
                                .rearrange("(t p) c -> p t c", p=128)),
                                e.dma_start(
                                    out=s_bs[:, (nch % 2) * 512:
                                             (nch % 2) * 512 + 512],
                                    in_=dr["bsyn_r"][0:128,
                                                     nch * 512:(nch + 1) * 512])]
                        ldt_sov = g_prev_sov.get(2 + nch % 2, sm_o.v) \
                            if nch <= 1 else sm_o.v
                        seq.dma("gpsimd", ldt, sm_t, 2,
                                dep=ts_used.get(nch - 2, None),
                                waits=((sm_8.s, t8_done),
                                       (sm_o.s, ldt_sov),
                                       (sm_1.s, x1_all)))
                        tv = sm_t.v

                        par = 4 * (nch % 2)

                        def mm1(e, nch=nch, buf=buf, par=par):
                            ins = None
                            for m in range(MT):
                                for dt_ in range(DT):
                                    ins = e.matmul(
                                        p_z[par + m][:],
                                        s_x1g[:, dt_ * 512 + m * 128:
                                              dt_ * 512 + (m + 1) * 128],
                                        buf[:, dt_ * 512:(dt_ + 1) * 512],
                                        start=(dt_ == 0),
                                        stop=(dt_ == DT - 1))
                            return ins
                        w = [(sm_t.s, tv)]
                        if nch == 0:
                            w.append((sm_1.s, x1v))
                        mm1_after = [ev1_ids.get(nch - 2, 0)]
                        if nch <= 1:
                            mm1_after.append(g_prev_psum)
                        ts_used[nch] = seq.stage("tensor", mm1, waits=w,
                                                 after=mm1_after)

                        def ev1(e, nch=nch, par=par):
                            ins = None
                            for m in range(MT):
                                ins = e.scalar_tensor_tensor(
                                    out=s_za[m][:, nch * 512:(nch + 1) * 512],
                                    in0=p_z[par + m][:], scalar=s_scal[:, 0:1],
                                    in1=s_bs[:, (nch % 2) * 512:
                                             (nch % 2) * 512 + 512],
                                    op0=ALU.mult, op1=ALU.add,
                                    accum_out=s_zp[:, m * 16 + nch:
                                                   m * 16 + nch + 1])
                            return ins
                        ev1_after = [ts_used[nch]]
                        if nch == 0:
                            ev1_after.append(g_prev_za)
                        ev1_ids[nch] = seq.stage("vector", ev1,
                                                 after=ev1_after)

                    if "z" in tp and g == 2:
                        def tz(e):
                            return [e.dma_start(
                                out=tp["z"][m * 128:(m + 1) * 128, :],
                                in_=s_za[m][:]) for m in range(MT)]
                        seq.dma("sync", tz, sm_o, MT)

                    zscope.close()
                    htscope = ExitStack()
                    p_ht = htscope.enter_context(
                        nc.psum_tensor(f"p_ht_{g}", [128, 1024], BF16))
                    p_ht2 = htscope.enter_context(
                        nc.psum_tensor(f"p_ht2_{g}", [128, 1024], BF16))

                    def col(m, j):
                        return s_pk[:, m * 16 + j:m * 16 + j + 1]

                    def tk0a(e):
                        ins = None
                        for m in range(MT):
                            ins = e.activation(out=s_zoneA[:], in_=s_za[m][:],
                                               func=AF.Square,
                                               accum_out=col(m, 13))
                        return ins
                    id_tk0a = seq.stage("scalar", tk0a,
                                        after=[ev1_ids[NCH - 1]])

                    def tk0b(e):
                        for m in range(MT):
                            e.tensor_reduce(out=col(m, 0),
                                            in_=s_zp[:, m * 16:m * 16 + 16],
                                            axis=AX.X, op=ALU.add)
                        for m in range(MT):
                            e.tensor_scalar(out=col(m, 0), in0=col(m, 0),
                                            scalar1=1.0 / N, scalar2=None,
                                            op0=ALU.mult)
                        for m in range(MT):
                            e.tensor_tensor(out=col(m, 11), in0=col(m, 0),
                                            in1=col(m, 0), op=ALU.mult)
                        ins = None
                        for m in range(MT):
                            ins = e.scalar_tensor_tensor(
                                out=col(m, 12), in0=col(m, 13),
                                scalar=1.0 / N, in1=col(m, 11),
                                op0=ALU.mult, op1=ALU.subtract)
                        return ins
                    id_tk0b = seq.stage("vector", tk0b, after=[id_tk0a])

                    def tk0c(e):
                        for m in range(MT):
                            e.activation(out=col(m, 12), in_=col(m, 12),
                                         func=AF.Ln)
                        ins = None
                        for m in range(MT):
                            ins = e.activation(out=col(m, 12), in_=col(m, 12),
                                               func=AF.Exp, scale=0.5)
                        return ins
                    id_tk0c = seq.stage("scalar", tk0c, after=[id_tk0b])

                    def tk0d(e):
                        ins = None
                        for m in range(MT):
                            for c_, coef in ((8, SIG_SPAN), (1, BR_LO),
                                             (2, BR_HI), (5, LAD0)):
                                e.scalar_tensor_tensor(
                                    out=col(m, c_), in0=col(m, 12),
                                    scalar=coef, in1=col(m, 0),
                                    op0=ALU.mult, op1=ALU.add)
                            e.memset(col(m, 3), 8192.0)
                            e.memset(col(m, 4), 0.0)
                            e.memset(col(m, 9), 0.0)
                            ins = e.memset(col(m, 10), -1.0)
                        return ins
                    id_upd = seq.stage("vector", tk0d, after=[id_tk0c])

                    for pi in range(NPROBE):
                        def cnt_v(e):
                            ins = None
                            for m in range(2):
                                ins = e.tensor_scalar(
                                    out=s_zoneA[:], in0=s_za[m][:],
                                    scalar1=col(m, 5),
                                    scalar2=None, op0=ALU.is_ge, op1=ALU.add,
                                    accum_out=col(m, 6))
                            return ins
                        id_cv = seq.stage("vector", cnt_v, after=[id_upd])

                        def cnt_s(e):
                            ins = None
                            for m in range(2, MT):
                                ins = e.activation(
                                    out=s_zoneA[:], in_=s_za[m][:],
                                    func=AF.Sign, scale=-1.0,
                                    bias=col(m, 5),
                                    accum_out=col(m, 6))
                            return ins
                        id_cs = seq.stage("scalar", cnt_s, after=[id_upd])

                        def upd(e, pi=pi, last=(pi == NPROBE - 1)):
                            P4 = s_pk[:].rearrange("p (m c) -> p m c", c=16)

                            def c4(j):
                                return P4[:, :, j:j + 1]
                            for m in range(2, MT):
                                e.tensor_scalar(out=col(m, 6), in0=col(m, 6),
                                                scalar1=-0.5,
                                                scalar2=float(N // 2),
                                                op0=ALU.mult, op1=ALU.add)
                            vjk(e)
                            e.tensor_scalar(out=c4(7), in0=c4(6),
                                            scalar1=float(TOPK) - 0.5,
                                            scalar2=None, op0=ALU.is_ge)
                            e.tensor_tensor(out=c4(11), in0=c4(5), in1=c4(1),
                                            op=ALU.subtract)
                            e.tensor_tensor(out=c4(14), in0=c4(6), in1=c4(3),
                                            op=ALU.subtract)
                            e.tensor_tensor(out=c4(11), in0=c4(11), in1=c4(7),
                                            op=ALU.mult)
                            e.tensor_tensor(out=c4(14), in0=c4(14), in1=c4(7),
                                            op=ALU.mult)
                            e.tensor_tensor(out=c4(1), in0=c4(1), in1=c4(11),
                                            op=ALU.add)
                            e.tensor_tensor(out=c4(3), in0=c4(3), in1=c4(14),
                                            op=ALU.add)
                            e.tensor_scalar(out=c4(7), in0=c4(7), scalar1=-1.0,
                                            scalar2=1.0, op0=ALU.mult,
                                            op1=ALU.add)
                            e.tensor_tensor(out=c4(11), in0=c4(5), in1=c4(2),
                                            op=ALU.subtract)
                            e.tensor_tensor(out=c4(14), in0=c4(6), in1=c4(4),
                                            op=ALU.subtract)
                            e.tensor_tensor(out=c4(11), in0=c4(11), in1=c4(7),
                                            op=ALU.mult)
                            e.tensor_tensor(out=c4(14), in0=c4(14), in1=c4(7),
                                            op=ALU.mult)
                            e.tensor_tensor(out=c4(2), in0=c4(2), in1=c4(11),
                                            op=ALU.add)
                            e.tensor_tensor(out=c4(4), in0=c4(4), in1=c4(14),
                                            op=ALU.add)
                            e.tensor_scalar(out=c4(7), in0=c4(6),
                                            scalar1=float(TOPK) - 0.5,
                                            scalar2=None, op0=ALU.is_ge)
                            e.tensor_scalar(out=c4(11), in0=c4(6),
                                            scalar1=float(TOPK + 7),
                                            scalar2=None, op0=ALU.is_le)
                            vjk(e)
                            e.tensor_tensor(out=c4(7), in0=c4(7), in1=c4(11),
                                            op=ALU.mult)
                            e.tensor_tensor(out=c4(14), in0=c4(5), in1=c4(9),
                                            op=ALU.subtract)
                            e.tensor_tensor(out=c4(11), in0=c4(6), in1=c4(10),
                                            op=ALU.subtract)
                            e.tensor_tensor(out=c4(14), in0=c4(14), in1=c4(7),
                                            op=ALU.mult)
                            e.tensor_tensor(out=c4(11), in0=c4(11), in1=c4(7),
                                            op=ALU.mult)
                            e.tensor_tensor(out=c4(9), in0=c4(9), in1=c4(14),
                                            op=ALU.add)
                            e.tensor_tensor(out=c4(10), in0=c4(10), in1=c4(11),
                                            op=ALU.add)
                            if pi == 0:
                                e.scalar_tensor_tensor(
                                    out=c4(5), in0=c4(12), scalar=LAD1,
                                    in1=c4(0), op0=ALU.mult, op1=ALU.add)
                            elif pi < NPROBE - 1 - N_FP:
                                e.tensor_tensor(out=c4(5), in0=c4(1),
                                                in1=c4(2), op=ALU.add)
                                vjk(e)
                                e.tensor_scalar(out=c4(5), in0=c4(5),
                                                scalar1=0.5, scalar2=None,
                                                op0=ALU.mult)
                            elif not last:
                                e.tensor_tensor(out=c4(7), in0=c4(3),
                                                in1=c4(4), op=ALU.subtract)
                                e.tensor_scalar(out=c4(11), in0=c4(3),
                                                scalar1=-AIM, scalar2=None,
                                                op0=ALU.add)
                                e.tensor_scalar(out=c4(7), in0=c4(7),
                                                scalar1=1e-6, scalar2=None,
                                                op0=ALU.add)
                                e.tensor_tensor(out=c4(14), in0=c4(2),
                                                in1=c4(1), op=ALU.subtract)
                                e.reciprocal(out=c4(7), in_=c4(7))
                                vjk(e)
                                e.tensor_tensor(out=c4(11), in0=c4(11),
                                                in1=c4(7), op=ALU.mult)
                                vjk(e)
                                e.tensor_tensor(out=c4(11), in0=c4(11),
                                                in1=c4(14), op=ALU.mult)
                                vjk(e)
                                e.tensor_tensor(out=c4(5), in0=c4(1),
                                                in1=c4(11), op=ALU.add)
                            return vjk(e)
                        id_upd = seq.stage("vector", upd,
                                           after=[id_cs, id_cv])

                    def fin(e):
                        for m in range(MT):
                            e.tensor_scalar(out=col(m, 7), in0=col(m, 10),
                                            scalar1=0.0, scalar2=None,
                                            op0=ALU.is_ge)
                        for dst, src in ((1, 9), (3, 10)):
                            for m in range(MT):
                                e.tensor_tensor(out=col(m, 11),
                                                in0=col(m, src),
                                                in1=col(m, dst),
                                                op=ALU.subtract)
                            for m in range(MT):
                                e.tensor_tensor(out=col(m, 11), in0=col(m, 11),
                                                in1=col(m, 7), op=ALU.mult)
                            for m in range(MT):
                                e.tensor_tensor(out=col(m, dst),
                                                in0=col(m, dst),
                                                in1=col(m, 11), op=ALU.add)
                        for m in range(MT):
                            e.tensor_scalar(out=col(m, 6), in0=col(m, 3),
                                            scalar1=float(-TOPK), scalar2=0.0,
                                            op0=ALU.add, op1=ALU.max)
                        for m in range(MT):
                            e.tensor_scalar(out=col(m, 6), in0=col(m, 6),
                                            scalar1=7.0, scalar2=None,
                                            op0=ALU.min)
                        for m in range(MT):
                            e.tensor_scalar(out=col(m, 14), in0=col(m, 6),
                                            scalar1=-0.25, scalar2=None,
                                            op0=ALU.add)
                        ins = None
                        for m in range(MT):
                            ins = e.tensor_scalar(out=col(m, 15),
                                                  in0=col(m, 6),
                                                  scalar1=0.6, scalar2=None,
                                                  op0=ALU.add)
                        return ins
                    seq.stage("vector", fin)

                    def tcm(e):
                        ins = None
                        for m in range(MT):
                            ins = e.tensor_scalar(
                                out=col(m, 7), in0=col(m, 1), scalar1=-1e-5,
                                scalar2=0.0, op0=ALU.add, op1=ALU.max)
                        return ins
                    seq.stage("vector", tcm)

                    evh_ids = {}
                    for m in range(MT):
                        zs = s_za[m][:]
                        hTm = s_za[m][:].bitcast(BF16)[:, 0:N]
                        pkb = m * 16

                        def hid(e, pkb=pkb, zs=zs):
                            return e.scalar_tensor_tensor(
                                out=s_hidt[:], in0=zs,
                                scalar=s_pk[:, pkb + 7:pkb + 8], in1=zs,
                                op0=ALU.is_ge, op1=ALU.mult)
                        id_hid = seq.stage("vector", hid)

                        for half in range(8):
                            kg = m * 8 + half
                            pht = p_ht if kg % 2 == 0 else p_ht2

                            def trh(e, half=half, pht=pht):
                                ins = None
                                for bi in range(8):
                                    nt = half * 8 + bi
                                    ins = e.transpose(
                                        pht[:, bi * 128:(bi + 1) * 128],
                                        s_hidt[:, nt * 128:(nt + 1) * 128],
                                        s_eye[:])
                                return ins
                            id_trh = seq.stage(
                                "tensor", trh,
                                after=[id_hid, evh_ids.get(kg - 2, 0)])

                            def evh(e, m=m, half=half, hTm=hTm, pht=pht):
                                ins = None
                                for bi in range(8):
                                    nt = half * 8 + bi
                                    ins = e.tensor_copy(
                                        out=hTm[:, nt * 128:(nt + 1) * 128],
                                        in_=pht[:, bi * 128:(bi + 1) * 128])
                                return ins
                            evh_ids[kg] = seq.stage("vector", evh,
                                                    after=[id_trh])

                    if "thr" in tp and g == 2:
                        def tthr(e):
                            return e.dma_start(out=tp["thr"][:],
                                               in_=s_pk[:])
                        seq.dma("sync", tthr, sm_o, 1)

                    htscope.close()
                    p_o = [gctx.enter_context(nc.psum_tensor(
                        f"p_o_{g}_{i}", [128, 512], F32)) for i in range(8)]
                    to_used = {}
                    ev2_ids = {}
                    for dh in range(2):
                        for ntq in range(NT // 4):
                            buf = s_to0 if ntq % 2 == 0 else s_to1

                            def ldo(e, ntq=ntq, dh=dh, buf=buf):
                                return e.dma_start(
                                    out=buf[:, 0:4096].rearrange(
                                        "p (a c) -> p a c", a=4),
                                    in_=t8out[ntq * 512:(ntq + 1) * 512,
                                              dh * 1024:(dh + 1) * 1024]
                                    .rearrange("(a p) c -> p a c", p=128))
                            ldep = to_used.get((dh, ntq - 2), None)
                            if dh == 0 and ntq == 0:
                                ldep = ts_used[NCH - 1]
                            seq.dma("gpsimd", ldo, sm_t, 1, dep=ldep)
                            tv = sm_t.v

                            def mm2(e, dh=dh, ntq=ntq, buf=buf):
                                ins = None
                                for m in range(MT):
                                    hTm = s_za[m][:].bitcast(BF16)
                                    for a in range(4):
                                        nt = ntq * 4 + a
                                        first = (ntq == 0 and a == 0)
                                        last = (ntq == NT // 4 - 1 and a == 3)
                                        for dq in range(2):
                                            ins = e.matmul(
                                                p_o[m * 2 + dq][:],
                                                hTm[:, nt * 128:
                                                    (nt + 1) * 128],
                                                buf[:, a * 1024 + dq * 512:
                                                    a * 1024 + (dq + 1) * 512],
                                                start=first, stop=last)
                                return ins
                            to_used[(dh, ntq)] = seq.stage(
                                "tensor", mm2, waits=((sm_t.s, tv),))

                        def ev2(e, dh=dh):
                            ins = None
                            for m in range(MT):
                                r2 = r2m(m)
                                for dq in range(2):
                                    do_ = dh * 1024 + dq * 512
                                    ins = e.scalar_tensor_tensor(
                                        out=r2[:, do_:do_ + 512],
                                        in0=p_o[m * 2 + dq][:],
                                        scalar=s_scal[:, 1:2],
                                        in1=s_x1k[:, m * D + do_:
                                                  m * D + do_ + 512],
                                        op0=ALU.mult, op1=ALU.add)
                            return ins
                        ev2_ids[dh] = seq.stage(
                            "vector", ev2, waits=((sm_1.s, x1kv),))

                    zBf = s_zoneB[:].bitcast(F32)
                    s_lnd = s_zoneA[:].bitcast(F32)[:, 0:D]

                    def r3q(m):
                        return zBf[:, (m % 2) * 2 * D:(m % 2) * 2 * D + D]

                    def o2q(m):
                        return zBf[:, (m % 2) * 2 * D + D:
                                   (m % 2) * 2 * D + 2 * D]

                    ln_sov = sm_o.v
                    sov = {}
                    lni_ids = {}
                    ln2sq_ids = {}
                    lnap_ids = {}
                    for m in range(MT):
                        def lni(e, m=m):
                            e.tensor_tensor(out=r3q(m), in0=r2m(m),
                                            in1=s_bout[:], op=ALU.add)
                            vjk(e)
                            return e.tensor_scalar(
                                out=s_lnd, in0=r3q(m),
                                scalar1=1.0, scalar2=None, op0=ALU.mult,
                                op1=ALU.add,
                                accum_out=s_pk[:, m * 16:m * 16 + 1])
                        lni_ids[m] = seq.stage(
                            "vector", lni, waits=((sm_o.s, ln_sov),),
                            after=[ln2sq_ids.get(m - 2, 0),
                                   ev2_ids[1]])

                        def lnsq(e, m=m):
                            return e.activation(
                                out=s_lnd, in_=r3q(m), func=AF.Square,
                                accum_out=s_pk[:, m * 16 + 1:m * 16 + 2])
                        id_lnsq = seq.stage("scalar", lnsq,
                                            after=[lni_ids[m]])

                        def lnab(e, m=m):
                            P = s_pk
                            mu = P[:, m * 16:m * 16 + 1]
                            sq = P[:, m * 16 + 1:m * 16 + 2]
                            va = P[:, m * 16 + 2:m * 16 + 3]
                            tq_ = P[:, m * 16 + 8:m * 16 + 9]
                            e.tensor_scalar(out=mu, in0=mu, scalar1=1.0 / D,
                                            scalar2=None, op0=ALU.mult)
                            e.tensor_scalar(out=va, in0=sq, scalar1=1.0 / D,
                                            scalar2=LNEPS, op0=ALU.mult,
                                            op1=ALU.add)
                            e.tensor_tensor(out=tq_, in0=mu, in1=mu,
                                            op=ALU.mult)
                            vjk(e)
                            return e.tensor_tensor(out=va, in0=va, in1=tq_,
                                                   op=ALU.subtract)
                        id_lnab = seq.stage("vector", lnab, after=[id_lnsq])

                        def lnrs(e, m=m):
                            P = s_pk
                            va = P[:, m * 16 + 2:m * 16 + 3]
                            rs = P[:, m * 16 + 3:m * 16 + 4]
                            e.activation(out=rs, in_=va, func=AF.Ln)
                            sjk(e)
                            return e.activation(out=rs, in_=rs, func=AF.Exp,
                                                scale=-0.5)
                        id_lnrs = seq.stage("scalar", lnrs, after=[id_lnab])

                        def lnap(e, m=m):
                            P = s_pk
                            mu = P[:, m * 16:m * 16 + 1]
                            rs = P[:, m * 16 + 3:m * 16 + 4]
                            e.tensor_scalar(out=r3q(m), in0=r3q(m),
                                            scalar1=mu, scalar2=rs,
                                            op0=ALU.subtract, op1=ALU.mult)
                            vjk(e)
                            e.tensor_tensor(out=r3q(m), in0=r3q(m),
                                            in1=s_gam[:], op=ALU.mult)
                            vjk(e)
                            e.tensor_tensor(out=r3q(m), in0=r3q(m),
                                            in1=s_bet[:], op=ALU.add)
                            vjk(e)
                            e.tensor_tensor(out=r3q(m), in0=r3q(m),
                                            in1=s_x1k[:, m * D:(m + 1) * D],
                                            op=ALU.add)
                            vjk(e)
                            return e.tensor_scalar(
                                out=s_lnd, in0=r3q(m), scalar1=1.0,
                                scalar2=None, op0=ALU.mult, op1=ALU.add,
                                accum_out=s_pk[:, m * 16 + 4:m * 16 + 5])
                        id_lnap = seq.stage("vector", lnap, after=[id_lnrs])
                        lnap_ids[m] = id_lnap

                        def ln2sq(e, m=m):
                            return e.activation(
                                out=s_lnd, in_=r3q(m), func=AF.Square,
                                accum_out=s_pk[:, m * 16 + 5:m * 16 + 6])
                        ln2sq_ids[m] = seq.stage("scalar", ln2sq,
                                                 after=[id_lnap])

                        def ln2ab(e, m=m):
                            P = s_pk
                            mu = P[:, m * 16 + 4:m * 16 + 5]
                            sq = P[:, m * 16 + 5:m * 16 + 6]
                            va = P[:, m * 16 + 6:m * 16 + 7]
                            tq_ = P[:, m * 16 + 9:m * 16 + 10]
                            e.tensor_scalar(out=mu, in0=mu, scalar1=1.0 / D,
                                            scalar2=None, op0=ALU.mult)
                            e.tensor_scalar(out=va, in0=sq, scalar1=1.0 / D,
                                            scalar2=LNEPS, op0=ALU.mult,
                                            op1=ALU.add)
                            e.tensor_tensor(out=tq_, in0=mu, in1=mu,
                                            op=ALU.mult)
                            vjk(e)
                            return e.tensor_tensor(out=va, in0=va, in1=tq_,
                                                   op=ALU.subtract)
                        id_ln2ab = seq.stage("vector", ln2ab,
                                             after=[ln2sq_ids[m]])

                        def ln2rs(e, m=m):
                            P = s_pk
                            va = P[:, m * 16 + 6:m * 16 + 7]
                            rs = P[:, m * 16 + 7:m * 16 + 8]
                            e.activation(out=rs, in_=va, func=AF.Ln)
                            sjk(e)
                            return e.activation(out=rs, in_=rs, func=AF.Exp,
                                                scale=-0.5)
                        id_ln2rs = seq.stage("scalar", ln2rs,
                                             after=[id_ln2ab])

                        def ln2ap(e, m=m):
                            P = s_pk
                            mu = P[:, m * 16 + 4:m * 16 + 5]
                            rs = P[:, m * 16 + 7:m * 16 + 8]
                            return e.tensor_scalar(out=o2q(m), in0=r3q(m),
                                                   scalar1=mu, scalar2=rs,
                                                   op0=ALU.subtract,
                                                   op1=ALU.mult)
                        id_ln2ap = seq.stage(
                            "vector", ln2ap, after=[id_ln2rs],
                            waits=((sm_o.s, sov.get(m - 2, ln_sov)),))

                        def sto(e, g=g, m=m):
                            r0 = g * 512 + m * 128
                            return e.dma_start(out=out[r0:r0 + 128, :],
                                               in_=o2q(m))
                        seq.dma("sync", sto, sm_o, 1, dep=id_ln2ap)
                        sov[m] = sm_o.v

                    g_prev_psum = ev2_ids[1]
                    g_prev_za = lni_ids[3]
                    g_prev_mm2 = to_used[(1, NT // 4 - 1)]
                    g_prev_lnap = lnap_ids[3]
                    g_prev_sov = {2: sov[2], 3: sov[3]}
                    seq.flush()
                trunc_point(f"g{g}")

            def final(e):
                e.wait_ge(sm_o.s, sm_o.v)
                return None
            seq.stage("sync", final, inc=False)
            seq.flush()

    return nc, dr, tp



def stage_inputs(inputs):
    x = np.ascontiguousarray(np.asarray(inputs["L0_attn_x"], dtype=np.float32))
    Wx = np.asarray(inputs["L0_attn_Wxproj"], dtype=np.float32)
    Wi = np.asarray(inputs["L0_attn_Winproj"], dtype=np.float32)
    Wo = np.asarray(inputs["L0_attn_Woutproj"], dtype=np.float32)
    cwn = np.asarray(inputs["L0_attn_conv_w"], dtype=np.float32)
    cbn = np.asarray(inputs["L0_attn_conv_b"], dtype=np.float32)
    Alog = np.asarray(inputs["L0_attn_A_log"], dtype=np.float32)
    Dv = np.asarray(inputs["L0_attn_D"], dtype=np.float32)
    Wsyn = np.asarray(inputs["L0_ffn_W_syn"], dtype=np.float32)
    bsyn = np.asarray(inputs["L0_ffn_b_syn"], dtype=np.float32)
    Wout = np.asarray(inputs["L0_ffn_W_out"], dtype=np.float32)
    bout = np.asarray(inputs["L0_ffn_b_out"], dtype=np.float32)
    gam = np.asarray(inputs["L0_ffn_gamma"], dtype=np.float32)
    bet = np.asarray(inputs["L0_ffn_beta"], dtype=np.float32)

    shared = {
        "wxT": np.ascontiguousarray(Wx.T),
        "wiT": np.ascontiguousarray(Wi.T),
        "woT": np.ascontiguousarray(Wo.T),
        "cwv": np.ascontiguousarray(
            cwn[:, 0, :].reshape(DT, 128, 4).transpose(1, 0, 2).reshape(128, DT * 4)),
        "cbv": np.ascontiguousarray(cbn.reshape(DT, 128).T),
        "alog": np.ascontiguousarray(Alog.reshape(ST, 128).T),
        "dvec": np.ascontiguousarray(Dv.reshape(DT, 128).T),
        "wsynT": np.ascontiguousarray(Wsyn.T),
        "woutT": np.ascontiguousarray(Wout.T),
        "bsyn_r": np.ascontiguousarray(np.broadcast_to(bsyn[None, :], (128, N))),
        "bout_r": np.ascontiguousarray(np.broadcast_to(bout[None, :], (128, D))),
        "gam_r": np.ascontiguousarray(np.broadcast_to(gam[None, :], (128, D))),
        "bet_r": np.ascontiguousarray(np.broadcast_to(bet[None, :], (128, D))),
    }

    in_maps = []
    for c in range(8):
        seqi, idx = divmod(c, 4)
        start = idx * CT
        xo = np.zeros((CT + 4, D), np.float32)
        lo = start - 4
        src0 = max(lo, 0)
        xo[src0 - lo:, :] = x[seqi, src0:start + CT, :]
        xp = np.zeros((PRE + 4, D), np.float32)
        lo2 = start - PRE - 4
        src0 = max(lo2, 0)
        if start > 0:
            xp[src0 - lo2:, :] = x[seqi, src0:start, :]
        m = dict(shared)
        m["x_own"] = np.ascontiguousarray(xo.T)
        m["x_pre"] = np.ascontiguousarray(xp.T)
        in_maps.append(m)
    return in_maps


_NC_CACHE = {}


def kernel(**inputs):
    key = ()
    if key not in _NC_CACHE:
        _NC_CACHE[key] = build_nc()
    nc, dr, tp = _NC_CACHE[key]
    in_maps = stage_inputs(inputs)
    res = run_bass_kernel_spmd(nc, in_maps, core_ids=list(range(8)))
    outs = [np.asarray(res.results[c]["out"], dtype=np.float32)
            for c in range(8)]
    return np.concatenate(outs, axis=0).reshape(B_, L_, D)
